# version 01 (final)
# baseline (speedup 1.0000x reference)
import numpy as np
from contextlib import ExitStack

import concourse.bass as bass
import concourse.mybir as mybir
from concourse.bass_utils import run_bass_kernel_spmd

F32 = mybir.dt.float32
BF16 = mybir.dt.bfloat16
AF = mybir.ActivationFunctionType
ALU = mybir.AluOpType
AX = mybir.AxisListType

NCORES = 8
T_TOK = 2048
NT = T_TOK // 128
D = 1024
KD = D // 128

BIG_NEG = -3.0e38


class Ctx:

    def __init__(self, nc, stack):
        self.nc = nc
        self.stack = stack
        self.T = nc.tensor
        self.V = nc.vector
        self.S = nc.scalar
        self.G = nc.gpsimd
        self.Y = nc.sync
        self._sems = {}
        self.cnt = {}
        self.dma_sems = set()
        self.res = {}

    def sem(self, name):
        if name not in self._sems:
            self._sems[name] = self.stack.enter_context(self.nc.semaphore(name))
            self.cnt[name] = 0
        return self._sems[name]

    def inc(self, name, inst, n=1):
        inst.then_inc(self.sem(name), n)
        self.cnt[name] += n
        if n >= 16:
            self.dma_sems.add(name)
        return self.cnt[name]

    def acquire(self, eng, key):
        r = self.res.get(key)
        if r is not None:
            eng.wait_ge(self.sem(r[0]), r[1])

    def release(self, key, semname, val):
        self.res[key] = (semname, val)

    def finish(self):
        for name in sorted(self.dma_sems):
            self.G.wait_ge(self.sem(name), self.cnt[name])

    def wait(self, eng, name, val=None):
        eng.wait_ge(self.sem(name), self.cnt[name] if val is None else val)

    def ve(self, inst, sem="Ve"):
        return self.inc(sem, inst)

    def vw(self, eng, val=None, sem="Ve"):
        eng.wait_ge(self.sem(sem), self.cnt[sem] if val is None else val)

    def sbuf(self, name, shape, dtype):
        return self.stack.enter_context(self.nc.sbuf_tensor(name, shape, dtype))

    def psum(self, name, shape, dtype=F32):
        return self.stack.enter_context(self.nc.psum_tensor(name, shape, dtype))



def load_bf16(c, dst_view, src_ap, sem="wl"):
    return c.inc(sem, c.G.dma_start(out=dst_view, in_=src_ap), 16)


def load_f32(c, dst_view, src_ap, slow=False, sem="wl"):
    return c.inc(sem, c.G.dma_start(out=dst_view, in_=src_ap,
                                    allow_slow_non_contiguous=slow), 16)


def emit_ln_transpose(c, x_dram, XNT, pools, prefix, xs_sb=None, eps=1e-5,
                      out_tm=None):
    V, S, T, Y = c.V, c.S, c.T, c.Y
    XT, XNB, STs, SQ, PT, IDN = (pools["XTR"], pools["XNB"], pools["ST"],
                                 pools["SQ"], pools["PT"], pools["IDN"])
    sx, sv, ss, sm, se = (prefix + k for k in ("x", "v", "s", "m", "e"))
    x_tiled = x_dram.rearrange("(t p) f -> p t f", p=128) if x_dram is not None else None
    sv0, ss0, sm0, se0 = (c.cnt.get(k, 0) for k in (sv, ss, sm, se))
    c.sem(sv), c.sem(ss), c.sem(sm), c.sem(se)
    if x_dram is not None:
        c.acquire(Y, "XTR")
    c.acquire(S, "OUT")
    c.acquire(S, "XNT")
    c.acquire(V, "ST")
    c.acquire(T, "PT0"), c.acquire(T, "PT1")

    for t in range(NT):
        xn = out_tm[:, 1024 * t:1024 * (t + 1)] if out_tm is not None else XNB[:, 0:1024]
        st = STs
        if x_tiled is not None:
            xt = XT[:, 0:1024]
            if t >= 1:
                Y.wait_ge(c.sem(sv), sv0 + 3 * t)
                Y.wait_ge(c.sem(ss), ss0 + 2 * t - 1)
            c.inc(sx, Y.dma_start(out=xt, in_=x_tiled[:, t, :]), 16)
            xsrc = xt
        else:
            xsrc = xs_sb[:, t * 1024:(t + 1) * 1024]

        if x_tiled is not None:
            c.wait(V, sx)
        e1 = c.ve(V.reduce_sum(st[:, 0:1], xsrc, axis=AX.X))
        c.vw(V, e1)
        inst = V.tensor_scalar_mul(st[:, 1:2], st[:, 0:1], 1.0 / D)
        e_mu = c.inc(sv, inst)
        if x_tiled is not None:
            c.wait(S, sx)
        if t >= 1:
            c.wait(S, sv, sv0 + 3 * t)
        inst = S.activation(SQ, xsrc, AF.Square, accum_out=st[:, 2:3])
        c.inc(ss, inst)
        c.wait(V, ss)
        e2 = c.ve(V.tensor_scalar_mul(st[:, 3:4], st[:, 2:3], 1.0 / D))
        c.wait(V, sv, e_mu)
        e3 = c.ve(V.tensor_mul(st[:, 4:5], st[:, 1:2], st[:, 1:2]))
        c.vw(V, e3)
        e4 = c.ve(V.tensor_sub(st[:, 5:6], st[:, 3:4], st[:, 4:5]))
        c.vw(V, e4)
        inst = V.tensor_scalar_add(st[:, 5:6], st[:, 5:6], eps)
        c.inc(sv, inst)
        c.wait(S, sv)
        inst = S.activation(st[:, 6:7], st[:, 5:6], AF.Sqrt)
        c.inc(ss, inst)
        c.wait(V, ss)
        e5 = c.ve(V.reciprocal(st[:, 7:8], st[:, 6:7]))
        c.vw(V, e5)
        e6 = c.ve(V.scalar_tensor_tensor(st[:, 8:9], st[:, 1:2], -1.0, st[:, 7:8],
                                         ALU.mult, ALU.mult))
        if t >= 1 and out_tm is None:
            c.wait(V, sm, sm0 + t)
        c.vw(V, e6)
        inst = V.tensor_scalar(xn, xsrc, st[:, 7:8], st[:, 8:9],
                               ALU.mult, ALU.add)
        c.inc(sv, inst)

        pt = PT[t % 2]
        c.wait(T, sv)
        if t >= 2:
            c.wait(T, se, se0 + t - 1)
        for f in range(KD):
            inst = T.transpose(pt[:, f * 128:(f + 1) * 128],
                               xn[:, f * 128:(f + 1) * 128], IDN[:, :])
        c.inc(sm, inst)
        c.wait(S, sm)
        xnt_dst = XNT[:, :].rearrange("p (f tt) -> p f tt", f=KD)[:, :, t * 128:(t + 1) * 128]
        inst = S.activation(xnt_dst, pt[:, :].rearrange("p (f tt) -> p f tt", f=KD),
                            AF.Copy)
        c.inc(se, inst)


def emit_bcast_row(c, dst_sb, row_bf16_ap, pools, n_cols, prefix, scale=None):
    T, V = c.T, c.V
    sm, sv = prefix + "bm", prefix + "bv"
    psbank = pools["PS"][0]
    ones = pools["ONES"]
    for off in range(0, n_cols, 512):
        n = min(512, n_cols - off)
        if off > 0:
            c.wait(T, sv)
        inst = T.matmul(psbank[:, 0:n], ones[:, :], row_bf16_ap[:, off:off + n],
                        start=True, stop=True)
        c.inc(sm, inst)
        c.wait(V, sm)
        if scale is None:
            inst = V.tensor_copy(dst_sb[:, off:off + n], psbank[:, 0:n])
        else:
            inst = V.tensor_scalar_mul(dst_sb[:, off:off + n], psbank[:, 0:n], scale)
        c.inc(sv, inst)


def emit_l2_router(c, XNT, RWS, FQ, SM8, PS, p, xnt_sem, xnt_done, extra_t0_wait=None):
    V, S, T = c.V, c.S, c.T
    sm, sv, ss = p + "rm", p + "rv", p + "rs"
    c.sem(sm), c.sem(sv), c.sem(ss)
    for t in range(NT):
        bank = PS[t % 2]
        if t == 0:
            c.wait(T, xnt_sem, xnt_done)
            if extra_t0_wait is not None:
                c.wait(T, extra_t0_wait[0], extra_t0_wait[1])
        if t >= 2:
            c.wait(T, sv, 3 * (t - 2) + 2)
        for k in range(KD):
            inst = T.matmul(bank[:, 0:8],
                            XNT[:, 2048 * k + 128 * t: 2048 * k + 128 * (t + 1)],
                            RWS[:, 8 * k: 8 * (k + 1)],
                            start=(k == 0), stop=(k == KD - 1))
        c.inc(sm, inst)
        c.wait(V, sm)
        inst = V.reduce_max(SM8[:, 8:9], bank[:, 0:8], axis=AX.X, negate=True)
        c.inc(sv, inst)
        inst = V.tensor_copy(SM8[:, 0:8], bank[:, 0:8])
        c.inc(sv, inst)
        c.wait(S, sv)
        inst = S.activation(SM8[:, 0:8], SM8[:, 0:8], AF.Exp,
                            bias=SM8[:, 8:9], scale=1.0, accum_out=SM8[:, 9:10])
        c.inc(ss, inst)
        c.wait(V, ss)
        e1 = c.ve(V.reciprocal(SM8[:, 10:11], SM8[:, 9:10]))
        c.vw(V, e1)
        e2 = c.ve(V.tensor_scalar_mul(SM8[:, 0:8], SM8[:, 0:8], SM8[:, 10:11]))
        c.vw(V, e2)
        e3 = c.ve(V.reduce_max(SM8[:, 11:12], SM8[:, 0:8], axis=AX.X))
        c.vw(V, e3)
        e4 = c.ve(V.tensor_scalar(SM8[:, 12:20], SM8[:, 0:8], SM8[:, 11:12], None,
                                  ALU.is_ge))
        c.vw(V, e4)
        e5 = c.ve(V.scalar_tensor_tensor(SM8[:, 12:20], SM8[:, 12:20], BIG_NEG,
                                         SM8[:, 0:8], ALU.mult, ALU.add))
        c.vw(V, e5)
        e6 = c.ve(V.reduce_max(SM8[:, 13:14], SM8[:, 12:20], axis=AX.X))
        c.vw(V, e6)
        e7 = c.ve(V.tensor_add(SM8[:, 14:15], SM8[:, 11:12], SM8[:, 13:14]))
        c.vw(V, e7)
        e8 = c.ve(V.reciprocal(SM8[:, 15:16], SM8[:, 14:15]))
        c.vw(V, e8)
        e9 = c.ve(V.tensor_mul(SM8[:, 16:17], SM8[:, 11:12], SM8[:, 15:16]))
        ea = c.ve(V.scalar_tensor_tensor(FQ[:, 2 * t:2 * t + 1], SM8[:, 13:14], 1.0,
                                         SM8[:, 15:16], ALU.mult, ALU.mult))
        c.vw(V, ea)
        eb = c.ve(V.tensor_add(FQ[:, 2 * t:2 * t + 1], FQ[:, 2 * t:2 * t + 1],
                               SM8[:, 16:17]))
        c.vw(V, eb)
        inst = V.tensor_scalar_mul(FQ[:, 2 * t + 1:2 * t + 2],
                                   FQ[:, 2 * t:2 * t + 1], 0.5)
        c.inc(sv, inst)
    return c.cnt[sv]



def emit_l2_ffn(c, x_dram, y_dram, w, pools, p):
    V, S, T, G, Y = c.V, c.S, c.T, c.G, c.Y
    XNT = pools["XNT"]
    PS = pools["PS"]

    W1S = c.sbuf(p + "w1", [128, KD * 4096], BF16)
    W2S = c.sbuf(p + "w2", [128, 32 * 1024], BF16)
    RWS = c.sbuf(p + "rw", [128, KD * 8], BF16)
    B1S = c.sbuf(p + "b1", [128, 32], F32)
    B2R = c.sbuf(p + "b2r", [1, 1024], BF16)
    B2B = c.sbuf(p + "b2b", [128, 1024], F32)
    HTC = c.sbuf(p + "ht", [128, 32 * 256], BF16)
    FQ = c.sbuf(p + "f", [128, 2 * NT], F32)
    SM8 = c.sbuf(p + "sm8", [128, 24], F32)
    XRT = pools["XTR"]
    OUT = pools["OUT"]

    wl = p + "wl"
    load_bf16(c, W1S[:, :].rearrange("q (k i) -> q k i", k=KD),
              w["w1"].rearrange("(k q) i -> q k i", q=128), sem=wl)
    load_bf16(c, W2S[:, :].rearrange("q (k i) -> q k i", k=32),
              w["w2"].rearrange("(k q) i -> q k i", q=128), sem=wl)
    load_bf16(c, RWS[:, :].rearrange("q (k i) -> q k i", k=KD),
              w["router_w"].rearrange("(k q) i -> q k i", q=128), sem=wl)
    load_f32(c, B1S[:, :], w["b1"].rearrange("(a q) -> q a", q=128), slow=True, sem=wl)
    load_bf16(c, B2R[:, :], w["b2"].rearrange("(o a) -> o a", o=1), sem=wl)
    wl_done = c.cnt[wl]

    emit_ln_transpose(c, x_dram, XNT, pools, p + "ln")
    lne = p + "lne"
    xnt_done = c.cnt[lne]

    c.wait(T, wl, wl_done)
    c.wait(S, wl, wl_done)
    c.acquire(T, "PS0"), c.acquire(T, "PS1")
    emit_bcast_row(c, B2B[:, :], B2R[:, :], pools, 1024, p)
    bcast_done = c.cnt[p + "bv"]

    f_done = emit_l2_router(c, XNT, RWS, FQ, SM8, PS, p, lne, xnt_done,
                            extra_t0_wait=(p + "bv", bcast_done))
    sv = p + "rv"
    c.acquire(T, "PS2"), c.acquire(T, "PS3"), c.acquire(T, "PS4"), c.acquire(T, "PS5")


    sa, se = p + "am", p + "ae"
    sb, sw = p + "Bm", p + "Bv"
    sxp = [p + "Bx0", p + "Bx1"]
    syp = [p + "By0", p + "By1"]
    for k in [sa, se, sb, sw] + sxp + syp:
        c.sem(k)
    TCW = 256
    NTT = TCW // 128
    NCH = T_TOK // TCW
    x_tiled = x_dram.rearrange("(t q) f -> q t f", q=128)
    y_tiled = y_dram.rearrange("(t q) f -> q t f", q=128)
    for tc in range(NCH):
        tcols = slice(TCW * tc, TCW * (tc + 1))
        for it in range(32):
            bank = PS[2 + (it % 2)]
            if tc > 0 and it == 0:
                c.wait(S, sb)
            if tc > 0 or it >= 2:
                c.wait(T, se, c.cnt[se] - 1)
            for k in range(KD):
                inst = T.matmul(bank[:, 0:TCW],
                                W1S[:, 4096 * k + 128 * it: 4096 * k + 128 * (it + 1)],
                                XNT[:, :].rearrange("q (k tt) -> q k tt", k=KD)[:, k, tcols],
                                start=(k == 0), stop=(k == KD - 1))
            c.inc(sa, inst)
            c.wait(S, sa)
            inst = S.activation(HTC[:, TCW * it: TCW * (it + 1)], bank[:, 0:TCW],
                                AF.Gelu_apprx_tanh, bias=B1S[:, it:it + 1],
                                scale=1.0)
            c.inc(se, inst)
        for tt in range(NTT):
            t = NTT * tc + tt
            slot = t % 2
            xrt = XRT[:, 1024 * slot: 1024 * (slot + 1)]
            if t < 2:
                Y.wait_ge(c.sem(p + "lnv"), 3 * NT)
                Y.wait_ge(c.sem(p + "lns"), 2 * NT)
            if t >= 2:
                Y.wait_ge(c.sem(sw), 2 * (t - 1))
            c.inc(sxp[slot], Y.dma_start(out=xrt, in_=x_tiled[:, t, :]), 16)
            for o2 in range(2):
                bank = PS[4 + o2]
                if t >= 1:
                    c.wait(T, sw, 2 * (t - 1) + o2 + 1)
                c.wait(T, se, c.cnt[se])
                for it in range(32):
                    inst = T.matmul(bank[:, 0:512],
                                    HTC[:, TCW * it + 128 * tt: TCW * it + 128 * (tt + 1)],
                                    W2S[:, 1024 * it + 512 * o2: 1024 * it + 512 * (o2 + 1)],
                                    start=(it == 0), stop=(it == 31))
                c.inc(sb, inst)
                c.wait(V, sb)
                c.wait(V, sxp[slot])
                if t < 2:
                    c.wait(V, p + "lns", 2 * NT)
                if t >= 2:
                    c.wait(V, syp[slot], 16 * (t // 2))
                ov = OUT[:, 1024 * slot + 512 * o2: 1024 * slot + 512 * (o2 + 1)]
                ee = c.ve(V.tensor_add(ov, bank[:, 0:512],
                                       B2B[:, 512 * o2: 512 * (o2 + 1)]))
                c.vw(V, ee)
                inst = V.scalar_tensor_tensor(ov, ov, FQ[:, 2 * t + 1: 2 * t + 2],
                                              xrt[:, 512 * o2: 512 * (o2 + 1)],
                                              ALU.mult, ALU.add)
                c.inc(sw, inst)
            c.wait(Y, sw)
            c.inc(syp[slot], Y.dma_start(out=y_tiled[:, t, :],
                                         in_=OUT[:, 1024 * slot: 1024 * (slot + 1)]), 16)
    c.wait(Y, syp[0])
    c.wait(Y, syp[1])



SQRT32 = float(np.sqrt(32.0))


def emit_l2_attn(c, x_dram, y_dram, w, pools, p):
    V, S, T, G, Y = c.V, c.S, c.T, c.G, c.Y
    XNT = pools["XNT"]
    PS = pools["PS"]
    PT = pools["PT"]
    ST = pools["ST"]
    ONES = pools["ONES"]
    XTR = pools["XTR"]
    OUT = pools["OUT"]

    WQT = c.sbuf(p + "wq", [128, KD * 1024], BF16)
    WUT = c.sbuf(p + "wu", [128, KD * 1024], BF16)
    BQS = c.sbuf(p + "bq", [128, 8], F32)
    BUR = c.sbuf(p + "bu", [1, 1024], BF16)
    QTS = c.sbuf(p + "qt", [128, KD * 2048], BF16)
    QPT = c.sbuf(p + "qp", [128, 2048], BF16)
    QPM = c.sbuf(p + "qm_", [128, 16 * 64], BF16)
    SCF = c.sbuf(p + "sf", [128, 2048], F32)
    SCB = c.sbuf(p + "sb", [128, 2048], BF16)
    ATT = c.sbuf(p + "at", [128, 16 * 512], BF16)
    CT2 = c.sbuf(p + "c2", [128, 2048], BF16)
    CFT = c.sbuf(p + "cf", [128, KD * 2048], BF16)

    wl = p + "wl"
    load_bf16(c, WQT[:, :].rearrange("q (k i) -> q k i", k=KD),
              w["wqT"].rearrange("(k q) i -> q k i", q=128), sem=wl)
    load_bf16(c, WUT[:, :].rearrange("q (k i) -> q k i", k=KD),
              w["wuT"].rearrange("(k q) i -> q k i", q=128), sem=wl)
    load_f32(c, BQS[:, :], w["bq"].rearrange("(a q) -> q a", q=128), slow=True, sem=wl)
    load_bf16(c, BUR[:, :], w["bu"].rearrange("(o a) -> o a", o=1), sem=wl)
    wl_done = c.cnt[wl]

    emit_ln_transpose(c, x_dram, XNT, pools, p + "ln")
    lne = p + "lne"
    xnt_done = c.cnt[lne]

    c.wait(V, wl, wl_done)
    inst = V.tensor_scalar_mul(BQS[:, :], BQS[:, :], 1.0 / SQRT32)
    bq_done = c.inc(p + "bqv", inst)

    qm, qe = p + "qm", p + "qe"
    c.sem(qm), c.sem(qe)
    c.wait(T, wl, wl_done)
    c.wait(T, lne, xnt_done)
    c.acquire(T, "PS0"), c.acquire(T, "PS1")
    c.wait(S, p + "bqv", bq_done)
    for oc in range(KD):
        for tch in range(4):
            i = oc * 4 + tch
            bank = PS[i % 2]
            if i >= 2:
                c.wait(T, qe, c.cnt[qe] - 1)
            for k in range(KD):
                inst = T.matmul(bank[:, 0:512],
                                WQT[:, 1024 * k + 128 * oc: 1024 * k + 128 * (oc + 1)],
                                XNT[:, 2048 * k + 512 * tch: 2048 * k + 512 * (tch + 1)],
                                start=(k == 0), stop=(k == KD - 1))
            c.inc(qm, inst)
            c.wait(S, qm)
            inst = S.activation(QTS[:, 2048 * oc + 512 * tch: 2048 * oc + 512 * (tch + 1)],
                                bank[:, 0:512], AF.Identity,
                                bias=BQS[:, oc:oc + 1], scale=1.0 / SQRT32)
            c.inc(qe, inst)
    qts_done = c.cnt[qe]

    qp, pm, pe = p + "qp", p + "pm", p + "pe"
    sv, ss = p + "sv", p + "ss"
    cm, ce = p + "cm", p + "ce"
    cfv = p + "cfv"
    for k in (qp, pm, pe, sv, ss, cm, ce, cfv):
        c.sem(k)
    c.acquire(V, "PT0"), c.acquire(V, "PT1")
    c.acquire(T, "PS2"), c.acquire(T, "PS3"), c.acquire(T, "PS4"), c.acquire(T, "PS5")
    ptctr = 0

    def pt_use(n_writes_fn, evac_fn):
        nonlocal ptctr
        bank = PT[ptctr % 2]
        if ptctr >= 2:
            c.wait(T, pe, ptctr - 1)
        n_writes_fn(bank)
        c.inc(pm, T_last[0])
        c.wait(S, pm)
        evac_fn(bank)
        ptctr += 1

    T_last = [None]

    for ch in range(NT):
        c.wait(V, qe, qts_done)
        if ch > 0:
            c.wait(V, qm, 32 + 16 * ch)
        qptv = QPT[:, :].rearrange("q (t cc) -> q cc t", cc=16)
        for cc in range(16):
            src = QTS[64 * (cc % 2):64 * (cc % 2) + 64,
                      2048 * (cc // 2) + 128 * ch: 2048 * (cc // 2) + 128 * (ch + 1)]
            inst = V.tensor_copy(qptv[0:64, cc, :], src)
        c.inc(qp, inst)
        c.wait(T, qp)
        for half in range(2):
            def wr(bank, half=half):
                for i in range(8):
                    kk = 8 * half + i
                    T_last[0] = T.transpose(bank[:, 64 * i:64 * (i + 1)],
                                            QPT[0:64, 128 * kk:128 * (kk + 1)],
                                            pools["IDN"][0:64, 0:64])
            def ev(bank, half=half):
                inst = S.activation(QPM[:, 512 * half:512 * (half + 1)],
                                    bank[:, 0:512], AF.Copy)
                c.inc(pe, inst)
            pt_use(wr, ev)
        for g in range(4):
            for rqi in range(4):
                rb = 4 * g + rqi
                c.wait(T, pe, ptctr)
                if ch > 0 or g > 0 or rqi > 0:
                    c.wait(T, ss, c.cnt[ss])
                    c.wait(T, sv, c.cnt[sv])
                    c.wait(T, ce, c.cnt[ce])
                for n in range(4):
                    inst = T.matmul(PS[2 + n][:, 0:512],
                                    QPT[0:64, 128 * rb:128 * (rb + 1)],
                                    QPT[0:64, 512 * n:512 * (n + 1)],
                                    start=True, stop=True)
                c.inc(qm, inst)
                c.wait(V, qm)
                e1 = c.ve(V.reduce_max(ST[:, 0:1], PS[2][:, 0:512], axis=AX.X))
                e2 = c.ve(V.reduce_max(ST[:, 1:2], PS[3][:, 0:512], axis=AX.X))
                e3 = c.ve(V.reduce_max(ST[:, 2:3], PS[4][:, 0:512], axis=AX.X))
                e4 = c.ve(V.reduce_max(ST[:, 3:4], PS[5][:, 0:512], axis=AX.X))
                c.vw(V, e4)
                e5 = c.ve(V.tensor_max(ST[:, 4:5], ST[:, 0:1], ST[:, 1:2]))
                e6 = c.ve(V.tensor_max(ST[:, 5:6], ST[:, 2:3], ST[:, 3:4]))
                c.vw(V, e6)
                e7 = c.ve(V.tensor_max(ST[:, 6:7], ST[:, 4:5], ST[:, 5:6]))
                c.vw(V, e7)
                inst = V.tensor_scalar_mul(ST[:, 7:8], ST[:, 6:7], -1.0)
                c.inc(sv, inst)
                c.wait(S, sv)
                for n in range(4):
                    inst = S.activation(SCF[:, 512 * n:512 * (n + 1)],
                                        PS[2 + n][:, 0:512], AF.Exp,
                                        bias=ST[:, 7:8], scale=1.0,
                                        accum_out=ST[:, 8 + n:9 + n])
                c.inc(ss, inst)
                c.wait(V, ss)
                e1 = c.ve(V.tensor_add(ST[:, 12:13], ST[:, 8:9], ST[:, 9:10]))
                e2 = c.ve(V.tensor_add(ST[:, 13:14], ST[:, 10:11], ST[:, 11:12]))
                c.vw(V, e2)
                e3 = c.ve(V.tensor_add(ST[:, 14:15], ST[:, 12:13], ST[:, 13:14]))
                c.vw(V, e3)
                e4 = c.ve(V.reciprocal(ST[:, 15:16], ST[:, 14:15]))
                c.vw(V, e4)
                if ch > 0 or g > 0 or rqi > 0:
                    c.wait(V, pm, c.cnt[pm])
                inst = V.tensor_scalar_mul(SCB[:, :], SCF[:, :], ST[:, 15:16])
                c.inc(sv, inst)
                c.wait(T, sv)
                if rqi == 0 and (ch > 0 or g > 0):
                    c.wait(S, cm, c.cnt[cm])
                for half in range(2):
                    def wr(bank, half=half):
                        for i in range(8):
                            kk = 8 * half + i
                            T_last[0] = T.transpose(bank[:, 128 * i:128 * (i + 1)],
                                                    SCB[:, 128 * kk:128 * (kk + 1)],
                                                    pools["IDN"][:, :])
                    def ev(bank, half=half, rqi=rqi):
                        dst = ATT[:, :].rearrange("q (kk cc) -> q kk cc", kk=16)[
                            :, 8 * half:8 * (half + 1), 128 * rqi:128 * (rqi + 1)]
                        inst = S.activation(dst,
                                            bank[:, :].rearrange("q (kk cc) -> q kk cc", kk=8),
                                            AF.Copy)
                        c.inc(pe, inst)
                    pt_use(wr, ev)
            c.wait(T, pe, ptctr)
            if ch > 0 or g > 0:
                c.wait(T, ce, c.cnt[ce])
            for kk in range(16):
                inst = T.matmul(PS[2][0:64, 0:512],
                                QPM[:, 64 * kk:64 * (kk + 1)],
                                ATT[:, 512 * kk:512 * (kk + 1)],
                                start=(kk == 0), stop=(kk == 15))
            c.inc(cm, inst)
            c.wait(S, cm)
            if g == 0:
                c.wait(S, cfv, 16 * ch)
            inst = S.activation(CT2[0:64, 512 * g:512 * (g + 1)], PS[2][0:64, 0:512],
                                AF.Copy)
            c.inc(ce, inst)
        c.wait(V, ce, c.cnt[ce])
        ct2v = CT2[:, :].rearrange("q (t cc) -> q cc t", cc=16)
        for cc in range(16):
            dst = CFT[64 * (cc % 2):64 * (cc % 2) + 64,
                      2048 * (cc // 2) + 128 * ch: 2048 * (cc // 2) + 128 * (ch + 1)]
            inst = V.tensor_copy(dst, ct2v[0:64, cc, :])
            c.inc(cfv, inst)
    cft_done = c.cnt[cfv]

    om, ow = p + "om", p + "ow"
    sxp = [p + "ox0", p + "ox1"]
    syp = [p + "oy0", p + "oy1"]
    for k in [om, ow] + sxp + syp:
        c.sem(k)
    x_tiled = x_dram.rearrange("(t q) f -> q t f", q=128)
    y_tiled = y_dram.rearrange("(t q) f -> q t f", q=128)
    c.wait(T, cfv, cft_done)
    c.wait(T, ss, c.cnt[ss])
    for t in range(NT):
        slot = t % 2
        xrt = XTR[:, 1024 * slot: 1024 * (slot + 1)]
        if t < 2:
            Y.wait_ge(c.sem(p + "lnv"), 3 * NT)
            Y.wait_ge(c.sem(p + "lns"), 2 * NT)
        if t >= 2:
            Y.wait_ge(c.sem(ow), 2 * (t - 1))
        c.inc(sxp[slot], Y.dma_start(out=xrt, in_=x_tiled[:, t, :]), 16)
        for o2 in range(2):
            bank = PS[4 + o2]
            if t >= 1:
                c.wait(T, ow, 2 * (t - 1) + o2 + 1)
            inst = T.matmul(bank[:, 0:512], ONES[:, :],
                            BUR[:, 512 * o2:512 * (o2 + 1)], start=True, stop=False)
            for k in range(KD):
                inst = T.matmul(bank[:, 0:512],
                                CFT[:, 2048 * k + 128 * t: 2048 * k + 128 * (t + 1)],
                                WUT[:, 1024 * k + 512 * o2: 1024 * k + 512 * (o2 + 1)],
                                start=False, stop=(k == KD - 1))
            c.inc(om, inst)
            c.wait(V, om)
            c.wait(V, sxp[slot])
            if t >= 2:
                c.wait(V, syp[slot], 16 * (t // 2))
            ov = OUT[:, 1024 * slot + 512 * o2: 1024 * slot + 512 * (o2 + 1)]
            inst = V.scalar_tensor_tensor(ov, bank[:, 0:512], 1.0,
                                          xrt[:, 512 * o2: 512 * (o2 + 1)],
                                          ALU.mult, ALU.add)
            c.inc(ow, inst)
        c.wait(Y, ow)
        c.inc(syp[slot], Y.dma_start(out=y_tiled[:, t, :],
                                     in_=OUT[:, 1024 * slot: 1024 * (slot + 1)]), 16)
    c.release("XNT", qm, 32)
    c.release("ST", ss, c.cnt[ss])
    c.release("PS0", qe, qts_done)
    c.release("PS1", qe, qts_done)
    c.release("PS2", ce, c.cnt[ce])
    c.release("PS3", ss, c.cnt[ss])
    c.release("PS4", ow, c.cnt[ow])
    c.release("PS5", ow, c.cnt[ow])
    c.release("PT0", pe, c.cnt[pe])
    c.release("PT1", pe, c.cnt[pe])
    c.release("XTR", ow, c.cnt[ow])
    c.release("OUT", syp[1], c.cnt[syp[1]])
    c.wait(Y, syp[0])
    c.wait(Y, syp[1])



def emit_l1_ffn(c, x_dram, y_dram, w, pools, p, h_scratch):
    V, S, T, G, Y = c.V, c.S, c.T, c.G, c.Y
    XNT = pools["XNT"]
    PS = pools["PS"]
    PT = pools["PT"]
    ST = pools["ST"]
    IDN = pools["IDN"]
    ONES = pools["ONES"]
    ONESF = pools["ONESF"]
    XTR = pools["XTR"]
    OUT = pools["OUT"]

    CMROW = c.sbuf(p + "cm", [1, 512], F32)
    CMB = c.sbuf(p + "cb", [128, 2048], F32)
    combD = c.nc.dram_tensor(p + "combD", [8, 2048], F32).ap()
    OACC = c.sbuf(p + "oa", [128, 8 * 2048], F32)
    inner = ExitStack()
    def isbuf(name, shape, dtype):
        return inner.enter_context(c.nc.sbuf_tensor(name, shape, dtype))
    SQD = isbuf(p + "sqd", [128, 1024], F32)
    XB = isbuf(p + "xb", [128, NT * 1024], BF16)
    AWT = isbuf(p + "aw", [128, KD * 1024], BF16)
    GWT = isbuf(p + "gw", [128, KD * 8], BF16)
    LNR = isbuf(p + "lnr", [1, 4 * 1024], BF16)
    GB = isbuf(p + "gb", [128, 4 * 1024], F32)
    X32 = pools["OUT"][:, 0:1024]

    NB = 4
    wl = p + "wl"
    load_bf16(c, AWT[:, :].rearrange("q (k i) -> q k i", k=KD),
              w["attn_wT"].rearrange("(k q) i -> q k i", q=128), sem=wl)
    load_bf16(c, GWT[:, :].rearrange("q (k i) -> q k i", k=KD),
              w["gate_wT"].rearrange("(k q) i -> q k i", q=128), sem=wl)
    load_bf16(c, LNR[:, :].rearrange("o (j a) -> o j a", j=4),
              w["ln_rows"].rearrange("(o j) a -> o j a", o=1), sem=wl)
    wl_done = c.cnt[wl]

    c.wait(T, wl, wl_done)
    c.acquire(T, "PS0"), c.acquire(T, "PS1")
    emit_bcast_row(c, GB[:, :], LNR[:, :], pools, 4096, p)
    gb_done = c.cnt[p + "bv"]

    sx, sv, ss, sm, se = (p + "1" + k for k in "xvsme")
    for k in (sx, sv, ss, sm, se):
        c.sem(k)
    x_tiled = x_dram.rearrange("(t q) f -> q t f", q=128)
    c.acquire(Y, "XTR")
    c.acquire(Y, "OUT")
    c.acquire(S, "OUT")
    c.acquire(S, "XNT")
    c.acquire(V, "ST")
    c.acquire(T, "PT0"), c.acquire(T, "PT1")
    XNB = pools["XNB"]
    SQ = SQD[:, :]

    def ln_stats(xsrc, t, col0):
        st = ST
        e1 = c.ve(V.reduce_sum(st[:, col0:col0 + 1], xsrc, axis=AX.X))
        c.vw(V, e1)
        e_mu = c.ve(V.tensor_scalar_mul(st[:, col0 + 1:col0 + 2],
                                        st[:, col0:col0 + 1], 1.0 / D))
        inst = S.activation(SQ, xsrc, AF.Square, accum_out=st[:, col0 + 2:col0 + 3])
        c.inc(ss, inst)
        c.wait(V, ss)
        e2 = c.ve(V.tensor_scalar_mul(st[:, col0 + 3:col0 + 4],
                                      st[:, col0 + 2:col0 + 3], 1.0 / D))
        c.vw(V, e_mu)
        e3 = c.ve(V.tensor_mul(st[:, col0 + 4:col0 + 5], st[:, col0 + 1:col0 + 2],
                               st[:, col0 + 1:col0 + 2]))
        c.vw(V, e3)
        e4 = c.ve(V.tensor_sub(st[:, col0 + 5:col0 + 6], st[:, col0 + 3:col0 + 4],
                               st[:, col0 + 4:col0 + 5]))
        c.vw(V, e4)
        inst = V.tensor_scalar_add(st[:, col0 + 5:col0 + 6],
                                   st[:, col0 + 5:col0 + 6], 1e-5)
        c.inc(sv, inst)
        c.wait(S, sv)
        inst = S.activation(st[:, col0 + 6:col0 + 7], st[:, col0 + 5:col0 + 6], AF.Sqrt)
        c.inc(ss, inst)
        c.wait(V, ss)
        e5 = c.ve(V.reciprocal(st[:, col0 + 7:col0 + 8], st[:, col0 + 6:col0 + 7]))
        c.vw(V, e5)
        e6 = c.ve(V.scalar_tensor_tensor(st[:, col0 + 8:col0 + 9],
                                         st[:, col0 + 1:col0 + 2], -1.0,
                                         st[:, col0 + 7:col0 + 8],
                                         ALU.mult, ALU.mult))
        c.vw(V, e6)
        return st[:, col0 + 7:col0 + 8], st[:, col0 + 8:col0 + 9]

    c.wait(V, p + "bv", gb_done)
    prev_x32_ve = prev_x32_ss = None
    for t in range(NT):
        if t >= 1:
            Y.wait_ge(c.sem("Ve"), prev_x32_ve)
            Y.wait_ge(c.sem(ss), prev_x32_ss)
        c.inc(sx, Y.dma_start(out=X32, in_=x_tiled[:, t, :]), 16)
        c.wait(V, sx)
        c.wait(S, sx)
        if t >= 1:
            c.wait(S, sv, 4 * t)
        rstd, negmur = ln_stats(X32, t, 0)
        XT32 = OUT[:, 1024:2048]
        e7 = c.ve(V.tensor_scalar(XT32, X32, rstd, negmur, ALU.mult, ALU.add))
        prev_x32_ve = e7
        prev_x32_ss = c.cnt[ss] - 1
        c.vw(V, e7)
        inst = V.tensor_copy(XB[:, 1024 * t:1024 * (t + 1)], XT32)
        c.inc(sv, inst)
        if t >= 1:
            c.wait(S, sv, 4 * t + 2)
        c.vw(S, e7)
        rstd1, negmur1 = ln_stats(XT32, t, 9)
        xn = XNB[:, 0:1024]
        e8 = c.ve(V.tensor_scalar(xn, XT32, rstd1, negmur1, ALU.mult, ALU.add))
        c.vw(V, e8)
        e9 = c.ve(V.tensor_mul(xn, xn, GB[:, 0:1024]))
        c.vw(V, e9)
        if t >= 1:
            c.wait(V, sm, t)
        inst = V.tensor_add(xn, xn, GB[:, 1024:2048])
        c.inc(sv, inst)
        pt = PT[t % 2]
        c.wait(T, sv)
        if t >= 2:
            c.wait(T, se, t - 1)
        for f in range(KD):
            inst = T.transpose(pt[:, f * 128:(f + 1) * 128],
                               xn[:, f * 128:(f + 1) * 128], IDN[:, :])
        c.inc(sm, inst)
        c.wait(S, sm)
        xnt_dst = XNT[:, :].rearrange("q (f tt) -> q f tt", f=KD)[:, :, t * 128:(t + 1) * 128]
        inst = S.activation(xnt_dst, pt[:, :].rearrange("q (f tt) -> q f tt", f=KD),
                            AF.Copy)
        c.inc(se, inst)
    ln1t_done = c.cnt[se]

    hm, hw = p + "hm", p + "hv"
    hyp = [p + "hy0", p + "hy1"]
    s2v, s2s, s2m, s2e = (p + "2" + k for k in "vsme")
    for k in [hm, hw, s2v, s2s, s2m, s2e] + hyp:
        c.sem(k)
    h_tiled = h_scratch.rearrange("(t q) f -> q t f", q=128)
    c.wait(T, se, ln1t_done)
    for t in range(NT):
        slot = t % 2
        ho = OUT[:, 1024 * slot:1024 * (slot + 1)]
        for o2 in range(2):
            bank = PS[2 + ((2 * t + o2) % 4)]
            if 2 * t + o2 >= 4:
                c.wait(T, hw, 2 * t + o2 - 3)
            for k in range(KD):
                inst = T.matmul(bank[:, 0:512],
                                XNT[:, 2048 * k + 128 * t: 2048 * k + 128 * (t + 1)],
                                AWT[:, 1024 * k + 512 * o2: 1024 * k + 512 * (o2 + 1)],
                                start=(k == 0), stop=(k == KD - 1))
            c.inc(hm, inst)
            c.wait(V, hm)
            if t == 0 and o2 == 0:
                c.wait(V, ss, 4 * NT)
                c.wait(V, sv, 4 * NT)
            if t >= 2 and o2 == 0:
                c.wait(V, hyp[slot], 16 * (t // 2))
            inst = V.scalar_tensor_tensor(ho[:, 512 * o2:512 * (o2 + 1)],
                                          bank[:, 0:512], 1.0,
                                          XB[:, 1024 * t + 512 * o2:1024 * t + 512 * (o2 + 1)],
                                          ALU.mult, ALU.add)
            c.inc(hw, inst)
        c.wait(Y, hw)
        c.inc(hyp[slot], Y.dma_start(out=h_tiled[:, t, :], in_=ho), 16)
        c.wait(V, hw, c.cnt[hw])
        if t >= 1:
            c.wait(S, s2v, 2 * t)
        st = ST
        e1 = c.ve(V.reduce_sum(st[:, 0:1], ho, axis=AX.X))
        c.vw(V, e1)
        e_mu = c.ve(V.tensor_scalar_mul(st[:, 1:2], st[:, 0:1], 1.0 / D))
        c.wait(S, hw, c.cnt[hw])
        inst = S.activation(SQ, ho, AF.Square, accum_out=st[:, 2:3])
        c.inc(s2s, inst)
        c.wait(V, s2s)
        e2 = c.ve(V.tensor_scalar_mul(st[:, 3:4], st[:, 2:3], 1.0 / D))
        c.vw(V, e_mu)
        e3 = c.ve(V.tensor_mul(st[:, 4:5], st[:, 1:2], st[:, 1:2]))
        c.vw(V, e3)
        e4 = c.ve(V.tensor_sub(st[:, 5:6], st[:, 3:4], st[:, 4:5]))
        c.vw(V, e4)
        inst = V.tensor_scalar_add(st[:, 5:6], st[:, 5:6], 1e-5)
        c.inc(s2v, inst)
        c.wait(S, s2v)
        inst = S.activation(st[:, 6:7], st[:, 5:6], AF.Sqrt)
        c.inc(s2s, inst)
        c.wait(V, s2s)
        e5 = c.ve(V.reciprocal(st[:, 7:8], st[:, 6:7]))
        c.vw(V, e5)
        e6 = c.ve(V.scalar_tensor_tensor(st[:, 8:9], st[:, 1:2], -1.0, st[:, 7:8],
                                         ALU.mult, ALU.mult))
        c.vw(V, e6)
        xn = XNB[:, 0:1024]
        e7 = c.ve(V.tensor_scalar(xn, ho, st[:, 7:8], st[:, 8:9], ALU.mult, ALU.add))
        c.vw(V, e7)
        e8 = c.ve(V.tensor_mul(xn, xn, GB[:, 2048:3072]))
        c.vw(V, e8)
        if t >= 1:
            c.wait(V, s2m, t)
        inst = V.tensor_add(xn, xn, GB[:, 3072:4096])
        c.inc(s2v, inst)
        pt = PT[t % 2]
        c.wait(T, s2v)
        if t >= 2:
            c.wait(T, s2e, t - 1)
        for f in range(KD):
            inst = T.transpose(pt[:, f * 128:(f + 1) * 128],
                               xn[:, f * 128:(f + 1) * 128], IDN[:, :])
        c.inc(s2m, inst)
        c.wait(S, s2m)
        xnt_dst = XNT[:, :].rearrange("q (f tt) -> q f tt", f=KD)[:, :, t * 128:(t + 1) * 128]
        inst = S.activation(xnt_dst, pt[:, :].rearrange("q (f tt) -> q f tt", f=KD),
                            AF.Copy)
        c.inc(s2e, inst)
    ln2t_done = c.cnt[s2e]

    rm, rv, rs = p + "rm", p + "rv", p + "rs"
    for k in (rm, rv, rs):
        c.sem(k)
    c.wait(T, s2e, ln2t_done)
    for t in range(NT):
        bank = PS[t % 2]
        if t >= 2:
            c.wait(T, rv, 2 * (t - 2) + 1)
        for k in range(KD):
            inst = T.matmul(bank[:, 0:8],
                            XNT[:, 2048 * k + 128 * t: 2048 * k + 128 * (t + 1)],
                            GWT[:, 8 * k: 8 * (k + 1)],
                            start=(k == 0), stop=(k == KD - 1))
        c.inc(rm, inst)
        c.wait(V, rm)
        st = ST
        e0 = c.ve(V.tensor_copy(st[:, 0:8], bank[:, 0:8]))
        c.vw(V, e0)
        e1 = c.ve(V.reduce_max(st[:, 8:9], st[:, 0:8], axis=AX.X))
        c.vw(V, e1)
        if t >= 1:
            c.wait(V, p + "cmy", 16 * t)
        e2 = c.ve(V.tensor_scalar(st[:, 9:17], st[:, 0:8], st[:, 8:9], None, ALU.is_ge))
        c.vw(V, e2)
        e3 = c.ve(V.scalar_tensor_tensor(st[:, 17:25], st[:, 9:17], BIG_NEG,
                                         st[:, 0:8], ALU.mult, ALU.add))
        c.vw(V, e3)
        e4 = c.ve(V.reduce_max(st[:, 25:26], st[:, 17:25], axis=AX.X))
        c.vw(V, e4)
        e5 = c.ve(V.tensor_scalar(st[:, 26:34], st[:, 0:8], st[:, 25:26], None, ALU.is_ge))
        e6 = c.ve(V.tensor_sub(st[:, 34:35], st[:, 25:26], st[:, 8:9]))
        c.vw(V, e6)
        inst = V.tensor_copy(st[:, 35:36], st[:, 34:35])
        c.inc(rv, inst)
        c.wait(S, rv)
        inst = S.activation(st[:, 36:37], st[:, 35:36], AF.Exp)
        c.inc(rs, inst)
        c.wait(V, rs)
        e7 = c.ve(V.tensor_scalar_add(st[:, 37:38], st[:, 36:37], 1.0))
        c.vw(V, e7)
        e8 = c.ve(V.reciprocal(st[:, 38:39], st[:, 37:38]))
        c.vw(V, e8)
        e9 = c.ve(V.tensor_mul(st[:, 39:40], st[:, 36:37], st[:, 38:39]))
        c.vw(V, e9)
        ea = c.ve(V.tensor_sub(st[:, 40:41], st[:, 38:39], st[:, 39:40]))
        c.vw(V, ea)
        c.vw(V, e5)
        eb = c.ve(V.tensor_scalar_mul(st[:, 41:49], st[:, 9:17], st[:, 40:41]))
        c.vw(V, eb)
        inst = V.scalar_tensor_tensor(st[:, 9:17], st[:, 26:34],
                                      st[:, 39:40], st[:, 41:49], ALU.mult, ALU.add)
        c.inc(rv, inst)
        c.wait(Y, rv)
        c.inc(p + "cmy",
              Y.dma_start(out=combD.rearrange("e (tl q) -> q tl e", q=128)[:, t, :],
                          in_=st[:, 9:17],
                          allow_slow_non_contiguous=True), 16)
    comb_done = c.cnt[rv]
    comb_stored = c.cnt[p + "cmy"]

    lg, ld_ = p + "lg", p + "ld"
    gm, ge = p + "gm", p + "ge"
    gv = p + "gv"
    dm, dv = p + "dm", p + "dv"
    bm, bv = p + "cbm", p + "cbv"
    for k in (lg, ld_, gm, ge, gv, dm, dv, bm, bv):
        c.sem(k)
    gwt = w["gpT"]
    uwt = w["upT"]
    dwt = w["dpT"]
    G.wait_ge(c.sem(hw), 2 * NT)
    G.wait_ge(c.sem(hyp[0]), 16 * (NT // 2))
    G.wait_ge(c.sem(hyp[1]), 16 * (NT // 2))
    G.wait_ge(c.sem(hm), 2 * NT)
    G.wait_ge(c.sem(s2v), 2 * NT)
    G.wait_ge(c.sem(rm), NT)
    G.wait_ge(c.sem(p + "cmy"), comb_stored)
    G.wait_ge(c.sem(ss), 4 * NT)
    G.wait_ge(c.sem("Ve"), c.cnt["Ve"])
    inner.close()
    WG = c.sbuf(p + "wg", [128, KD * 1024], BF16)
    WU = c.sbuf(p + "wu", [128, KD * 1024], BF16)
    WD = c.sbuf(p + "wd", [128, 8 * 1024], BF16)
    ACT = c.sbuf(p + "ac", [128, 8 * 2048], BF16)
    SG = c.sbuf(p + "sg", [128, 2 * 512], BF16)
    UU = pools["OUT"][:, 0:1024]
    for e in range(8):
        if e == 0:
            c.acquire(T, "PS4"), c.acquire(T, "PS5")
            G.wait_ge(c.sem(p + "cmy"), comb_stored)
        for q4 in range(4):
            if e > 0 or q4 > 0:
                G.wait_ge(c.sem(bm), c.cnt[bm])
            c.inc(p + "cmr", G.dma_start(out=CMROW[:, :],
                                         in_=combD[e:e + 1, 512 * q4:512 * (q4 + 1)]), 16)
            c.wait(T, p + "cmr")
            if e > 0 or q4 > 0:
                c.wait(T, bv, c.cnt[bv])
            if e > 0 and q4 == 0:
                c.wait(T, gv, c.cnt[gv])
                c.wait(T, ge, c.cnt[ge])
            inst = T.matmul(PS[4][:, 0:512], ONESF[:, :], CMROW[0:1, :],
                            start=True, stop=True)
            c.inc(bm, inst)
            c.wait(V, bm)
            inst = V.tensor_copy(CMB[:, 512 * q4:512 * (q4 + 1)], PS[4][:, 0:512])
            c.inc(bv, inst)
        cmb_ready = c.cnt[bv]
        for b in range(NB):
            if e > 0 or b > 0:
                G.wait_ge(c.sem(gm), c.cnt[gm])
            c.inc(lg, G.dma_start(out=WG[:, :].rearrange("q (k i) -> q k i", k=KD),
                                  in_=gwt[e].rearrange("(k q) i -> q k i", q=128)[:, :, 1024 * b:1024 * (b + 1)]), 16)
            c.inc(lg, G.dma_start(out=WU[:, :].rearrange("q (k i) -> q k i", k=KD),
                                  in_=uwt[e].rearrange("(k q) i -> q k i", q=128)[:, :, 1024 * b:1024 * (b + 1)]), 16)
            if e > 0 or b > 0:
                G.wait_ge(c.sem(dm), c.cnt[dm])
            c.inc(ld_, G.dma_start(out=WD[:, :].rearrange("q (i h) -> q i h", i=8),
                                   in_=dwt[e].rearrange("(i q) h -> q i h", q=128)[:, 8 * b:8 * (b + 1), :]), 16)
            c.wait(T, lg, c.cnt[lg])
            if b == 0:
                c.wait(T, bv, cmb_ready)
            c.wait(V, bv, cmb_ready)
            for itl in range(KD):
                for tch in range(4):
                    j = itl * 4 + tch
                    gbank, ubank = PS[4], PS[5]
                    if e > 0 or b > 0 or j > 0:
                        c.wait(T, ge, c.cnt[ge])
                        c.wait(T, gv, c.cnt[gv])
                    for k in range(KD):
                        inst = T.matmul(gbank[:, 0:512],
                                        WG[:, 1024 * k + 128 * itl: 1024 * k + 128 * (itl + 1)],
                                        XNT[:, 2048 * k + 512 * tch: 2048 * k + 512 * (tch + 1)],
                                        start=(k == 0), stop=(k == KD - 1))
                    for k in range(KD):
                        inst2 = T.matmul(ubank[:, 0:512],
                                         WU[:, 1024 * k + 128 * itl: 1024 * k + 128 * (itl + 1)],
                                         XNT[:, 2048 * k + 512 * tch: 2048 * k + 512 * (tch + 1)],
                                         start=(k == 0), stop=(k == KD - 1))
                    c.inc(gm, inst2)
                    slot = j % 2
                    c.wait(S, gm)
                    inst = S.activation(SG[:, 512 * slot:512 * (slot + 1)], gbank[:, 0:512],
                                        AF.Silu)
                    c.inc(ge, inst)
                    c.wait(V, gm)
                    if e == 0 and b == 0 and j == 0:
                        c.wait(V, hyp[0], 16 * (NT // 2))
                        c.wait(V, hyp[1], 16 * (NT // 2))
                        c.wait(V, s2s, 2 * NT)
                        c.wait(V, s2v, 2 * NT)
                    ee = c.ve(V.tensor_mul(UU[:, 512 * slot:512 * (slot + 1)],
                                           ubank[:, 0:512], CMB[:, 512 * tch:512 * (tch + 1)]))
                    c.wait(V, ge)
                    c.vw(V, ee)
                    inst = V.tensor_mul(ACT[:, 2048 * itl + 512 * tch: 2048 * itl + 512 * (tch + 1)],
                                        SG[:, 512 * slot:512 * (slot + 1)],
                                        UU[:, 512 * slot:512 * (slot + 1)])
                    c.inc(gv, inst)
            c.wait(T, ld_, c.cnt[ld_])
            c.wait(T, gv, c.cnt[gv])
            for tch in range(4):
                for hc in range(8):
                    j2 = tch * 8 + hc
                    bank = PS[2 + (j2 % 2)]
                    if e == 0 and b == 0 and j2 < 2:
                        c.acquire(T, "PS" + str(2 + j2))
                    if e > 0 or b > 0 or j2 >= 2:
                        c.wait(T, dv, c.cnt[dv] - 1)
                    for it in range(KD):
                        inst = T.matmul(bank[:, 0:512],
                                        WD[:, 1024 * it + 128 * hc: 1024 * it + 128 * (hc + 1)],
                                        ACT[:, 2048 * it + 512 * tch: 2048 * it + 512 * (tch + 1)],
                                        start=(it == 0), stop=(it == KD - 1))
                    c.inc(dm, inst)
                    c.wait(V, dm)
                    dst = OACC[:, 2048 * hc + 512 * tch: 2048 * hc + 512 * (tch + 1)]
                    if e == 0 and b == 0:
                        inst = V.tensor_copy(dst, bank[:, 0:512])
                    else:
                        inst = V.tensor_add(dst, dst, bank[:, 0:512])
                    c.inc(dv, inst)
    moe_done = c.cnt[dv]

    fm, fw = p + "fm", p + "fv"
    fxp = [p + "fx0", p + "fx1"]
    fhp = [p + "fh0", p + "fh1"]
    fyp = [p + "fy0", p + "fy1"]
    for k in [fm, fw] + fxp + fhp + fyp:
        c.sem(k)
    y_tiled = y_dram.rearrange("(t q) f -> q t f", q=128)
    c.wait(T, dv, moe_done)
    c.wait(T, ge, c.cnt[p + "ge"])
    c.wait(T, gv, c.cnt[p + "gv"])
    for t in range(NT):
        slot = t % 2
        xrt = XTR[:, 1024 * slot:1024 * (slot + 1)]
        if t < 2:
            Y.wait_ge(c.sem(p + "cmy"), comb_stored)
        if t >= 2:
            Y.wait_ge(c.sem(fw), 8 * (t - 1))
        c.inc(fxp[slot], Y.dma_start(out=xrt, in_=x_tiled[:, t, :]), 16)
        hslot = OUT[:, 1024 * slot:1024 * (slot + 1)]
        if t < 2:
            Y.wait_ge(c.sem(s2v), 2 * NT)
            Y.wait_ge(c.sem(s2s), 2 * NT)
            Y.wait_ge(c.sem(hyp[0]), 16 * (NT // 2))
            Y.wait_ge(c.sem(hyp[1]), 16 * (NT // 2))
            Y.wait_ge(c.sem(gv), c.cnt[gv])
        else:
            Y.wait_ge(c.sem(fyp[slot]), 16 * (t // 2))
        c.inc(fhp[slot], Y.dma_start(out=hslot, in_=h_tiled[:, t, :]), 16)
        for fhalf in range(2):
            bank = PS[4 + fhalf]
            if t >= 1:
                c.wait(T, fw, c.cnt[fw] - 4)
            for ff in range(4):
                f = 4 * fhalf + ff
                inst = T.transpose(bank[:, 128 * ff:128 * (ff + 1)],
                                   OACC[:, 2048 * f + 128 * t: 2048 * f + 128 * (t + 1)],
                                   IDNF(c, pools))
            c.inc(fm, inst)
            c.wait(V, fm)
            c.wait(V, fxp[slot])
            c.wait(V, fhp[slot])
            if t >= 2 and fhalf == 0:
                c.wait(V, fyp[slot], 16 * (t // 2))
            for ff in range(4):
                f = 4 * fhalf + ff
                seg = slice(128 * f, 128 * (f + 1))
                ee = c.ve(V.scalar_tensor_tensor(hslot[:, seg], hslot[:, seg], 1.0,
                                                 bank[:, 128 * ff:128 * (ff + 1)],
                                                 ALU.mult, ALU.add))
                c.vw(V, ee)
                inst = V.scalar_tensor_tensor(hslot[:, seg], hslot[:, seg], 0.5,
                                              xrt[:, seg], ALU.mult, ALU.add)
                c.inc(fw, inst)
        c.wait(Y, fw)
        c.inc(fyp[slot], Y.dma_start(out=y_tiled[:, t, :], in_=hslot), 16)
    c.release("XNT", gm, c.cnt[gm])
    c.release("ST", rs, c.cnt[rs])
    c.release("PS0", rv, comb_done)
    c.release("PS1", rv, comb_done)
    c.release("PS2", dv, moe_done)
    c.release("PS3", dv, moe_done)
    c.release("PS4", fw, c.cnt[fw])
    c.release("PS5", fw, c.cnt[fw])
    c.release("PT0", s2e, ln2t_done)
    c.release("PT1", s2e, ln2t_done)
    c.release("XTR", fw, c.cnt[fw])
    c.release("OUT", fyp[1], c.cnt[fyp[1]])
    c.wait(Y, fyp[0])
    c.wait(Y, fyp[1])


def IDNF(c, pools):
    return pools["IDNF"][:, :]



QUAT_SIGNS = [
    [(0, 1), (1, 1), (2, 1), (3, 1)],
    [(1, -1), (0, 1), (3, 1), (2, -1)],
    [(2, -1), (3, -1), (0, 1), (1, 1)],
    [(3, -1), (2, 1), (1, -1), (0, 1)],
]
QUAT_BSIGNS = [
    [1, -1, -1, -1],
    [1, 1, 1, -1],
    [1, -1, 1, 1],
    [1, 1, -1, 1],
]


def emit_l1_attn(c, x_dram, y_dram, w, pools, p):
    V, S, T, G, Y = c.V, c.S, c.T, c.G, c.Y
    XNT = pools["XNT"]
    PS = pools["PS"]
    PT = pools["PT"]
    ST = pools["ST"]
    IDN = pools["IDN"]
    ONES = pools["ONES"]
    XTR = pools["XTR"]
    OUT = pools["OUT"]
    SQ = pools["SQ"]

    BR = c.sbuf(p + "br", [1, 4 * 1024], BF16)
    CB = c.sbuf(p + "cbj", [128, 2], F32)
    VB = c.sbuf(p + "vb", [128, NT * 1024], BF16)
    CFT = c.sbuf(p + "cft", [128, KD * 2048], BF16)
    SCB = c.sbuf(p + "sb", [128, 2048], BF16)
    ATT = c.sbuf(p + "at", [128, 16 * 512], BF16)
    QTD = c.nc.dram_tensor(p + "qtd", [1024, 2048], BF16).ap()
    KTD = c.nc.dram_tensor(p + "ktd", [1024, 2048], BF16).ap()

    wl = p + "wl"
    mm_, me_ = p + "mm", p + "me"
    c.sem(mm_), c.sem(me_)

    innerA = ExitStack()
    WS = innerA.enter_context(c.nc.sbuf_tensor(p + "ws", [128, 4 * 2 * 256], BF16))
    MM = innerA.enter_context(c.nc.sbuf_tensor(p + "m", [128, KD * 1024], BF16))
    BS = innerA.enter_context(c.nc.sbuf_tensor(p + "bs", [1, 4 * 256], BF16))
    JS = innerA.enter_context(c.nc.sbuf_tensor(p + "js", [1, 1024], F32))
    QS = innerA.enter_context(c.nc.sbuf_tensor(p + "qs", [128, 1024], F32))
    NV = innerA.enter_context(c.nc.sbuf_tensor(p + "nv", [128, 2 * 256], F32))
    QN = innerA.enter_context(c.nc.sbuf_tensor(p + "qn", [128, 1024], BF16))
    QSTG = innerA.enter_context(c.nc.sbuf_tensor(p + "qstg", [128, 2 * 1024], BF16))

    load_f32(c, JS[:, :], w["J"].rearrange("(o a) -> o a", o=1), sem=wl)
    jl = c.cnt[wl]
    emit_ln_transpose(c, x_dram, XNT, pools, p + "ln")
    lne = p + "lne"
    xnt_done = c.cnt[lne]

    c.wait(S, wl, jl)
    inst = S.activation(JS[:, :], JS[:, :], AF.Sin)
    c.inc(p + "js_", inst)
    c.wait(V, p + "js_")
    e1 = c.ve(V.reduce_sum(ST[0:1, 0:1], JS[:, :], axis=AX.X))
    c.vw(V, e1)
    inst = V.tensor_scalar_mul(ST[0:1, 0:1], ST[0:1, 0:1], 0.1 / 1024.0)
    jv = c.inc(p + "jv", inst)
    c.wait(V, p + "jv", jv)
    inst = V.tensor_copy(BS[0:1, 0:1], ST[0:1, 0:1])
    jb = c.inc(p + "jv", inst)
    c.wait(T, p + "jv", jb)
    c.acquire(T, "PS0")
    inst = T.matmul(PS[0][:, 0:1], ONES[:, :], BS[0:1, 0:1], start=True, stop=True)
    c.inc(mm_, inst)
    c.wait(V, mm_)
    inst = V.tensor_copy(CB[:, 0:1], PS[0][:, 0:1])
    cj_done = c.inc(p + "jv", inst)

    sx, sv, ss = p + "qx", p + "qv", p + "qs_"
    qm, qe = p + "qm", p + "qe"
    tm, te = p + "tm", p + "te"
    for k in (sx, sv, ss, qm, qe, tm, te):
        c.sem(k)
    c.acquire(T, "PS1"), c.acquire(T, "PS2"), c.acquire(T, "PS3")
    c.acquire(T, "PT0"), c.acquire(T, "PT1")
    SCALE8 = 1.0 / (64.0 ** 0.25)

    def emit_proj(nameW, nameB, dst, norm, scale_fold, pidx):
        if pidx == 0:
            G.wait_ge(c.sem(mm_), 1)
        if pidx > 0:
            G.wait_ge(c.sem(qm), c.cnt[qm])
        load_bf16(c, WS[:, :].rearrange("q (wa o) -> q wa o", wa=8),
                  w[nameW].rearrange("w (a q) o -> q (w a) o", q=128), sem=wl)
        load_bf16(c, BS[:, :].rearrange("o (j a) -> o j a", j=4),
                  w[nameB].rearrange("(o w) a -> o w a", o=1), sem=wl)
        wld = c.cnt[wl]
        tyb = [c.cnt.get(p + "ty0", 0), c.cnt.get(p + "ty1", 0)]
        c.sem(p + "ty0"), c.sem(p + "ty1")
        c.wait(V, wl, wld)
        if pidx > 0:
            c.wait(V, qm, c.cnt[qm])
        wsv = WS[:, :].rearrange("q (w a o) -> q w a o", w=4, a=2)
        for cc in range(4):
            for a in range(2):
                for j in range(4):
                    wi, s = QUAT_SIGNS[cc][j]
                    dstv = MM[:, 1024 * (2 * cc + a) + 256 * j: 1024 * (2 * cc + a) + 256 * (j + 1)]
                    if s > 0:
                        inst = V.tensor_copy(dstv, wsv[:, wi, a, :])
                    else:
                        inst = V.tensor_scalar_mul(dstv, wsv[:, wi, a, :], -1.0)
        bsv = BS[:, :].rearrange("o (j a) -> o j a", j=4)
        for j in range(4):
            dstb = BR[0:1, 1024 * pidx + 256 * j: 1024 * pidx + 256 * (j + 1)]
            e1 = c.ve(V.tensor_copy(dstb, bsv[0:1, 0, :]) if QUAT_BSIGNS[j][0] > 0
                      else V.tensor_scalar_mul(dstb, bsv[0:1, 0, :], -1.0))
            c.vw(V, e1)
            for wi in range(1, 4):
                e1 = c.ve(V.tensor_tensor(dstb, dstb, bsv[0:1, wi, :],
                                          ALU.add if QUAT_BSIGNS[j][wi] > 0 else ALU.subtract))
                c.vw(V, e1)
        inst = V.tensor_copy(BR[0:1, 1024 * pidx:1024 * pidx + 1],
                             BR[0:1, 1024 * pidx:1024 * pidx + 1])
        masm = c.inc(sv, inst)
        c.wait(T, sv, masm)
        c.wait(T, lne, xnt_done)
        for t in range(NT):
            for o2 in range(2):
                bank = PS[1 + o2]
                if pidx > 0 or t > 0:
                    c.wait(T, qe, c.cnt[qe])
                inst = T.matmul(bank[:, 0:512], ONES[:, :],
                                BR[0:1, 1024 * pidx + 512 * o2: 1024 * pidx + 512 * (o2 + 1)],
                                start=True, stop=False)
                for k in range(KD):
                    inst = T.matmul(bank[:, 0:512],
                                    XNT[:, 2048 * k + 128 * t: 2048 * k + 128 * (t + 1)],
                                    MM[:, 1024 * k + 512 * o2: 1024 * k + 512 * (o2 + 1)],
                                    start=False, stop=(k == KD - 1))
            c.inc(qm, inst)
            bank0 = PS[1]
            bank1 = PS[2]
            if not norm:
                c.wait(S, qm)
                inst = S.activation(dst[:, 1024 * t:1024 * t + 512], bank0[:, 0:512], AF.Copy)
                inst = S.activation(dst[:, 1024 * t + 512:1024 * (t + 1)], bank1[:, 0:512], AF.Copy)
                c.inc(qe, inst, 2)
            else:
                c.wait(S, qm)
                inst = S.activation(QS[:, 0:512], bank0[:, 0:512], AF.Copy)
                inst = S.activation(QS[:, 512:1024], bank1[:, 0:512], AF.Copy)
                c.inc(qe, inst, 2)
                c.wait(S, qe, c.cnt[qe])
                inst = S.activation(SQ, QS[:, :], AF.Square)
                c.inc(ss, inst)
                c.wait(V, ss)
                sqv = SQ.rearrange("q (j o) -> q o j", j=4)
                e1 = c.ve(V.tensor_reduce(NV[:, 0:256], sqv, axis=AX.X, op=ALU.add))
                c.vw(V, e1)
                e2 = c.ve(V.tensor_scalar_add(NV[:, 0:256], NV[:, 0:256], 1e-6))
                c.vw(V, e2)
                inst = V.tensor_copy(NV[:, 256:512], NV[:, 0:256])
                nn_ = c.inc(sv, inst)
                c.wait(S, sv, nn_)
                inst = S.activation(NV[:, 0:256], NV[:, 256:512], AF.Sqrt)
                c.inc(ss, inst)
                c.wait(V, ss)
                e3 = c.ve(V.reciprocal(NV[:, 256:512], NV[:, 0:256]))
                c.vw(V, e3)
                e4 = c.ve(V.tensor_scalar_mul(NV[:, 256:512], NV[:, 256:512],
                                              scale_fold))
                c.vw(V, e4)
                for j in range(4):
                    inst = V.tensor_mul(QN[:, 256 * j:256 * (j + 1)],
                                        QS[:, 256 * j:256 * (j + 1)], NV[:, 256:512])
                qn_done = c.inc(sv, inst)
                c.wait(T, sv, qn_done)
                pt = PT[t % 2]
                if t >= 2:
                    c.wait(T, te, c.cnt[te] - 1)
                for f in range(KD):
                    inst = T.transpose(pt[:, f * 128:(f + 1) * 128],
                                       QN[:, f * 128:(f + 1) * 128], IDN[:, :])
                c.inc(tm, inst)
                c.wait(S, tm)
                slot = t % 2
                stg = QSTG[:, 1024 * slot:1024 * (slot + 1)]
                c.wait(S, p + "ty" + str(slot), tyb[slot] + 16 * (t // 2))
                inst = S.activation(stg, pt[:, :], AF.Copy)
                c.inc(te, inst)
                c.wait(Y, te)
                dd = dst.rearrange("(f q) tt -> q f tt", q=128)[:, :, t * 128:(t + 1) * 128]
                c.inc(p + "ty" + str(slot),
                      Y.dma_start(out=dd, in_=stg.rearrange("q (f tt) -> q f tt", f=KD),
                                  allow_slow_non_contiguous=True), 16)

    emit_proj("qW", "qb", QTD, True, SCALE8 * SCALE8, 0)
    emit_proj("kW", "kb", KTD, True, 1.0, 1)
    emit_proj("vW", "vb", VB, False, 1.0, 2)

    G.wait_ge(c.sem(qm), c.cnt[qm])
    G.wait_ge(c.sem(qe), c.cnt[qe])
    G.wait_ge(c.sem(tm), c.cnt[tm])
    G.wait_ge(c.sem(te), c.cnt[te])
    G.wait_ge(c.sem(p + "ty0"), c.cnt.get(p + "ty0", 0))
    G.wait_ge(c.sem(p + "ty1"), c.cnt.get(p + "ty1", 0))
    G.wait_ge(c.sem("Ve"), c.cnt["Ve"])
    G.wait_ge(c.sem(p + "lns"), c.cnt[p + "lns"])
    G.wait_ge(c.sem(ss), c.cnt[ss])
    innerA.close()
    QH = c.sbuf(p + "qh", [128, 2 * 2048], BF16)
    KH = c.sbuf(p + "kh", [128, 2 * 2048], BF16)
    SCF = pools["OUT"][:, 0:2048]

    sv2, ss2 = p + "av", p + "as"
    pm2, pe2 = p + "pm", p + "pe"
    cm2, ce2 = p + "cm", p + "ce"
    for k in (sv2, ss2, pm2, pe2, cm2, ce2):
        c.sem(k)
    ptctr = [0]

    def pt_use2(wr_fn, ev_fn):
        bank = PT[ptctr[0] % 2]
        if ptctr[0] >= 2:
            c.wait(T, pe2, ptctr[0] - 1)
        last = wr_fn(bank)
        c.inc(pm2, last)
        c.wait(S, pm2)
        ev_fn(bank)
        ptctr[0] += 1

    c.wait(V, p + "jv", cj_done)
    hl = [p + "hl0", p + "hl1"]
    c.sem(hl[0]), c.sem(hl[1])
    for h in range(16):
        hs = h % 2
        if h >= 2:
            G.wait_ge(c.sem(qm), c.cnt[qm])
        c.inc(hl[hs], G.dma_start(out=QH[0:64, 2048 * hs:2048 * (hs + 1)],
                                  in_=QTD[64 * h:64 * (h + 1), :]), 16)
        c.inc(hl[hs], G.dma_start(out=KH[0:64, 2048 * hs:2048 * (hs + 1)],
                                  in_=KTD[64 * h:64 * (h + 1), :]), 16)
        c.wait(T, hl[hs])
        hp = 0
        hf = 2048 * hs
        for qg in range(4):
            for qi in range(4):
                qb = 4 * qg + qi
                if h > 0 or qg > 0 or qi > 0:
                    c.wait(T, ss2, c.cnt[ss2])
                    c.wait(T, sv2, c.cnt[sv2])
                    c.wait(T, ce2, c.cnt[ce2])
                for n in range(4):
                    inst = T.matmul(PS[2 + n][:, 0:512],
                                    QH[hp:hp + 64, hf + 128 * qb:hf + 128 * (qb + 1)],
                                    KH[hp:hp + 64, hf + 512 * n:hf + 512 * (n + 1)],
                                    start=True, stop=True)
                c.inc(qm, inst)
                c.wait(V, qm)
                st = ST
                e1 = c.ve(V.reduce_max(st[:, 0:1], PS[2][:, 0:512], axis=AX.X))
                e2 = c.ve(V.reduce_max(st[:, 1:2], PS[3][:, 0:512], axis=AX.X))
                e3 = c.ve(V.reduce_max(st[:, 2:3], PS[4][:, 0:512], axis=AX.X))
                e4 = c.ve(V.reduce_max(st[:, 3:4], PS[5][:, 0:512], axis=AX.X))
                c.vw(V, e4)
                e5 = c.ve(V.tensor_max(st[:, 4:5], st[:, 0:1], st[:, 1:2]))
                e6 = c.ve(V.tensor_max(st[:, 5:6], st[:, 2:3], st[:, 3:4]))
                c.vw(V, e6)
                e7 = c.ve(V.tensor_max(st[:, 6:7], st[:, 4:5], st[:, 5:6]))
                c.vw(V, e7)
                inst = V.scalar_tensor_tensor(st[:, 7:8], st[:, 6:7], -1.0,
                                              CB[:, 0:1], ALU.mult, ALU.add)
                c.inc(sv2, inst)
                c.wait(S, sv2)
                for n in range(4):
                    inst = S.activation(SCF[:, 512 * n:512 * (n + 1)],
                                        PS[2 + n][:, 0:512], AF.Exp,
                                        bias=st[:, 7:8], scale=1.0,
                                        accum_out=st[:, 8 + n:9 + n])
                c.inc(ss2, inst)
                c.wait(V, ss2)
                e1 = c.ve(V.tensor_add(st[:, 12:13], st[:, 8:9], st[:, 9:10]))
                e2 = c.ve(V.tensor_add(st[:, 13:14], st[:, 10:11], st[:, 11:12]))
                c.vw(V, e2)
                e3 = c.ve(V.tensor_add(st[:, 14:15], st[:, 12:13], st[:, 13:14]))
                c.vw(V, e3)
                e4 = c.ve(V.reciprocal(st[:, 15:16], st[:, 14:15]))
                c.vw(V, e4)
                if h > 0 or qg > 0 or qi > 0:
                    c.wait(V, pm2, c.cnt[pm2])
                inst = V.tensor_scalar_mul(SCB[:, :], SCF[:, :], st[:, 15:16])
                c.inc(sv2, inst)
                c.wait(T, sv2)
                if qi == 0 and (h > 0 or qg > 0):
                    c.wait(S, cm2, c.cnt[cm2])
                for half in range(2):
                    def wr(bank, half=half):
                        for i in range(8):
                            kk = 8 * half + i
                            last = T.transpose(bank[:, 128 * i:128 * (i + 1)],
                                               SCB[:, 128 * kk:128 * (kk + 1)],
                                               IDN[:, :])
                        return last
                    def ev(bank, half=half, qi=qi):
                        dst = ATT[:, :].rearrange("q (kk cc) -> q kk cc", kk=16)[
                            :, 8 * half:8 * (half + 1), 128 * qi:128 * (qi + 1)]
                        inst = S.activation(dst,
                                            bank[:, :].rearrange("q (kk cc) -> q kk cc", kk=8),
                                            AF.Copy)
                        c.inc(pe2, inst)
                    pt_use2(wr, ev)
            c.wait(T, pe2, ptctr[0])
            if h > 0 or qg > 0:
                c.wait(T, ce2, c.cnt[ce2])
            for kk in range(16):
                inst = T.matmul(PS[2][0:64, 0:512],
                                VB[:, 1024 * kk + 64 * h: 1024 * kk + 64 * (h + 1)],
                                ATT[:, 512 * kk:512 * (kk + 1)],
                                start=(kk == 0), stop=(kk == 15))
            c.inc(cm2, inst)
            c.wait(S, cm2)
            dd = CFT[64 * (h % 2):64 * (h % 2) + 64,
                     2048 * (h // 2) + 512 * qg:2048 * (h // 2) + 512 * (qg + 1)]
            inst = S.activation(dd, PS[2][0:64, 0:512], AF.Copy)
            c.inc(ce2, inst)
    cft_done = c.cnt[ce2]

    MM2 = c.sbuf(p + "m2", [128, KD * 1024], BF16)
    WS2 = c.sbuf(p + "ws2", [128, 4 * 2 * 256], BF16)
    BS2 = c.sbuf(p + "bs2", [1, 4 * 256], BF16)
    G.wait_ge(c.sem(qm), c.cnt[qm])
    load_bf16(c, WS2[:, :].rearrange("q (wa o) -> q wa o", wa=8),
              w["oW"].rearrange("w (a q) o -> q (w a) o", q=128), sem=wl)
    load_bf16(c, BS2[:, :].rearrange("o (j a) -> o j a", j=4),
              w["ob"].rearrange("(o w) a -> o w a", o=1), sem=wl)
    wld = c.cnt[wl]
    c.wait(V, wl, wld)
    c.wait(V, qm, c.cnt[qm])
    wsv = WS2[:, :].rearrange("q (w a o) -> q w a o", w=4, a=2)
    for cc in range(4):
        for a in range(2):
            for j in range(4):
                wi, s = QUAT_SIGNS[cc][j]
                dstv = MM2[:, 1024 * (2 * cc + a) + 256 * j: 1024 * (2 * cc + a) + 256 * (j + 1)]
                if s > 0:
                    inst = V.tensor_copy(dstv, wsv[:, wi, a, :])
                else:
                    inst = V.tensor_scalar_mul(dstv, wsv[:, wi, a, :], -1.0)
    bsv = BS2[:, :].rearrange("o (j a) -> o j a", j=4)
    for j in range(4):
        dstb = BR[0:1, 3 * 1024 + 256 * j: 3 * 1024 + 256 * (j + 1)]
        e1 = c.ve(V.tensor_copy(dstb, bsv[0:1, 0, :]) if QUAT_BSIGNS[j][0] > 0
                  else V.tensor_scalar_mul(dstb, bsv[0:1, 0, :], -1.0))
        c.vw(V, e1)
        for wi in range(1, 4):
            e1 = c.ve(V.tensor_tensor(dstb, dstb, bsv[0:1, wi, :],
                                      ALU.add if QUAT_BSIGNS[j][wi] > 0 else ALU.subtract))
            c.vw(V, e1)
    inst = V.tensor_copy(BR[0:1, 3072:3073], BR[0:1, 3072:3073])
    mo_done = c.inc(sv, inst)

    om, ow = p + "om", p + "ow"
    oxp = [p + "ox0", p + "ox1"]
    oyp = [p + "oy0", p + "oy1"]
    for k in [om, ow] + oxp + oyp:
        c.sem(k)
    x_tiled = x_dram.rearrange("(t q) f -> q t f", q=128)
    y_tiled = y_dram.rearrange("(t q) f -> q t f", q=128)
    c.wait(T, ce2, cft_done)
    c.wait(T, sv, mo_done)
    c.wait(T, ss2, c.cnt[ss2])
    c.wait(T, sv2, c.cnt[sv2])
    for t in range(NT):
        slot = t % 2
        xrt = XTR[:, 1024 * slot:1024 * (slot + 1)]
        if t < 2:
            Y.wait_ge(c.sem(p + "lnv"), 3 * NT)
            Y.wait_ge(c.sem(p + "lns"), 2 * NT)
        if t >= 2:
            Y.wait_ge(c.sem(ow), 2 * (t - 1))
        c.inc(oxp[slot], Y.dma_start(out=xrt, in_=x_tiled[:, t, :]), 16)
        for o2 in range(2):
            bank = PS[4 + o2]
            if t >= 1:
                c.wait(T, ow, 2 * (t - 1) + o2 + 1)
            inst = T.matmul(bank[:, 0:512], ONES[:, :],
                            BR[0:1, 3072 + 512 * o2:3072 + 512 * (o2 + 1)],
                            start=True, stop=False)
            for k in range(KD):
                inst = T.matmul(bank[:, 0:512],
                                CFT[:, 2048 * k + 128 * t: 2048 * k + 128 * (t + 1)],
                                MM2[:, 1024 * k + 512 * o2: 1024 * k + 512 * (o2 + 1)],
                                start=False, stop=(k == KD - 1))
            c.inc(om, inst)
            c.wait(V, om)
            c.wait(V, oxp[slot])
            if t < 2:
                c.wait(V, p + "lns", 2 * NT)
                c.wait(V, ss, c.cnt[ss])
                c.wait(V, ss2, c.cnt[ss2])
            if t >= 2:
                c.wait(V, oyp[slot], 16 * (t // 2))
            ov = OUT[:, 1024 * slot + 512 * o2: 1024 * slot + 512 * (o2 + 1)]
            inst = V.scalar_tensor_tensor(ov, bank[:, 0:512], 1.0,
                                          xrt[:, 512 * o2: 512 * (o2 + 1)],
                                          ALU.mult, ALU.add)
            c.inc(ow, inst)
        c.wait(Y, ow)
        c.inc(oyp[slot], Y.dma_start(out=y_tiled[:, t, :],
                                     in_=OUT[:, 1024 * slot: 1024 * (slot + 1)]), 16)
    c.release("XNT", qm, c.cnt[qm])
    c.release("ST", ss2, c.cnt[ss2])
    c.release("PS0", p + "jv", cj_done)
    c.release("PS1", qe, c.cnt[qe])
    c.release("PS2", ce2, c.cnt[ce2])
    c.release("PS3", ss2, c.cnt[ss2])
    c.release("PS4", ow, c.cnt[ow])
    c.release("PS5", ow, c.cnt[ow])
    c.release("PT0", pe2, c.cnt[pe2])
    c.release("PT1", pe2, c.cnt[pe2])
    c.release("XTR", ow, c.cnt[ow])
    c.release("OUT", oyp[1], c.cnt[oyp[1]])
    c.wait(Y, oyp[0])
    c.wait(Y, oyp[1])



def emit_l0_attn(c, x_dram, y_dram, w, pools, p):
    V, S, T, G, Y = c.V, c.S, c.T, c.G, c.Y
    XNT = pools["XNT"]
    PS = pools["PS"]
    PT = pools["PT"]
    ST = pools["ST"]
    IDN = pools["IDN"]
    ONES = pools["ONES"]
    XTR = pools["XTR"]
    OUT = pools["OUT"]
    SQ = pools["SQ"]

    XB = c.sbuf(p + "xb", [128, NT * 1024], BF16)
    VB = c.sbuf(p + "vb", [128, NT * 1024], BF16)
    CFT = c.sbuf(p + "cft", [128, KD * 2048], BF16)
    WQS = c.sbuf(p + "wqs", [128, KD * 1024], BF16)
    QSTG = c.sbuf(p + "qstg", [128, 2 * 1024], BF16)
    SCB = c.sbuf(p + "scb", [128, 256], BF16)
    ATT = c.sbuf(p + "att", [128, 2 * 256], BF16)
    ONESC = c.sbuf(p + "onc", [128, 1], F32)
    SROW = c.sbuf(p + "srow", [1, 1024], F32)
    QTD = c.nc.dram_tensor(p + "qtd", [1024, 2048], BF16).ap()
    KTD = c.nc.dram_tensor(p + "ktd", [1024, 2048], BF16).ap()
    chD = c.nc.dram_tensor(p + "chd", [T_TOK, D], F32).ap()
    sumD = c.nc.dram_tensor(p + "sumd", [8, 1024], F32).ap()
    gbD = c.nc.dram_tensor(p + "gbd", [8, 1024], F32).ap()

    wl = p + "wl"
    inst = V.memset(ONESC[:, :], 1.0)
    onc_done = c.inc(p + "onc_", inst)

    emit_ln_transpose(c, x_dram, XNT, pools, p + "ln", out_tm=XB)
    lne = p + "lne"
    xnt_done = c.cnt[lne]

    sv, ss = p + "qv", p + "qs"
    qm, qe = p + "qm", p + "qe"
    tm, te = p + "tm", p + "te"
    for k in (sv, ss, qm, qe, tm, te):
        c.sem(k)
    c.acquire(T, "PS1"), c.acquire(T, "PS2")
    c.acquire(T, "PT0"), c.acquire(T, "PT1")
    c.wait(T, lne, xnt_done)

    def emit_proj(nameW, dstD, dst_tm, scale, pidx):
        if pidx > 0:
            G.wait_ge(c.sem(qm), c.cnt[qm])
        load_bf16(c, WQS[:, :].rearrange("q (k i) -> q k i", k=KD),
                  w[nameW].rearrange("(k q) i -> q k i", q=128), sem=wl)
        wld = c.cnt[wl]
        tyb = [c.cnt.get(p + "ty0", 0), c.cnt.get(p + "ty1", 0)]
        c.sem(p + "ty0"), c.sem(p + "ty1")
        c.wait(T, wl, wld)
        for t in range(NT):
            for o2 in range(2):
                bank = PS[1 + o2]
                if pidx > 0 or t > 0:
                    c.wait(T, qe, c.cnt[qe])
                for k in range(KD):
                    inst = T.matmul(bank[:, 0:512],
                                    XNT[:, 2048 * k + 128 * t: 2048 * k + 128 * (t + 1)],
                                    WQS[:, 1024 * k + 512 * o2: 1024 * k + 512 * (o2 + 1)],
                                    start=(k == 0), stop=(k == KD - 1))
            c.inc(qm, inst)
            c.wait(S, qm)
            if dst_tm is not None:
                inst = S.activation(dst_tm[:, 1024 * t:1024 * t + 512], PS[1][:, 0:512], AF.Copy)
                inst = S.activation(dst_tm[:, 1024 * t + 512:1024 * (t + 1)], PS[2][:, 0:512], AF.Copy)
                c.inc(qe, inst, 2)
            else:
                slot = t % 2
                stg0 = QSTG[:, 1024 * slot:1024 * slot + 1024]
                c.wait(S, p + "ty" + str(slot), tyb[slot] + 16 * (t // 2))
                qn = pools["XNB"][:, 0:1024]
                c.wait(S, tm, c.cnt[tm])
                inst = S.activation(qn[:, 0:512], PS[1][:, 0:512], AF.Copy, scale=scale)
                inst = S.activation(qn[:, 512:1024], PS[2][:, 0:512], AF.Copy, scale=scale)
                c.inc(qe, inst, 2)
                c.wait(T, qe, c.cnt[qe])
                pt = PT[t % 2]
                if pidx > 0 or t >= 2:
                    c.wait(T, te, c.cnt[te] - 1)
                for f in range(KD):
                    inst = T.transpose(pt[:, f * 128:(f + 1) * 128],
                                       qn[:, f * 128:(f + 1) * 128], IDN[:, :])
                c.inc(tm, inst)
                c.wait(S, tm)
                inst = S.activation(stg0, pt[:, :], AF.Copy)
                c.inc(te, inst)
                c.wait(Y, te)
                dd = dstD.rearrange("(f q) tt -> q f tt", q=128)[:, :, t * 128:(t + 1) * 128]
                c.inc(p + "ty" + str(slot),
                      Y.dma_start(out=dd, in_=stg0.rearrange("q (f tt) -> q f tt", f=KD),
                                  allow_slow_non_contiguous=True), 16)

    c.wait(S, p + "lns", 2 * NT)
    emit_proj("Wq_l", QTD, None, 0.125, 0)
    emit_proj("Wk_l", KTD, None, 1.0, 1)
    emit_proj("Wv_l", None, VB, 1.0, 2)

    QH = c.sbuf(p + "qh", [128, 2 * 2048], BF16)
    KH = c.sbuf(p + "kh", [128, 2 * 2048], BF16)
    SCF = OUT[:, 0:2048]
    sv2, ss2 = p + "av", p + "as"
    pm2, pe2 = p + "pm", p + "pe"
    cm2, ce2 = p + "cm", p + "ce"
    hl = [p + "hl0", p + "hl1"]
    for k in [sv2, ss2, pm2, pe2, cm2, ce2] + hl:
        c.sem(k)
    G.wait_ge(c.sem(qm), c.cnt[qm])
    G.wait_ge(c.sem("Ve"), c.cnt["Ve"])
    G.wait_ge(c.sem(p + "ty0"), c.cnt.get(p + "ty0", 0))
    G.wait_ge(c.sem(p + "ty1"), c.cnt.get(p + "ty1", 0))
    pt_uses = [0]

    def pt_use2(wr_fn, ev_fn):
        bank = PT[pt_uses[0] % 2]
        if pt_uses[0] >= 2:
            c.wait(T, pe2, pt_uses[0] - 1)
        elif True:
            c.wait(T, te, c.cnt[te])
        last = wr_fn(bank)
        c.inc(pm2, last)
        c.wait(S, pm2)
        ev_fn(bank)
        pt_uses[0] += 1

    for h in range(16):
        hs = h % 2
        if h >= 2:
            G.wait_ge(c.sem(qm), c.cnt[qm])
        c.inc(hl[hs], G.dma_start(out=QH[0:64, 2048 * hs:2048 * (hs + 1)],
                                  in_=QTD[64 * h:64 * (h + 1), :]), 16)
        c.inc(hl[hs], G.dma_start(out=KH[0:64, 2048 * hs:2048 * (hs + 1)],
                                  in_=KTD[64 * h:64 * (h + 1), :]), 16)
        c.wait(T, hl[hs])
        for ch in range(8):
            for qt in range(2):
                if h > 0 or ch > 0 or qt > 0:
                    c.wait(T, ss2, c.cnt[ss2])
                    c.wait(T, sv2, c.cnt[sv2])
                    c.wait(T, ce2, c.cnt[ce2])
                inst = T.matmul(PS[3][:, 0:256],
                                QH[0:64, 2048 * hs + 256 * ch + 128 * qt:
                                   2048 * hs + 256 * ch + 128 * (qt + 1)],
                                KH[0:64, 2048 * hs + 256 * ch:2048 * hs + 256 * (ch + 1)],
                                start=True, stop=True)
                c.inc(qm, inst)
                c.wait(V, qm)
                st = ST
                e1 = c.ve(V.reduce_max(st[:, 0:1], PS[3][:, 0:256], axis=AX.X,
                                       negate=True))
                c.vw(V, e1)
                inst = V.tensor_copy(st[:, 1:2], st[:, 0:1])
                c.inc(sv2, inst)
                c.wait(S, sv2)
                inst = S.activation(SCF[:, 0:256], PS[3][:, 0:256], AF.Exp,
                                    bias=st[:, 1:2], scale=1.0,
                                    accum_out=st[:, 2:3])
                c.inc(ss2, inst)
                c.wait(V, ss2)
                e2 = c.ve(V.reciprocal(st[:, 3:4], st[:, 2:3]))
                c.vw(V, e2)
                if h > 0 or ch > 0 or qt > 0:
                    c.wait(V, pm2, c.cnt[pm2])
                inst = V.tensor_scalar_mul(SCB[:, 0:256], SCF[:, 0:256], st[:, 3:4])
                c.inc(sv2, inst)
                c.wait(T, sv2)
                if qt == 0 and (h > 0 or ch > 0):
                    c.wait(S, cm2, c.cnt[cm2])
                def wr(bank, qt=qt):
                    for i in range(2):
                        last = T.transpose(bank[:, 256 * qt + 128 * i:256 * qt + 128 * (i + 1)],
                                           SCB[:, 128 * i:128 * (i + 1)], IDN[:, :])
                    return last
                def ev(bank, qt=qt):
                    dst = ATT[:, :].rearrange("q (kk cc) -> q kk cc", kk=2)[
                        :, :, 128 * qt:128 * (qt + 1)]
                    inst = S.activation(dst,
                                        bank[:, 256 * qt:256 * (qt + 1)].rearrange(
                                            "q (kk cc) -> q kk cc", kk=2),
                                        AF.Copy)
                    c.inc(pe2, inst)
                pt_use2(wr, ev)
            c.wait(T, pe2, pt_uses[0])
            if h > 0 or ch > 0:
                c.wait(T, ce2, c.cnt[ce2])
            for kk in range(2):
                inst = T.matmul(PS[4][0:64, 0:256],
                                VB[:, 1024 * (2 * ch + kk) + 64 * h:
                                   1024 * (2 * ch + kk) + 64 * (h + 1)],
                                ATT[:, 256 * kk:256 * (kk + 1)],
                                start=(kk == 0), stop=(kk == 1))
            c.inc(cm2, inst)
            c.wait(S, cm2)
            dd = CFT[64 * (h % 2):64 * (h % 2) + 64,
                     2048 * (h // 2) + 256 * ch:2048 * (h // 2) + 256 * (ch + 1)]
            inst = S.activation(dd, PS[4][0:64, 0:256], AF.Copy)
            c.inc(ce2, inst)
    cft_done = c.cnt[ce2]

    om, ow = p + "om", p + "ov"
    oyp_ = [p + "oy0", p + "oy1"]
    mm2, me2 = p + "mm2", p + "me2"
    sy = p + "sy"
    for k in [om, ow, mm2, me2, sy] + oyp_:
        c.sem(k)
    G.wait_ge(c.sem(qm), c.cnt[qm])
    load_bf16(c, WQS[:, :].rearrange("q (k i) -> q k i", k=KD),
              w["Wo_l"].rearrange("(k q) i -> q k i", q=128), sem=wl)
    wld = c.cnt[wl]
    c.wait(T, wl, wld)
    c.wait(T, ce2, cft_done)
    c.wait(T, ss2, c.cnt[ss2])
    c.wait(V, p + "onc_", onc_done)
    ch_tiled = chD.rearrange("(t q) f -> q t f", q=128)
    for t in range(NT):
        slot = t % 2
        ho = OUT[:, 1024 * slot:1024 * (slot + 1)]
        for o2 in range(2):
            bank = PS[1 + o2]
            if t >= 1:
                c.wait(T, ow, 2 * (t - 1) + o2 + 1)
            for k in range(KD):
                inst = T.matmul(bank[:, 0:512],
                                CFT[:, 2048 * k + 128 * t: 2048 * k + 128 * (t + 1)],
                                WQS[:, 1024 * k + 512 * o2: 1024 * k + 512 * (o2 + 1)],
                                start=(k == 0), stop=(k == KD - 1))
            c.inc(om, inst)
            c.wait(V, om)
            if t == 0 and o2 == 0:
                c.wait(V, ss2, c.cnt[ss2])
            if t == 0 and o2 == 0:
                c.wait(V, qe, c.cnt[qe])
            if t >= 2 and o2 == 0:
                c.wait(V, oyp_[slot], 16 * (t // 2))
            inst = V.scalar_tensor_tensor(ho[:, 512 * o2:512 * (o2 + 1)],
                                          bank[:, 0:512], 1.0,
                                          XB[:, 1024 * t + 512 * o2:1024 * t + 512 * (o2 + 1)],
                                          ALU.mult, ALU.add)
            c.inc(ow, inst)
        c.wait(Y, ow)
        c.inc(oyp_[slot], Y.dma_start(out=ch_tiled[:, t, :], in_=ho), 16)
        c.wait(T, ow, c.cnt[ow])
        if t % 2 == 0:
            c.wait(T, me2, c.cnt[me2])
        for half, bank3 in ((0, PS[3]), (1, PS[5])):
            inst = T.matmul(bank3[0:1, 0:512],
                            ONESC[:, :], ho[:, 512 * half:512 * (half + 1)],
                            start=(t % 2 == 0), stop=(t % 2 == 1))
        c.inc(mm2, inst)
        if t % 2 == 1:
            chnk = t // 2
            c.wait(S, mm2)
            c.wait(S, sy, c.cnt[sy])
            inst = S.activation(SROW[0:1, 0:512], PS[3][0:1, 0:512], AF.Copy,
                                scale=1.0 / 256.0)
            inst = S.activation(SROW[0:1, 512:1024], PS[5][0:1, 0:512], AF.Copy,
                                scale=1.0 / 256.0)
            c.inc(me2, inst)
            c.wait(Y, me2)
            c.inc(sy, Y.dma_start(out=sumD[chnk:chnk + 1, :], in_=SROW[0:1, :]), 16)
            c.wait(Y, sy)
    oy_done = (c.cnt[oyp_[0]], c.cnt[oyp_[1]])

    SUMM = c.sbuf(p + "summ", [8, 1024], F32)
    SMB = c.sbuf(p + "smb", [8, 1024], BF16)
    SUMT = c.sbuf(p + "sumt", [128, 64], BF16)
    QG = c.sbuf(p + "qg", [8, 3 * 1024], BF16)
    QGT = c.sbuf(p + "qgt", [128, 2 * 64], BF16)
    CTG = c.sbuf(p + "ctg", [8, 1024], BF16)
    gm3, ge3, gv3, gs3, gy3 = (p + "3" + k for k in "mevsy")
    for k in (gm3, ge3, gv3, gs3, gy3):
        c.sem(k)
    G.wait_ge(c.sem(sy), c.cnt[sy])
    c.inc(p + "sl", G.dma_start(out=SUMM[:, :], in_=sumD[:, :]), 16)
    c.wait(V, p + "sl")
    inst = V.tensor_copy(SMB[:, :], SUMM[:, :])
    smb_done = c.inc(gv3, inst)
    c.wait(T, gv3, smb_done)
    c.wait(T, me2, c.cnt[me2])
    def tr_small(dst_sb, src_sb, ncols, dtype_bank):
        bank = PT[pt_uses[0] % 2]
        if pt_uses[0] >= 2:
            c.wait(T, pe2, pt_uses[0] - 1)
        for f in range(ncols):
            last = T.transpose(bank[:, 8 * f:8 * (f + 1)],
                               src_sb[0:8, 128 * f:128 * (f + 1)], IDN[0:8, 0:8])
        c.inc(pm2, last)
        c.wait(S, pm2)
        inst = S.activation(dst_sb[:, 0:8 * ncols], bank[:, 0:8 * ncols], AF.Copy)
        c.inc(pe2, inst)
        pt_uses[0] += 1

    tr_small(SUMT, SMB, 8, None)
    for gi, (nameW, scale) in enumerate((("Wq_g", 0.125), ("Wk_g", 1.0), ("Wv_g", 1.0))):
        G.wait_ge(c.sem(gm3), c.cnt[gm3])
        load_bf16(c, WQS[:, :].rearrange("q (k i) -> q k i", k=KD),
                  w[nameW].rearrange("(k q) i -> q k i", q=128), sem=wl)
        wld = c.cnt[wl]
        c.wait(T, wl, wld)
        c.wait(T, pe2, pt_uses[0])
        for o2 in range(2):
            bank = PS[1 + o2]
            if gi > 0:
                c.wait(T, ge3, c.cnt[ge3])
            for k in range(KD):
                inst = T.matmul(bank[0:8, 0:512],
                                SUMT[:, 8 * k:8 * (k + 1)],
                                WQS[:, 1024 * k + 512 * o2: 1024 * k + 512 * (o2 + 1)],
                                start=(k == 0), stop=(k == KD - 1))
            c.inc(gm3, inst)
            c.wait(S, gm3)
            inst = S.activation(QG[0:8, 1024 * gi + 512 * o2:1024 * gi + 512 * (o2 + 1)],
                                bank[0:8, 0:512], AF.Copy, scale=scale)
            c.inc(ge3, inst)
    c.wait(T, ge3, c.cnt[ge3])
    for h in range(16):
        f = h // 2
        tr_small(QGT[:, 0:64][:, 0:64].rearrange("q a -> q a") if False else QGT[:, 0:8],
                 QG[0:8, 0:1024][0:8, 128 * f:128 * (f + 1)], 1, None)
        tr_small(QGT[:, 64:72], QG[0:8, 1024 + 128 * f:1024 + 128 * (f + 1)], 1, None)
        hp = 64 * (h % 2)
        c.wait(T, pe2, pt_uses[0])
        inst = T.matmul(PS[3][0:8, 0:8], QGT[hp:hp + 64, 0:8], QGT[hp:hp + 64, 64:72],
                        start=True, stop=True)
        c.inc(gm3, inst)
        c.wait(V, gm3)
        st = ST
        e1 = c.ve(V.reduce_max(st[0:8, 4:5], PS[3][0:8, 0:8], axis=AX.X, negate=True))
        c.vw(V, e1)
        inst = V.tensor_copy(st[0:8, 5:6], st[0:8, 4:5])
        c.inc(gv3, inst)
        c.wait(S, gv3)
        inst = S.activation(SMB[0:8, 0:8], PS[3][0:8, 0:8], AF.Exp,
                            bias=st[0:8, 5:6], scale=1.0, accum_out=st[0:8, 6:7])
        c.inc(gs3, inst)
        c.wait(V, gs3)
        e2 = c.ve(V.reciprocal(st[0:8, 7:8], st[0:8, 6:7]))
        c.vw(V, e2)
        inst = V.tensor_scalar_mul(SMB[0:8, 0:8], SMB[0:8, 0:8], st[0:8, 7:8])
        c.inc(gv3, inst)
        c.wait(T, gv3)
        bank = PT[pt_uses[0] % 2]
        if pt_uses[0] >= 2:
            c.wait(T, pe2, pt_uses[0] - 1)
        last = T.transpose(bank[0:8, 0:8], SMB[0:8, 0:8], IDN[0:8, 0:8])
        c.inc(pm2, last)
        c.wait(S, pm2)
        inst = S.activation(SMB[0:8, 8:16], bank[0:8, 0:8], AF.Copy)
        c.inc(pe2, inst)
        pt_uses[0] += 1
        c.wait(T, pe2, pt_uses[0])
        c.wait(T, gs3, c.cnt[gs3])
        inst = T.matmul(PS[4][0:8, 0:64], SMB[0:8, 8:16],
                        QG[0:8, 2048 + 64 * h:2048 + 64 * (h + 1)],
                        start=True, stop=True)
        c.inc(gm3, inst)
        c.wait(S, gm3)
        inst = S.activation(CTG[0:8, 64 * h:64 * (h + 1)], PS[4][0:8, 0:64], AF.Copy)
        c.inc(ge3, inst)
    c.wait(T, ge3, c.cnt[ge3])
    G.wait_ge(c.sem(gm3), c.cnt[gm3])
    load_bf16(c, WQS[:, :].rearrange("q (k i) -> q k i", k=KD),
              w["Wo_g"].rearrange("(k q) i -> q k i", q=128), sem=wl)
    wld = c.cnt[wl]
    tr_small(SUMT, CTG, 8, None)
    c.wait(T, wl, wld)
    for o2 in range(2):
        bank = PS[1 + o2]
        c.wait(T, pe2, pt_uses[0])
        for k in range(KD):
            inst = T.matmul(bank[0:8, 0:512],
                            SUMT[:, 8 * k:8 * (k + 1)],
                            WQS[:, 1024 * k + 512 * o2: 1024 * k + 512 * (o2 + 1)],
                            start=(k == 0), stop=(k == KD - 1))
        c.inc(gm3, inst)
        c.wait(V, gm3)
        inst = V.scalar_tensor_tensor(SUMM[0:8, 512 * o2:512 * (o2 + 1)],
                                      bank[0:8, 0:512], 1.0,
                                      SUMM[0:8, 512 * o2:512 * (o2 + 1)],
                                      ALU.mult, ALU.add)
        c.inc(gv3, inst)
    c.wait(Y, gv3)
    c.inc(gy3, Y.dma_start(out=gbD[:, :], in_=SUMM[:, :]), 16)

    GBC = c.sbuf(p + "gbc", [128, 1024], F32)
    fm, fv = p + "fm", p + "fv"
    fxp = [p + "fx0", p + "fx1"]
    fhp = [p + "fh0", p + "fh1"]
    fyp = [p + "fy0", p + "fy1"]
    gr = p + "gr"
    for k in [fm, fv, gr] + fxp + fhp + fyp:
        c.sem(k)
    x_tiled = x_dram.rearrange("(t q) f -> q t f", q=128)
    y_tiled = y_dram.rearrange("(t q) f -> q t f", q=128)
    for chn in range(8):
        G.wait_ge(c.sem(gy3), c.cnt[gy3])
        if chn > 0:
            G.wait_ge(c.sem(fm), c.cnt[fm])
        c.inc(gr, G.dma_start(out=SROW[0:1, :], in_=gbD[chn:chn + 1, :]), 16)
        c.wait(T, gr)
        if chn > 0:
            c.wait(T, fv, c.cnt[fv])
        for q2 in range(2):
            inst = T.matmul(PS[4 + q2][:, 0:512], pools["ONESF"][:, :],
                            SROW[0:1, 512 * q2:512 * (q2 + 1)], start=True, stop=True)
        c.inc(fm, inst)
        c.wait(V, fm)
        e0 = c.ve(V.tensor_copy(GBC[:, 0:512], PS[4][:, 0:512]))
        inst = V.tensor_copy(GBC[:, 512:1024], PS[5][:, 0:512])
        gbc_done = c.inc(fv, inst)
        for tt in range(2):
            t = 2 * chn + tt
            slot = t % 2
            xrt = XTR[:, 1024 * slot:1024 * (slot + 1)]
            if t < 2:
                Y.wait_ge(c.sem(ow), c.cnt[ow])
            if t >= 2:
                Y.wait_ge(c.sem(fv), c.cnt[fv] - 1)
            c.inc(fxp[slot], Y.dma_start(out=xrt, in_=x_tiled[:, t, :]), 16)
            ho = OUT[:, 1024 * slot:1024 * (slot + 1)]
            if t < 2:
                Y.wait_ge(c.sem(oyp_[0]), oy_done[0])
                Y.wait_ge(c.sem(oyp_[1]), oy_done[1])
            else:
                Y.wait_ge(c.sem(fyp[slot]), 16 * (t // 2))
            c.inc(fhp[slot], Y.dma_start(out=ho, in_=ch_tiled[:, t, :]), 16)
            c.wait(V, fxp[slot])
            c.wait(V, fhp[slot])
            c.vw(V, e0)
            c.wait(V, fv, gbc_done)
            e1 = c.ve(V.tensor_add(ho, ho, GBC[:, :]))
            c.vw(V, e1)
            inst = V.tensor_add(ho, ho, xrt)
            c.inc(fv, inst)
            c.wait(Y, fv)
            c.inc(fyp[slot], Y.dma_start(out=y_tiled[:, t, :], in_=ho), 16)
    c.release("XNT", qm, c.cnt[qm])
    c.release("ST", gs3, c.cnt[gs3])
    for i in range(6):
        c.release("PS%d" % i, fv, c.cnt[fv])
    c.release("PT0", pe2, c.cnt[pe2])
    c.release("PT1", pe2, c.cnt[pe2])
    c.release("XTR", fv, c.cnt[fv])
    c.release("OUT", fyp[1], c.cnt[fyp[1]])
    c.wait(Y, fyp[0])
    c.wait(Y, fyp[1])



def emit_l3_ffn(c, x_dram, y_dram, w, pools, p):
    V, S, T, G, Y = c.V, c.S, c.T, c.G, c.Y
    XNT = pools["XNT"]
    PS = pools["PS"]
    PT = pools["PT"]
    ST = pools["ST"]
    IDN = pools["IDN"]
    IDNF = pools["IDNF"]
    ONES = pools["ONES"]
    ONESF = pools["ONESF"]
    XTR = pools["XTR"]
    OUT = pools["OUT"]
    SQ = pools["SQ"]
    XNB = pools["XNB"]

    WDT = c.sbuf(p + "wdt", [128, 8 * 512], BF16)
    WUT = c.sbuf(p + "wut", [128, 8 * 1024], BF16)
    HGD = c.sbuf(p + "hgd", [128, 512], F32)
    SQD3 = c.sbuf(p + "sqd", [128, 1024], F32)
    BDS = c.sbuf(p + "bd", [64, 8], F32)
    BUS = c.sbuf(p + "bu", [8, 1024], BF16)
    GWS = c.sbuf(p + "gw", [128, 8 * 8], F32)
    POOLR = c.sbuf(p + "plr", [1, 1024], F32)
    POOLT = c.sbuf(p + "plt", [128, 8], F32)
    W8B = c.sbuf(p + "w8b", [128, 8], F32)
    W8R = c.sbuf(p + "w8r", [1, 8], BF16)
    W8C = c.sbuf(p + "w8c", [8, 1], BF16)
    BUE = c.sbuf(p + "bue", [1, 1024], BF16)
    HTB = c.sbuf(p + "htb", [64, 8 * 512], BF16)

    wl = p + "wl"
    for e_ in range(8):
        load_bf16(c, WDT[:, :].rearrange("q (k a) -> q k a", k=KD)[:, :, 64 * e_:64 * (e_ + 1)],
                  w["down_wT"][e_].rearrange("(k q) a -> q k a", q=128), sem=wl)
    load_bf16(c, WUT[0:64, :].rearrange("q (e a) -> q e a", e=8, a=1024),
              w["up_wT"].rearrange("e q a -> q e a"), sem=wl)
    load_f32(c, BDS[:, :], w["down_b"].rearrange("e q -> q e"), slow=True, sem=wl)
    load_bf16(c, BUS[:, :], w["up_b"], sem=wl)
    load_f32(c, GWS[:, :].rearrange("q (k e) -> q k e", k=KD),
             w["gate_wT"].rearrange("(k q) e -> q k e", q=128), sem=wl)
    wl_done = c.cnt[wl]

    sx, sv, ss, sm, se = (p + "1" + k for k in "xvsme")
    pm_, pe_ = p + "plm", p + "ple"
    for k in (sx, sv, ss, sm, se, pm_, pe_):
        c.sem(k)
    x_tiled = x_dram.rearrange("(t q) f -> q t f", q=128)
    y_tiled = y_dram.rearrange("(t q) f -> q t f", q=128)
    c.acquire(Y, "XTR")
    c.acquire(Y, "OUT")
    c.acquire(S, "OUT")
    c.acquire(S, "XNT")
    c.acquire(V, "ST")
    c.acquire(T, "PT0"), c.acquire(T, "PT1")
    c.acquire(T, "PS4"), c.acquire(T, "PS5")
    X32 = OUT[:, 0:1024]
    XT32 = OUT[:, 1024:2048]
    inst = V.memset(pools.get("ONESC1", POOLT[:, 0:1])
                    if False else POOLT[:, 0:1], 1.0)
    onc = c.inc(p + "onc", inst)
    prev_ve = prev_ss = None
    st = ST
    for t in range(NT):
        if t >= 1:
            Y.wait_ge(c.sem("Ve"), prev_ve)
            Y.wait_ge(c.sem(ss), prev_ss)
        c.inc(sx, Y.dma_start(out=X32, in_=x_tiled[:, t, :]), 16)
        c.wait(V, sx)
        c.wait(S, sx)
        if t >= 1:
            c.wait(S, sv, 3 * t)
        e1 = c.ve(V.reduce_sum(st[:, 0:1], X32, axis=AX.X))
        c.vw(V, e1)
        e_mu = c.ve(V.tensor_scalar_mul(st[:, 1:2], st[:, 0:1], 1.0 / D))
        inst = S.activation(SQD3[:, :], X32, AF.Square, accum_out=st[:, 2:3])
        prev_ss = c.inc(ss, inst)
        c.wait(V, ss)
        e2 = c.ve(V.tensor_scalar_mul(st[:, 3:4], st[:, 2:3], 1.0 / D))
        c.vw(V, e_mu)
        e3 = c.ve(V.tensor_mul(st[:, 4:5], st[:, 1:2], st[:, 1:2]))
        c.vw(V, e3)
        e4 = c.ve(V.tensor_sub(st[:, 5:6], st[:, 3:4], st[:, 4:5]))
        c.vw(V, e4)
        inst = V.tensor_scalar_add(st[:, 5:6], st[:, 5:6], 1e-5)
        c.inc(sv, inst)
        c.wait(S, sv)
        inst = S.activation(st[:, 6:7], st[:, 5:6], AF.Sqrt)
        c.inc(ss, inst)
        c.wait(V, ss)
        e5 = c.ve(V.reciprocal(st[:, 7:8], st[:, 6:7]))
        c.vw(V, e5)
        e6 = c.ve(V.scalar_tensor_tensor(st[:, 8:9], st[:, 1:2], -1.0, st[:, 7:8],
                                         ALU.mult, ALU.mult))
        c.vw(V, e6)
        e7 = c.ve(V.tensor_scalar(XT32, X32, st[:, 7:8], st[:, 8:9],
                                  ALU.mult, ALU.add))
        prev_ve = e7
        c.vw(V, e7)
        if t >= 1:
            c.wait(V, sm, t)
        inst = V.tensor_copy(XNB[:, 0:1024], XT32)
        c.inc(sv, inst)
        c.wait(T, sv)
        c.wait(T, p + "onc", onc)
        for half, bank in ((0, PS[4]), (1, PS[5])):
            inst = T.matmul(bank[0:1, 0:512], POOLT[:, 0:1],
                            XT32[:, 512 * half:512 * (half + 1)],
                            start=(t == 0), stop=(t == NT - 1))
        c.inc(pm_, inst)
        pt = PT[t % 2]
        if t >= 2:
            c.wait(T, se, t - 1)
        for f in range(KD):
            inst = T.transpose(pt[:, f * 128:(f + 1) * 128],
                               XNB[:, f * 128:(f + 1) * 128], IDN[:, :])
        c.inc(sm, inst)
        c.wait(S, sm)
        xnt_dst = XNT[:, :].rearrange("q (f tt) -> q f tt", f=KD)[:, :, t * 128:(t + 1) * 128]
        inst = S.activation(xnt_dst, pt[:, :].rearrange("q (f tt) -> q f tt", f=KD),
                            AF.Copy)
        c.inc(se, inst)
        inst = V.tensor_copy(st[:, 9:10], st[:, 7:8])
        c.inc(sv, inst)
    xnt_done = c.cnt[se]

    c.wait(S, pm_, c.cnt[pm_])
    inst = S.activation(POOLR[0:1, 0:512], PS[4][0:1, 0:512], AF.Copy, scale=1.0 / 2048.0)
    inst = S.activation(POOLR[0:1, 512:1024], PS[5][0:1, 0:512], AF.Copy, scale=1.0 / 2048.0)
    pool_done = c.inc(pe_, inst)
    gm, ge, gv, gs = (p + "g" + k for k in "mevs")
    for k in (gm, ge, gv, gs):
        c.sem(k)
    c.wait(T, pe_, pool_done)
    c.acquire(T, "PS0")
    u = 0
    for f in range(KD):
        bank = PT[u % 2]
        inst = T.transpose(PS[0][0:128, 8 * f:8 * f + 1] if False else PS[0][:, f:f + 1],
                           POOLR[0:1, 128 * f:128 * (f + 1)], IDNF[0:1, 0:1])
        u += 1
    c.inc(gm, inst)
    c.wait(S, gm)
    inst = S.activation(POOLT[:, 0:8], PS[0][:, 0:8], AF.Copy)
    c.inc(ge, inst)
    c.wait(T, ge, c.cnt[ge])
    c.wait(T, wl, wl_done)
    for k in range(KD):
        inst = T.matmul(PS[0][0:1, 0:8], POOLT[:, k:k + 1] if False else POOLT[:, k:k + 1],
                        GWS[:, 8 * k:8 * (k + 1)],
                        start=(k == 0), stop=(k == KD - 1))
    c.inc(gm, inst)
    c.wait(V, gm)
    e1 = c.ve(V.reduce_max(st[0:1, 16:17], PS[0][0:1, 0:8], axis=AX.X, negate=True))
    e0 = c.ve(V.tensor_copy(st[0:1, 20:28], PS[0][0:1, 0:8]))
    c.vw(V, e0)
    c.vw(V, e1)
    inst = V.tensor_copy(st[0:1, 17:18], st[0:1, 16:17])
    c.inc(gv, inst)
    c.wait(S, gv)
    inst = S.activation(st[0:1, 20:28], st[0:1, 20:28], AF.Exp,
                        bias=st[0:1, 17:18], scale=1.0, accum_out=st[0:1, 18:19])
    c.inc(gs, inst)
    c.wait(V, gs)
    e2 = c.ve(V.reciprocal(st[0:1, 19:20], st[0:1, 18:19]))
    c.vw(V, e2)
    e3 = c.ve(V.tensor_scalar_mul(st[0:1, 20:28], st[0:1, 20:28], st[0:1, 19:20]))
    c.vw(V, e3)
    e4 = c.ve(V.reduce_max(st[0:1, 28:29], st[0:1, 20:28], axis=AX.X))
    c.vw(V, e4)
    e5 = c.ve(V.tensor_scalar(st[0:1, 29:37], st[0:1, 20:28], st[0:1, 28:29], None,
                              ALU.is_ge))
    c.vw(V, e5)
    e6 = c.ve(V.scalar_tensor_tensor(st[0:1, 37:45], st[0:1, 29:37], BIG_NEG,
                                     st[0:1, 20:28], ALU.mult, ALU.add))
    c.vw(V, e6)
    e7 = c.ve(V.reduce_max(st[0:1, 45:46], st[0:1, 37:45], axis=AX.X))
    c.vw(V, e7)
    e8 = c.ve(V.tensor_scalar(st[0:1, 37:45], st[0:1, 20:28], st[0:1, 45:46], None,
                              ALU.is_ge))
    e9 = c.ve(V.tensor_add(st[0:1, 46:47], st[0:1, 28:29], st[0:1, 45:46]))
    c.vw(V, e9)
    ea = c.ve(V.tensor_scalar_add(st[0:1, 46:47], st[0:1, 46:47], 1e-8))
    c.vw(V, ea)
    eb = c.ve(V.reciprocal(st[0:1, 47:48], st[0:1, 46:47]))
    c.vw(V, eb)
    ec = c.ve(V.tensor_mul(st[0:1, 48:49], st[0:1, 28:29], st[0:1, 47:48]))
    ed = c.ve(V.tensor_mul(st[0:1, 49:50], st[0:1, 45:46], st[0:1, 47:48]))
    c.vw(V, ed)
    ee = c.ve(V.tensor_sub(st[0:1, 50:51], st[0:1, 48:49], st[0:1, 49:50]))
    c.vw(V, ee)
    c.vw(V, e8)
    ef = c.ve(V.tensor_scalar_mul(st[0:1, 29:37], st[0:1, 29:37], st[0:1, 50:51]))
    c.vw(V, ef)
    inst = V.scalar_tensor_tensor(st[0:1, 29:37], st[0:1, 37:45], st[0:1, 49:50],
                                  st[0:1, 29:37], ALU.mult, ALU.add)
    w8_done = c.inc(gv, inst)
    c.wait(V, gv, w8_done)
    inst = V.tensor_copy(W8R[0:1, :], st[0:1, 29:37])
    w8b_done = c.inc(gv, inst)
    c.wait(T, gv, w8b_done)
    c.wait(T, gs, c.cnt[gs])
    inst = T.matmul(PS[0][:, 0:8], ONESF[:, :], st[0:1, 29:37], start=True, stop=True)
    c.inc(gm, inst)
    c.wait(V, gm)
    inst = V.tensor_copy(W8B[:, :], PS[0][:, 0:8])
    w8bb = c.inc(gv, inst)
    c.wait(T, se, c.cnt[se])
    inst = T.transpose(PT[0][0:8, 0:1], W8R[0:1, 0:8], IDN[0:1, 0:1])
    c.inc(gm, inst)
    c.wait(S, gm)
    inst = S.activation(W8C[0:8, 0:1], PT[0][0:8, 0:1], AF.Copy)
    c.inc(ge, inst)
    for half in range(2):
        c.wait(T, ge, c.cnt[ge])
        inst = T.matmul(PS[1][0:1, 0:512], W8C[0:8, 0:1],
                        BUS[0:8, 512 * half:512 * (half + 1)],
                        start=True, stop=True)
        c.inc(gm, inst)
        c.wait(S, gm)
        inst = S.activation(BUE[0:1, 512 * half:512 * (half + 1)], PS[1][0:1, 0:512],
                            AF.Copy)
        c.inc(ge, inst)
    bue_done = c.cnt[ge]

    hm, he, hv = p + "hm", p + "he", p + "hv"
    um, uv = p + "um", p + "uv"
    fxp = [p + "fx0", p + "fx1"]
    fyp = [p + "fy0", p + "fy1"]
    for k in [hm, he, hv, um, uv] + fxp + fyp:
        c.sem(k)
    c.wait(T, se, xnt_done)
    c.wait(T, ge, bue_done)
    c.acquire(T, "PS1"), c.acquire(T, "PS2"), c.acquire(T, "PS3")
    c.wait(V, gv, w8bb)
    for tch in range(4):
        for e in range(8):
            j = tch * 8 + e
            bank = PS[1 + (j % 2)]
            if tch > 0 and e < 2:
                c.wait(T, uv, c.cnt[uv])
            if j >= 2:
                c.wait(T, hv, c.cnt[hv] - 1)
            for k in range(KD):
                inst = T.matmul(bank[0:64, 0:512],
                                WDT[:, 512 * k + 64 * e: 512 * k + 64 * (e + 1)],
                                XNT[:, 2048 * k + 512 * tch: 2048 * k + 512 * (tch + 1)],
                                start=(k == 0), stop=(k == KD - 1))
            c.inc(hm, inst)
            c.wait(S, hm)
            c.wait(S, hv, c.cnt[hv])
            inst = S.activation(HGD[0:64, 0:512],
                                bank[0:64, 0:512], AF.Gelu,
                                bias=BDS[0:64, e:e + 1], scale=1.0)
            c.inc(he, inst)
            c.wait(V, he)
            if tch > 0 and e == 0:
                c.wait(V, um, c.cnt[um])
            inst = V.tensor_scalar_mul(HTB[0:64, 512 * e:512 * (e + 1)],
                                       HGD[0:64, 0:512], W8B[0:64, e:e + 1])
            c.inc(hv, inst)
        for tt in range(4):
            t = 4 * tch + tt
            slot = t % 2
            xrt = XTR[:, 1024 * slot:1024 * (slot + 1)]
            if t >= 2:
                Y.wait_ge(c.sem(uv), 2 * (t - 1))
            c.inc(fxp[slot], Y.dma_start(out=xrt, in_=x_tiled[:, t, :]), 16)
            for oh in range(2):
                bank = PS[3] if oh else PS[2]
                if t >= 1:
                    c.wait(T, uv, 2 * (t - 1) + oh + 1)
                c.wait(T, hv, c.cnt[hv])
                inst = T.matmul(bank[:, 0:512], ONES[:, :],
                                BUE[0:1, 512 * oh:512 * (oh + 1)],
                                start=True, stop=False)
                for e in range(8):
                    inst = T.matmul(bank[:, 0:512],
                                    HTB[0:64, 512 * e + 128 * tt: 512 * e + 128 * (tt + 1)],
                                    WUT[0:64, 1024 * e + 512 * oh: 1024 * e + 512 * (oh + 1)],
                                    start=False, stop=(e == 7))
                c.inc(um, inst)
                c.wait(V, um)
                c.wait(V, fxp[slot])
                if t < 2:
                    c.wait(V, ss, c.cnt[ss])
                    c.wait(V, he, c.cnt[he])
                if t >= 2:
                    c.wait(V, fyp[slot], 16 * (t // 2))
                ov = OUT[:, 1024 * slot + 512 * oh: 1024 * slot + 512 * (oh + 1)]
                inst = V.scalar_tensor_tensor(ov, bank[:, 0:512], 0.5,
                                              xrt[:, 512 * oh: 512 * (oh + 1)],
                                              ALU.mult, ALU.add)
                c.inc(uv, inst)
            c.wait(Y, uv)
            c.inc(fyp[slot], Y.dma_start(out=y_tiled[:, t, :],
                                         in_=OUT[:, 1024 * slot: 1024 * (slot + 1)]), 16)
    c.release("XNT", hm, c.cnt[hm])
    c.release("ST", gs, c.cnt[gs])
    c.release("PS0", gm, c.cnt[gm])
    c.release("PS1", ge, c.cnt[ge])
    c.release("PS2", uv, c.cnt[uv])
    c.release("PS3", uv, c.cnt[uv])
    c.release("PS4", pe_, c.cnt[pe_])
    c.release("PS5", pe_, c.cnt[pe_])
    c.release("PT0", se, xnt_done)
    c.release("PT1", se, xnt_done)
    c.release("XTR", uv, c.cnt[uv])
    c.release("OUT", fyp[1], c.cnt[fyp[1]])
    c.wait(Y, fyp[0])
    c.wait(Y, fyp[1])



A_SLICES = [(128 * i, 128) for i in range(21)] + [(2688, 42)]


def emit_l0_glu(c, w, pools, lp, li, ixD, load_shift_tiles, bcast_row_ps,
                SXB, GBB, ROWS, SQD, bar, phase_barrier):
    V, S, T, G, Y = c.V, c.S, c.T, c.G, c.Y
    XNT = pools["XNT"]
    PS = pools["PS"]
    PT = pools["PT"]
    ST = pools["ST"]
    IDN = pools["IDN"]
    ONES = pools["ONES"]
    XTR = pools["XTR"]
    OUT = pools["OUT"]
    XNB = pools["XNB"]
    gp = lp[:3] + "g"
    ix_t = ixD.rearrange("(t q) f -> q t f", q=128)
    st = ST

    gm, ge, gv, gs, tm, te = (gp + k for k in "mevs") if False else (
        gp + "m", gp + "e", gp + "v", gp + "s", gp + "tm", gp + "te")
    sxl = [gp + "sx0", gp + "sx1"]
    wlg = gp + "wl"
    for k in [gm, ge, gv, gs, tm, te, wlg] + sxl:
        c.sem(k)

    stG = ExitStack()
    def gsbuf(n_, sh, dt_):
        return stG.enter_context(c.nc.sbuf_tensor(n_, sh, dt_))
    W2S = gsbuf(lp + "gw2", [128, 22 * 1024], BF16)
    W1A = gsbuf(lp + "gw1a", [128, 2 * KD * 128], BF16)
    W1G = gsbuf(lp + "gw1g", [128, 2 * KD * 128], BF16)
    B1A = gsbuf(lp + "gb1a", [128, 22], F32)
    B1G = gsbuf(lp + "gb1g", [128, 22], F32)
    ABUF = gsbuf(lp + "gab", [128, 512], F32)
    GBUF = gsbuf(lp + "ggb", [128, 512], F32)
    ACTT = gsbuf(lp + "gact", [128, 22 * 1024], BF16)
    G, Y2 = c.G, c.Y
    G.wait_ge(c.sem(bar), c.cnt[bar])
    bv_ = c.cnt[bar]
    c.wait(c.T, bar, bv_)
    c.wait(c.V, bar, bv_)
    c.wait(c.S, bar, bv_)
    c.wait(c.Y, bar, bv_)

    load_bf16(c, ROWS[0:1, 0:1024], w["ln_g"][li].rearrange("(o a) -> o a", o=1), sem=wlg)
    load_bf16(c, ROWS[0:1, 1024:2048], w["ln_b"][li].rearrange("(o a) -> o a", o=1), sem=wlg)
    load_bf16(c, ROWS[0:1, 2048:3072], w["b2"][li].rearrange("(o a) -> o a", o=1), sem=wlg)
    load_f32(c, B1A[:, 0:21], w["b1"][li][0:2688].rearrange("(a q) -> q a", q=128),
             slow=True, sem=wlg)
    load_f32(c, B1A[0:42, 21:22], w["b1"][li][2688:2730].rearrange("(a q) -> q a", q=42),
             slow=True, sem=wlg)
    load_f32(c, B1G[:, 0:21], w["b1"][li][2730:5418].rearrange("(a q) -> q a", q=128),
             slow=True, sem=wlg)
    load_f32(c, B1G[0:42, 21:22], w["b1"][li][5418:5460].rearrange("(a q) -> q a", q=42),
             slow=True, sem=wlg)
    load_bf16(c, W2S[:, :].rearrange("q (k i) -> q k i", k=22)[:, 0:21, :],
              w["W2"][li][0:2688].rearrange("(k q) i -> q k i", q=128), sem=wlg)
    load_bf16(c, W2S[0:42, :].rearrange("q (k i) -> q k i", k=22)[:, 21, :],
              w["W2"][li][2688:2730], sem=wlg)
    wld = c.cnt[wlg]
    c.wait(T, wlg, wld)
    c.wait(S, wlg, wld)
    bcast_row_ps(GBB[:, 0:1024], ROWS[0:1, 0:1024], 1024, gm, gv, PS[0])
    bcast_row_ps(GBB[:, 1024:2048], ROWS[0:1, 1024:2048], 1024, gm, gv, PS[0])
    gbb_done = c.cnt[gv]

    tm0 = c.cnt[tm]
    prev_ve = prev_ss = None
    c.wait(V, gv, gbb_done)
    for t in range(NT):
        slot = t % 2
        if prev_ve is not None:
            Y.wait_ge(c.sem("Ve"), prev_ve)
            Y.wait_ge(c.sem(gs), prev_ss)
        load_shift_tiles(gp, dst_bf=None, dst_f32=SXB, t=t, sems=sxl[slot])
        c.wait(V, sxl[slot])
        c.wait(S, sxl[slot])
        if t == 0:
            c.vw(S, c.shift_edge)
        if t >= 1:
            c.wait(S, gv, c.cnt[gv] - 0)
        e1 = c.ve(V.reduce_sum(st[:, 0:1], SXB[:, :], axis=AX.X))
        c.vw(V, e1)
        e_mu = c.ve(V.tensor_scalar_mul(st[:, 1:2], st[:, 0:1], 1.0 / D))
        inst = S.activation(SQD[:, :], SXB[:, :], AF.Square, accum_out=st[:, 2:3])
        prev_ss = c.inc(gs, inst)
        c.wait(V, gs)
        e2 = c.ve(V.tensor_scalar_mul(st[:, 3:4], st[:, 2:3], 1.0 / D))
        c.vw(V, e_mu)
        e3 = c.ve(V.tensor_mul(st[:, 4:5], st[:, 1:2], st[:, 1:2]))
        c.vw(V, e3)
        e4 = c.ve(V.tensor_sub(st[:, 5:6], st[:, 3:4], st[:, 4:5]))
        c.vw(V, e4)
        inst = V.tensor_scalar_add(st[:, 5:6], st[:, 5:6], 1e-5)
        c.inc(gv, inst)
        c.wait(S, gv)
        inst = S.activation(st[:, 6:7], st[:, 5:6], AF.Sqrt)
        c.inc(gs, inst)
        c.wait(V, gs)
        e5 = c.ve(V.reciprocal(st[:, 7:8], st[:, 6:7]))
        c.vw(V, e5)
        e6 = c.ve(V.scalar_tensor_tensor(st[:, 8:9], st[:, 1:2], -1.0, st[:, 7:8],
                                         ALU.mult, ALU.mult))
        c.vw(V, e6)
        e7 = c.ve(V.tensor_scalar(XNB[:, 0:1024], SXB[:, :], st[:, 7:8], st[:, 8:9],
                                  ALU.mult, ALU.add))
        prev_ve = e7
        c.vw(V, e7)
        e8 = c.ve(V.tensor_mul(XNB[:, 0:1024], XNB[:, 0:1024], GBB[:, 0:1024]))
        c.vw(V, e8)
        if t >= 1:
            c.wait(V, tm, tm0 + t)
        inst = V.tensor_add(XNB[:, 0:1024], XNB[:, 0:1024], GBB[:, 1024:2048])
        c.inc(gv, inst)
        c.wait(T, gv)
        pt = PT[t % 2]
        if c.cnt[te] >= 1:
            c.wait(T, te, c.cnt[te] - 1)
        for f in range(KD):
            inst = T.transpose(pt[:, f * 128:(f + 1) * 128],
                               XNB[:, f * 128:(f + 1) * 128], IDN[:, :])
        c.inc(tm, inst)
        c.wait(S, tm)
        xnt_dst = XNT[:, :].rearrange("q (f tt) -> q f tt", f=KD)[:, :, t * 128:(t + 1) * 128]
        inst = S.activation(xnt_dst, pt[:, :].rearrange("q (f tt) -> q f tt", f=KD),
                            AF.Copy)
        c.inc(te, inst)
    xn_done = c.cnt[te]
    c.wait(T, te, xn_done)

    hm, he, hv = gp + "hm", gp + "he", gp + "hv"
    um, uv = gp + "um", gp + "uv"
    wls = [gp + "wla", gp + "wlb"]
    oxp = [gp + "oxa", gp + "oxb"]
    oyp = [gp + "oya", gp + "oyb"]
    for k in [hm, he, hv, um, uv] + wls + oxp + oyp:
        c.sem(k)
    uvb = c.cnt[uv]
    oypb = [c.cnt[oyp[0]], c.cnt[oyp[1]]]
    for tc2 in range(2):
        for sl, (off, wdt) in enumerate(A_SLICES):
            dslot = sl % 2
            if tc2 > 0 or sl >= 2:
                G.wait_ge(c.sem(hm), c.cnt[hm] - 4)
            c.inc(wls[dslot], G.dma_start(
                out=W1A[:, :].rearrange("q (s k i) -> q s k i", s=2, k=KD)[:, dslot, :, 0:wdt],
                in_=w["W1"][li].rearrange("(k q) i -> q k i", q=128)[:, :, off:off + wdt]), 16)
            c.inc(wls[dslot], G.dma_start(
                out=W1G[:, :].rearrange("q (s k i) -> q s k i", s=2, k=KD)[:, dslot, :, 0:wdt],
                in_=w["W1"][li].rearrange("(k q) i -> q k i", q=128)[:, :, 2730 + off:2730 + off + wdt]), 16)
            c.wait(T, wls[dslot])
            for tci in range(2):
                tch = 2 * tc2 + tci
                j = sl * 2 + tci
                for gg, (WW, BB) in enumerate(((W1A, B1A), (W1G, B1G))):
                    bank = PS[1 + 2 * gg + (j % 2)]
                    if tc2 > 0 or sl > 0 or tci > 0:
                        c.wait(T, he if gg else hv, c.cnt[he if gg else hv] - 1)
                    for k in range(KD):
                        inst = T.matmul(bank[0:wdt, 0:512],
                                        WW[:, :].rearrange("q (s k i) -> q s k i", s=2, k=KD)[
                                            :, dslot, k, 0:wdt],
                                        XNT[:, 2048 * k + 512 * tch: 2048 * k + 512 * (tch + 1)],
                                        start=(k == 0), stop=(k == KD - 1))
                    c.inc(hm, inst)
                    if gg == 0:
                        c.wait(V, hm)
                        c.wait(V, hv, c.cnt[hv])
                        inst = V.tensor_scalar_add(ABUF[0:wdt, 0:512], bank[0:wdt, 0:512],
                                                   BB[0:wdt, sl:sl + 1])
                        c.inc(hv, inst)
                    else:
                        c.wait(S, hm)
                        c.wait(S, hv, c.cnt[hv])
                        inst = S.activation(GBUF[0:wdt, 0:512], bank[0:wdt, 0:512],
                                            AF.Gelu, bias=BB[0:wdt, sl:sl + 1], scale=1.0)
                        c.inc(he, inst)
                c.wait(V, he)
                c.wait(V, hv, c.cnt[hv])
                inst = V.tensor_mul(ACTT[0:wdt, 1024 * sl + 512 * tci: 1024 * sl + 512 * (tci + 1)],
                                    ABUF[0:wdt, 0:512], GBUF[0:wdt, 0:512])
                c.inc(hv, inst)
        for tt in range(8):
            t = 8 * tc2 + tt
            slot = t % 2
            xrt = XTR[:, 1024 * slot:1024 * (slot + 1)]
            if t >= 2:
                Y.wait_ge(c.sem(uv), uvb + 2 * (t - 1))
            c.inc(oxp[slot], Y.dma_start(out=xrt, in_=ix_t[:, t, :]), 16)
            for oh in range(2):
                bank = PS[5]
                if t > 8 * tc2 or oh > 0:
                    c.wait(T, uv, c.cnt[uv])
                else:
                    c.wait(T, hv, c.cnt[hv])
                    c.wait(T, he, c.cnt[he])
                    if c.cnt[uv]:
                        c.wait(T, uv, c.cnt[uv])
                inst = T.matmul(bank[:, 0:512], ONES[:, :],
                                ROWS[0:1, 2048 + 512 * oh:2048 + 512 * (oh + 1)],
                                start=True, stop=False)
                for sl, (off, wdt) in enumerate(A_SLICES):
                    inst = T.matmul(bank[:, 0:512],
                                    ACTT[0:wdt, 1024 * sl + 128 * (t - 8 * tc2):
                                         1024 * sl + 128 * (t - 8 * tc2) + 128],
                                    W2S[:, :].rearrange("q (k i) -> q k i", k=22)[0:wdt, sl, 512 * oh:512 * (oh + 1)],
                                    start=False, stop=(sl == 21))
                c.inc(um, inst)
                c.wait(V, um)
                c.wait(V, oxp[slot])
                if t < 2:
                    c.wait(V, gs, c.cnt[gs])
                if t >= 2:
                    c.wait(V, oyp[slot], oypb[slot] + 16 * (t // 2))
                ov2 = OUT[:, 1024 * slot + 512 * oh: 1024 * slot + 512 * (oh + 1)]
                inst = V.scalar_tensor_tensor(ov2, bank[:, 0:512], 1.0,
                                              xrt[:, 512 * oh: 512 * (oh + 1)],
                                              ALU.mult, ALU.add)
                c.inc(uv, inst)
            c.wait(Y, uv)
            c.inc(oyp[slot], Y.dma_start(out=ix_t[:, t, :],
                                         in_=OUT[:, 1024 * slot: 1024 * (slot + 1)]), 16)
    c.wait(Y, oyp[0])
    c.wait(Y, oyp[1])
    phase_barrier([gm, ge, gv, gs, tm, te, hm, he, hv, um, uv, wlg,
                   sxl[0], sxl[1], wls[0], wls[1], oxp[0], oxp[1],
                   oyp[0], oyp[1]])
    stG.close()


def emit_l0_ffn(c, x_dram, y_dram, w, pools, p):
    V, S, T, G, Y = c.V, c.S, c.T, c.G, c.Y
    XNT = pools["XNT"]
    PS = pools["PS"]
    PT = pools["PT"]
    ST = pools["ST"]
    IDN = pools["IDN"]
    ONES = pools["ONES"]
    XTR = pools["XTR"]
    OUT = pools["OUT"]
    XNB = pools["XNB"]

    ixD = c.nc.dram_tensor(p + "ix", [T_TOK, D], F32).ap()
    QTD = c.nc.dram_tensor(p + "qtd", [1024, 2048], BF16).ap()
    KTD = c.nc.dram_tensor(p + "ktd", [1024, 2048], BF16).ap()
    ix_t = ixD.rearrange("(t q) f -> q t f", q=128)
    x_t = x_dram.rearrange("(t q) f -> q t f", q=128)
    y_t = y_dram.rearrange("(t q) f -> q t f", q=128)

    SQD = c.sbuf(p + "sqd", [128, 1024], F32)
    QSB = c.sbuf(p + "qsb", [128, 2 * 64], F32)
    NS = c.sbuf(p + "ns", [128, 192], F32)
    QS = OUT[:, 0:1024]
    QN = c.sbuf(p + "qn", [128, 1024], BF16)
    SXB = c.sbuf(p + "sxb", [128, 1024], F32)
    GBB = c.sbuf(p + "gbb", [128, 2 * 1024], F32)
    ROWS = c.sbuf(p + "rows", [1, 4 * 1024], BF16)
    WQK = c.sbuf(p + "wqk", [128, KD * 1024], BF16)
    BARS = c.sbuf(p + "bar", [1, 8], F32)
    bar = p + "barrier"
    c.sem(bar)

    def phase_barrier(sem_list):
        for s_ in sem_list:
            if s_ in c.cnt:
                c.G.wait_ge(c.sem(s_), c.cnt[s_])
        inst = G.memset(BARS[0:1, 0:1], 0.0)
        return c.inc(bar, inst)

    wl = p + "wl"
    msk_done = 0

    sx, sv, ss = (p + "0" + k for k in "xvs")
    zy = [p + "zy0", p + "zy1"]
    for k in [sx, sv, ss] + zy:
        c.sem(k)
    zyb = [c.cnt[zy[0]], c.cnt[zy[1]]]
    c.acquire(Y, "XTR")
    c.acquire(Y, "OUT")
    c.acquire(S, "OUT")
    c.acquire(V, "ST")
    st = ST
    X32 = XTR[:, 0:1024]
    prev_ve = prev_ss = None
    for t in range(NT):
        XO32 = OUT[:, 1024 * (t % 2):1024 * (t % 2) + 1024]
        if t >= 1:
            Y.wait_ge(c.sem("Ve"), prev_ve)
            Y.wait_ge(c.sem(ss), prev_ss)
        c.inc(sx, Y.dma_start(out=X32, in_=x_t[:, t, :]), 16)
        c.wait(V, sx)
        c.wait(S, sx)
        if t >= 1:
            c.wait(S, sv, 2 * t)
        e1 = c.ve(V.reduce_sum(st[:, 0:1], X32, axis=AX.X))
        c.vw(V, e1)
        e_mu = c.ve(V.tensor_scalar_mul(st[:, 1:2], st[:, 0:1], 1.0 / D))
        inst = S.activation(SQD[:, :], X32, AF.Square, accum_out=st[:, 2:3])
        prev_ss = c.inc(ss, inst)
        c.wait(V, ss)
        e2 = c.ve(V.tensor_scalar_mul(st[:, 3:4], st[:, 2:3], 1.0 / D))
        c.vw(V, e_mu)
        e3 = c.ve(V.tensor_mul(st[:, 4:5], st[:, 1:2], st[:, 1:2]))
        c.vw(V, e3)
        e4 = c.ve(V.tensor_sub(st[:, 5:6], st[:, 3:4], st[:, 4:5]))
        c.vw(V, e4)
        inst = V.tensor_scalar_add(st[:, 5:6], st[:, 5:6], 1e-5)
        c.inc(sv, inst)
        c.wait(S, sv)
        inst = S.activation(st[:, 6:7], st[:, 5:6], AF.Sqrt)
        c.inc(ss, inst)
        c.wait(V, ss)
        e5 = c.ve(V.reciprocal(st[:, 7:8], st[:, 6:7]))
        c.vw(V, e5)
        e6 = c.ve(V.scalar_tensor_tensor(st[:, 8:9], st[:, 1:2], -1.0, st[:, 7:8],
                                         ALU.mult, ALU.mult))
        c.vw(V, e6)
        if t >= 2:
            c.wait(V, zy[t % 2], zyb[t % 2] + 16 * (t // 2))
        e7 = c.ve(V.tensor_scalar(XO32, X32, st[:, 7:8], st[:, 8:9],
                                  ALU.mult, ALU.add))
        prev_ve = e7
        inst = V.tensor_copy(st[:, 9:10], st[:, 8:9])
        c.inc(sv, inst)
        c.wait(Y, "Ve", e7)
        c.inc(zy[t % 2], Y.dma_start(out=ix_t[:, t, :], in_=XO32), 16)
    z0_done = (c.cnt[zy[0]], c.cnt[zy[1]])
    c.wait(Y, zy[0])
    c.wait(Y, zy[1])

    def load_shift_tiles(pref, dst_bf=None, dst_f32=None, t=0, sems=None):
        dst = dst_bf if dst_bf is not None else dst_f32
        c.inc(sems, Y.dma_start(out=dst[:, 0:512], in_=ix_t[:, t, 0:512]), 16)
        if t == 0:
            inst = V.memset(dst[0:1, 512:1024], 0.0)
            c.shift_edge = c.ve(inst)
            c.vw(V, c.shift_edge)
            if 127 > 0:
                c.inc(sems, Y.dma_start(out=dst[1:128, 512:1024],
                                        in_=ixD[0:127, 512:1024]), 16)
        else:
            c.inc(sems, Y.dma_start(out=dst[:, 512:1024],
                                    in_=ixD[128 * t - 1:128 * (t + 1) - 1, 512:1024]), 16)

    def bcast_row_ps(dst, row_bf, ncols, semm, semv, bank):
        for off in range(0, ncols, 512):
            n = min(512, ncols - off)
            c.wait(T, semv, c.cnt[semv])
            inst = T.matmul(bank[:, 0:n], ONES[:, :], row_bf[0:1, off:off + n],
                            start=True, stop=True)
            c.inc(semm, inst)
            c.wait(V, semm)
            inst = V.tensor_copy(dst[:, off:off + n], bank[:, 0:n])
            c.inc(semv, inst)

    for li in range(6):
        lp = p + "L%d" % li
        stA = ExitStack()
        def asbuf(n_, sh, dt_):
            return stA.enter_context(c.nc.sbuf_tensor(n_, sh, dt_))
        VB = asbuf(lp + "vb", [128, NT * 1024], BF16)
        CFT = asbuf(lp + "cft", [128, KD * 2048], BF16)
        TBL = asbuf(lp + "tbl", [128, 4 * NT * 64], F32)
        MSK = asbuf(lp + "msk", [128, 4 * 512], F32)
        QSTG = asbuf(lp + "qstg", [128, 2 * 1024], BF16)
        SCB = asbuf(lp + "scb", [128, 512], BF16)
        ATT = asbuf(lp + "att", [128, 4 * 256], BF16)
        QH = asbuf(lp + "qh", [128, 2 * 2048], BF16)
        KH = asbuf(lp + "kh", [128, 2 * 2048], BF16)
        if li > 0:
            c.wait(G, bar, c.cnt[bar]) if False else G.wait_ge(c.sem(bar), c.cnt[bar])
        load_f32(c, MSK[:, :].rearrange("q (i a) -> q i a", i=4),
                 w["masks"].rearrange("(i q) a -> q i a", q=128), sem=p + "wr")
        if li > 0:
            bv_ = c.cnt[bar]
            c.wait(T, bar, bv_)
            c.wait(V, bar, bv_)
            c.wait(S, bar, bv_)
            c.wait(Y, bar, bv_)
        am, ae = p + "am", p + "ae"
        av, as_ = p + "av", p + "as"
        tm, te = p + "tm", p + "te"
        sxl = [p + "sx0", p + "sx1"]
        for k in [am, ae, av, as_, tm, te] + sxl:
            c.sem(k)
        for i4 in range(4):
            c.inc(p + "wr", G.dma_start(
                out=TBL[:, :].rearrange("q (i t a) -> q i t a", i=4, t=NT)[:, i4],
                in_=w["rope"][i4].rearrange("(t q) a -> q t a", q=128)), 16)
        c.inc(p + "wr", G.dma_start(out=ROWS[0:1, 0:128],
                                    in_=w["qkscale"][li].rearrange("(o a) -> o a", o=1)), 16)
        wrd = c.cnt[p + "wr"]
        c.wait(T, p + "wr", wrd)
        c.wait(V, p + "wr", wrd)
        if li == 0:
            c.acquire(T, "PS0"), c.acquire(T, "PS1"), c.acquire(T, "PS2")
            c.acquire(T, "PS3"), c.acquire(T, "PS4"), c.acquire(T, "PS5")
            c.acquire(T, "PT0"), c.acquire(T, "PT1")
            c.acquire(S, "XNT")
        bank = PS[0]
        inst = T.matmul(bank[:, 0:128], ONES[:, :], ROWS[0:1, 0:128],
                        start=True, stop=True)
        c.inc(am, inst)
        c.wait(V, am)
        e0 = c.ve(V.tensor_copy(QSB[:, 0:128], bank[:, 0:128]))
        c.vw(V, e0)
        inst = V.tensor_scalar_mul(QSB[:, 0:64], QSB[:, 0:64], 8.0)
        qsb_done = c.inc(av, inst)
        c.wait(V, av, qsb_done)
        qv = QSB[:, 0:128].rearrange("q (i h d) -> q i h d", i=2, h=2)
        e2 = c.ve(V.tensor_copy(NS[:, 0:64].rearrange("q (h d) -> q h d", h=2)[:, 0, :],
                                qv[:, 0, 1, :]))
        e3 = c.ve(V.tensor_copy(NS[:, 0:64].rearrange("q (h d) -> q h d", h=2)[:, 1, :],
                                qv[:, 0, 0, :]))
        e4 = c.ve(V.tensor_copy(NS[:, 64:128].rearrange("q (h d) -> q h d", h=2)[:, 0, :],
                                qv[:, 1, 1, :]))
        e5 = c.ve(V.tensor_copy(NS[:, 64:128].rearrange("q (h d) -> q h d", h=2)[:, 1, :],
                                qv[:, 1, 0, :]))
        c.vw(V, e5)
        c.vw(V, e2)
        c.vw(V, e3)
        c.vw(V, e4)
        def bc64(ap_col):
            t_ = ap_col
            return bass.AP(t_.tensor, t_.offset, [t_.ap[0], [0, NT], [1, 64]])
        tblv = TBL[:, :].rearrange("q (i t a) -> q i t a", i=4, t=NT)
        e6 = c.ve(V.tensor_tensor(tblv[:, 0, :, :], tblv[:, 0, :, :],
                                  bc64(QSB[:, 0:64]), ALU.mult))
        e7 = c.ve(V.tensor_tensor(tblv[:, 1, :, :], tblv[:, 1, :, :],
                                  bc64(NS[:, 0:64]), ALU.mult))
        e8 = c.ve(V.tensor_tensor(tblv[:, 2, :, :], tblv[:, 2, :, :],
                                  bc64(QSB[:, 64:128]), ALU.mult))
        e9 = c.ve(V.tensor_tensor(tblv[:, 3, :, :], tblv[:, 3, :, :],
                                  bc64(NS[:, 64:128]), ALU.mult))
        c.vw(V, e9)

        tm0 = c.cnt[tm]
        prev_copy = None
        for t in range(NT):
            slot = t % 2
            if prev_copy is not None:
                Y.wait_ge(c.sem("Ve"), prev_copy)
            load_shift_tiles(lp, dst_bf=None, dst_f32=SXB, t=t, sems=sxl[slot])
            c.wait(V, sxl[slot])
            if t >= 1:
                c.wait(V, tm, tm0 + t)
            inst = V.tensor_copy(QN[:, :], SXB[:, :])
            prev_copy = c.ve(inst)
            c.inc(av, inst) if False else None
            c.vw(T, prev_copy)
            pt = PT[t % 2]
            if c.cnt[te] >= 1:
                c.wait(T, te, c.cnt[te] - 1)
            for f in range(KD):
                inst = T.transpose(pt[:, f * 128:(f + 1) * 128],
                                   QN[:, f * 128:(f + 1) * 128], IDN[:, :])
            c.inc(tm, inst)
            c.wait(S, tm)
            xnt_dst = XNT[:, :].rearrange("q (f tt) -> q f tt", f=KD)[:, :, t * 128:(t + 1) * 128]
            inst = S.activation(xnt_dst, pt[:, :].rearrange("q (f tt) -> q f tt", f=KD),
                                AF.Copy)
            c.inc(te, inst)
        sxt_done = c.cnt[te]

        tyb = [c.cnt.get(p + "ty0", 0), c.cnt.get(p + "ty1", 0)]
        c.sem(p + "ty0"), c.sem(p + "ty1")
        c.wait(T, te, sxt_done)
        for pj, dstD in (("q", QTD), ("k", KTD), ("v", None)):
            G.wait_ge(c.sem(am), c.cnt[am])
            off = {"q": 0, "k": 1024, "v": 2048}[pj]
            load_bf16(c, WQK[:, :].rearrange("q (k i) -> q k i", k=KD),
                      w["Wqkv"][li].rearrange("(k q) i -> q k i", q=128)[:, :, off:off + 1024],
                      sem=p + "w" + pj)
            wld = c.cnt[p + "w" + pj]
            c.wait(T, p + "w" + pj, wld)
            tyb = [c.cnt.get(p + "ty0", 0), c.cnt.get(p + "ty1", 0)]
            for t in range(NT):
                for o2 in range(2):
                    bank = PS[1 + o2]
                    if not (pj == "q" and t == 0):
                        c.wait(T, ae, c.cnt[ae])
                    for k in range(KD):
                        inst = T.matmul(bank[:, 0:512],
                                        XNT[:, 2048 * k + 128 * t: 2048 * k + 128 * (t + 1)],
                                        WQK[:, 1024 * k + 512 * o2: 1024 * k + 512 * (o2 + 1)],
                                        start=(k == 0), stop=(k == KD - 1))
                c.inc(am, inst)
                c.wait(S, am)
                if pj == "v":
                    inst = S.activation(VB[:, 1024 * t:1024 * t + 512], PS[1][:, 0:512], AF.Copy)
                    inst = S.activation(VB[:, 1024 * t + 512:1024 * (t + 1)], PS[2][:, 0:512], AF.Copy)
                    c.inc(ae, inst, 2)
                else:
                    c.wait(S, av, c.cnt[av])
                    inst = S.activation(QS[:, 0:512], PS[1][:, 0:512], AF.Copy)
                    inst = S.activation(QS[:, 512:1024], PS[2][:, 0:512], AF.Copy)
                    c.inc(ae, inst, 2)
                    c.wait(S, ae, c.cnt[ae])
                    inst = S.activation(SQD[:, :], QS, AF.Square)
                    c.inc(as_, inst)
                    c.wait(V, as_)
                    nsq = NS[:, 128:144]
                    e1 = c.ve(V.tensor_reduce(nsq, SQD[:, :].rearrange(
                        "q (h d) -> q h d", h=16), axis=AX.X, op=ALU.add))
                    c.vw(V, e1)
                    inst = V.tensor_copy(NS[:, 144:160], nsq)
                    nn_ = c.inc(av, inst)
                    c.wait(S, av, nn_)
                    inst = S.activation(NS[:, 144:160], NS[:, 144:160], AF.Sqrt)
                    c.inc(as_, inst)
                    c.wait(V, as_)
                    e2 = c.ve(V.tensor_scalar_max(NS[:, 144:160], NS[:, 144:160], 1e-12))
                    c.vw(V, e2)
                    e3 = c.ve(V.reciprocal(NS[:, 160:176], NS[:, 144:160]))
                    c.vw(V, e3)
                    invv = NS[:, 160:176]
                    inv_b = bass.AP(invv.tensor, invv.offset,
                                    [invv.ap[0], [1, 16], [0, 64]])
                    e4 = c.ve(V.tensor_tensor(QS[:, :].rearrange("q (h d) -> q h d", h=16),
                                              QS[:, :].rearrange("q (h d) -> q h d", h=16),
                                              inv_b, ALU.mult))
                    c.vw(V, e4)
                    qv_ = QS[:, :].rearrange("q (h s d) -> q h s d", h=16, s=2)
                    rv_ = SQD[:, :].rearrange("q (h s d) -> q h s d", h=16, s=2)
                    e5 = c.ve(V.tensor_scalar_mul(rv_[:, :, 0, :], qv_[:, :, 1, :], -1.0))
                    e6 = c.ve(V.tensor_copy(rv_[:, :, 1, :], qv_[:, :, 0, :]))
                    c.vw(V, e6)
                    c.vw(V, e5)
                    i0 = 0 if pj == "q" else 2
                    t1 = tblv[:, i0, t, :]
                    t2 = tblv[:, i0 + 1, t, :]
                    t1b = bass.AP(t1.tensor, t1.offset, [t1.ap[0], [0, 16], [1, 64]])
                    t2b = bass.AP(t2.tensor, t2.offset, [t2.ap[0], [0, 16], [1, 64]])
                    e7 = c.ve(V.tensor_tensor(QS[:, :].rearrange("q (h d) -> q h d", h=16),
                                              QS[:, :].rearrange("q (h d) -> q h d", h=16),
                                              t1b, ALU.mult))
                    e8 = c.ve(V.tensor_tensor(SQD[:, :].rearrange("q (h d) -> q h d", h=16),
                                              SQD[:, :].rearrange("q (h d) -> q h d", h=16),
                                              t2b, ALU.mult))
                    c.vw(V, e8)
                    c.vw(V, e7)
                    inst = V.tensor_add(QN[:, :], QS[:, :], SQD[:, :])
                    c.inc(av, inst)
                    c.wait(T, av)
                    pt = PT[t % 2]
                    c.wait(T, te, c.cnt[te] - 1)
                    for f in range(KD):
                        inst = T.transpose(pt[:, f * 128:(f + 1) * 128],
                                           QN[:, f * 128:(f + 1) * 128], IDN[:, :])
                    c.inc(tm, inst)
                    c.wait(S, tm)
                    slot = t % 2
                    stg = QSTG[:, 1024 * slot:1024 * (slot + 1)]
                    c.wait(S, p + "ty" + str(slot), tyb[slot] + 16 * (t // 2))
                    inst = S.activation(stg, pt[:, :], AF.Copy)
                    c.inc(te, inst)
                    c.wait(Y, te)
                    dd = dstD.rearrange("(f q) tt -> q f tt", q=128)[:, :, t * 128:(t + 1) * 128]
                    c.inc(p + "ty" + str(slot),
                          Y.dma_start(out=dd, in_=stg.rearrange("q (f tt) -> q f tt", f=KD),
                                      allow_slow_non_contiguous=True), 16)

        SCF = OUT[:, 0:512]
        hl = [p + "hl0", p + "hl1"]
        pm2, pe2, cm2, ce2, av2, as2 = (p + k for k in
                                        ("pm", "pe", "cm", "ce", "v2", "s2"))
        for k in hl + [pm2, pe2, cm2, ce2, av2, as2]:
            c.sem(k)
        G.wait_ge(c.sem(p + "ty0"), c.cnt[p + "ty0"])
        G.wait_ge(c.sem(p + "ty1"), c.cnt[p + "ty1"])
        peb = c.cnt[pe2]
        pt_uses = [0]
        for h in range(16):
            hs = h % 2
            if h >= 2:
                G.wait_ge(c.sem(am), c.cnt[am])
            c.inc(hl[hs], G.dma_start(out=QH[0:64, 2048 * hs:2048 * (hs + 1)],
                                      in_=QTD[64 * h:64 * (h + 1), :]), 16)
            c.inc(hl[hs], G.dma_start(out=KH[0:64, 2048 * hs:2048 * (hs + 1)],
                                      in_=KTD[64 * h:64 * (h + 1), :]), 16)
            c.wait(T, hl[hs])
            for wn in range(8):
                kw = 512 if wn > 0 else 256
                kbase = 256 * (wn - 1) if wn > 0 else 0
                for qt in range(2):
                    if h > 0 or wn > 0 or qt > 0:
                        c.wait(T, as2, c.cnt[as2])
                        c.wait(T, av2, c.cnt[av2])
                        c.wait(T, ce2, c.cnt[ce2])
                    inst = T.matmul(PS[3][:, 0:kw],
                                    QH[0:64, 2048 * hs + 256 * wn + 128 * qt:
                                       2048 * hs + 256 * wn + 128 * (qt + 1)],
                                    KH[0:64, 2048 * hs + kbase:2048 * hs + kbase + kw],
                                    start=True, stop=True)
                    c.inc(am, inst)
                    c.wait(V, am)
                    mi = (2 + qt) if wn == 0 else qt
                    e1 = c.ve(V.tensor_tensor(SCF[:, 0:kw], PS[3][:, 0:kw],
                                              MSK[:, 512 * mi:512 * mi + kw], ALU.add))
                    c.vw(V, e1)
                    e2 = c.ve(V.reduce_max(st[:, 16:17], SCF[:, 0:kw], axis=AX.X,
                                           negate=True))
                    c.vw(V, e2)
                    inst = V.tensor_copy(st[:, 17:18], st[:, 16:17])
                    c.inc(av2, inst)
                    c.wait(S, av2)
                    inst = S.activation(SCF[:, 0:kw], SCF[:, 0:kw], AF.Exp,
                                        bias=st[:, 17:18], scale=1.0,
                                        accum_out=st[:, 18:19])
                    c.inc(as2, inst)
                    c.wait(V, as2)
                    e3 = c.ve(V.reciprocal(st[:, 19:20], st[:, 18:19]))
                    c.vw(V, e3)
                    if h > 0 or wn > 0 or qt > 0:
                        c.wait(V, pm2, c.cnt[pm2])
                    inst = V.tensor_scalar_mul(SCB[:, 0:kw], SCF[:, 0:kw], st[:, 19:20])
                    c.inc(av2, inst)
                    c.wait(T, av2)
                    if qt == 0 and (h > 0 or wn > 0):
                        c.wait(S, cm2, c.cnt[cm2])
                    bank = PT[pt_uses[0] % 2]
                    if pt_uses[0] >= 2:
                        c.wait(T, pe2, peb + pt_uses[0] - 1)
                    else:
                        c.wait(T, te, c.cnt[te])
                    nk = kw // 128
                    for i in range(nk):
                        last = T.transpose(bank[:, 128 * i:128 * (i + 1)],
                                           SCB[:, 128 * i:128 * (i + 1)], IDN[:, :])
                    c.inc(pm2, last)
                    c.wait(S, pm2)
                    dst = ATT[:, :].rearrange("q (kk cc) -> q kk cc", kk=4)[
                        :, 0:nk, 128 * qt:128 * (qt + 1)]
                    inst = S.activation(dst, bank[:, 0:128 * nk].rearrange(
                        "q (kk cc) -> q kk cc", kk=nk), AF.Copy)
                    c.inc(pe2, inst)
                    pt_uses[0] += 1
                c.wait(T, pe2, peb + pt_uses[0])
                if h > 0 or wn > 0:
                    c.wait(T, ce2, c.cnt[ce2])
                nk = kw // 128
                for kk in range(nk):
                    ktile = (2 * (wn - 1) + kk) if wn > 0 else kk
                    attv = ATT[:, :].rearrange("q (kk cc) -> q kk cc", kk=4)
                    inst = T.matmul(PS[4][0:64, 0:256],
                                    VB[:, 1024 * ktile + 64 * h: 1024 * ktile + 64 * (h + 1)],
                                    attv[:, kk, :],
                                    start=(kk == 0), stop=(kk == nk - 1))
                c.inc(cm2, inst)
                c.wait(S, cm2)
                dd = CFT[64 * (h % 2):64 * (h % 2) + 64,
                         2048 * (h // 2) + 256 * wn:2048 * (h // 2) + 256 * (wn + 1)]
                inst = S.activation(dd, PS[4][0:64, 0:256], AF.Copy)
                c.inc(ce2, inst)
        cft_done = c.cnt[ce2]

        G.wait_ge(c.sem(am), c.cnt[am])
        load_bf16(c, WQK[:, :].rearrange("q (k i) -> q k i", k=KD),
                  w["Wo"][li].rearrange("(k q) i -> q k i", q=128), sem=p + "wo")
        wld = c.cnt[p + "wo"]
        c.wait(T, p + "wo", wld)
        c.wait(T, ce2, cft_done)
        c.wait(T, as2, c.cnt[as2])
        ov_, oy_ = p + "ov", p + "oy"
        oxp = [p + "oxa", p + "oxb"]
        oyp2 = [p + "oyb0", p + "oyb1"]
        for k in [ov_, oy_] + oxp + oyp2:
            c.sem(k)
        ovb = c.cnt[ov_]
        oyb = [c.cnt[oyp2[0]], c.cnt[oyp2[1]]]
        for t in range(NT):
            slot = t % 2
            xrt = XTR[:, 1024 * slot:1024 * (slot + 1)]
            if t >= 2:
                Y.wait_ge(c.sem(ov_), ovb + 2 * (t - 1))
            c.inc(oxp[slot], Y.dma_start(out=xrt, in_=ix_t[:, t, :]), 16)
            for o2 in range(2):
                bank = PS[1 + o2]
                if t >= 1:
                    c.wait(T, ov_, ovb + 2 * (t - 1) + o2 + 1)
                for k in range(KD):
                    inst = T.matmul(bank[:, 0:512],
                                    CFT[:, 2048 * k + 128 * t: 2048 * k + 128 * (t + 1)],
                                    WQK[:, 1024 * k + 512 * o2: 1024 * k + 512 * (o2 + 1)],
                                    start=(k == 0), stop=(k == KD - 1))
                c.inc(am, inst)
                c.wait(V, am)
                c.wait(V, oxp[slot])
                if t < 2:
                    c.wait(V, as2, c.cnt[as2])
                if t >= 2:
                    c.wait(V, oyp2[slot], oyb[slot] + 16 * (t // 2))
                ov2 = OUT[:, 1024 * slot + 512 * o2: 1024 * slot + 512 * (o2 + 1)]
                inst = V.scalar_tensor_tensor(ov2, bank[:, 0:512], 1.0,
                                              xrt[:, 512 * o2: 512 * (o2 + 1)],
                                              ALU.mult, ALU.add)
                c.inc(ov_, inst)
            c.wait(Y, ov_)
            c.inc(oyp2[slot], Y.dma_start(out=ix_t[:, t, :],
                                          in_=OUT[:, 1024 * slot: 1024 * (slot + 1)]), 16)
        c.wait(Y, oyp2[0])
        c.wait(Y, oyp2[1])
        phase_barrier([am, ae, av, as_, tm, te, av2, as2, pm2, pe2, cm2, ce2,
                       ov_, hl[0], hl[1], p + "ty0", p + "ty1",
                       oxp[0], oxp[1], oyp2[0], oyp2[1], sxl[0], sxl[1],
                       p + "wr", p + "wq", p + "wk", p + "wv", p + "wo"])
        stA.close()

        emit_l0_glu(c, w, pools, lp, li, ixD, load_shift_tiles, bcast_row_ps,
                    SXB, GBB, ROWS, SQD, bar, phase_barrier)

    fm = p + "f"
    fxp = [p + "fxa", p + "fxb"]
    fzp = [p + "fza", p + "fzb"]
    fyp = [p + "fya", p + "fyb"]
    for k in [fm] + fxp + fzp + fyp:
        c.sem(k)
    fmb = c.cnt[fm]
    fyb = [c.cnt[fyp[0]], c.cnt[fyp[1]]]
    for t in range(NT):
        slot = t % 2
        xrt = XTR[:, 1024 * slot:1024 * (slot + 1)]
        zo = OUT[:, 1024 * slot:1024 * (slot + 1)]
        if t >= 2:
            Y.wait_ge(c.sem(fm), fmb + t - 1)
            Y.wait_ge(c.sem(fyp[slot]), fyb[slot] + 16 * (t // 2))
        c.inc(fxp[slot], Y.dma_start(out=xrt, in_=x_t[:, t, :]), 16)
        c.inc(fzp[slot], Y.dma_start(out=zo, in_=ix_t[:, t, :]), 16)
        c.wait(V, fxp[slot])
        c.wait(V, fzp[slot])
        inst = V.scalar_tensor_tensor(zo, zo, 0.5, xrt, ALU.mult, ALU.add)
        c.inc(fm, inst)
        c.wait(Y, fm)
        c.inc(fyp[slot], Y.dma_start(out=y_t[:, t, :], in_=zo), 16)
    c.release("XNT", am, c.cnt[am])
    c.release("ST", as_, c.cnt[as_])
    for i in range(6):
        c.release("PS%d" % i, fm, c.cnt[fm])
    c.release("PT0", pe2, c.cnt[pe2])
    c.release("PT1", pe2, c.cnt[pe2])
    c.release("XTR", fm, c.cnt[fm])
    c.release("OUT", fyp[1], c.cnt[fyp[1]])
    c.wait(Y, fyp[0])
    c.wait(Y, fyp[1])



def emit_l3_attn(c, x_dram, y_dram, w, pools, p):
    V, S, T, G, Y = c.V, c.S, c.T, c.G, c.Y
    XNT = pools["XNT"]
    PS = pools["PS"]
    PT = pools["PT"]
    ST = pools["ST"]
    IDN = pools["IDN"]
    IDNF = pools["IDNF"]
    ONES = pools["ONES"]
    ONESF = pools["ONESF"]
    XTR = pools["XTR"]
    OUT = pools["OUT"]
    XNB = pools["XNB"]
    st = ST
    DI = 2048
    CH = 32
    NCH = T_TOK // CH

    xnD = c.nc.dram_tensor(p + "xn", [T_TOK, D], F32).ap()
    zTD = c.nc.dram_tensor(p + "zt", [DI, T_TOK], F32).ap()
    xgTD = c.nc.dram_tensor(p + "xg", [DI, T_TOK], F32).ap()
    xcD = c.nc.dram_tensor(p + "xc", [DI, T_TOK], F32).ap()
    knD = c.nc.dram_tensor(p + "kn", [T_TOK, DI], F32).ap()
    qD = c.nc.dram_tensor(p + "q", [T_TOK, DI], F32).ap()
    vD = c.nc.dram_tensor(p + "v", [T_TOK, 32], F32).ap()
    g4D = c.nc.dram_tensor(p + "g4", [4, 16, T_TOK], F32).ap()
    retD = c.nc.dram_tensor(p + "ret", [T_TOK, 32], F32).ap()

    x_t = x_dram.rearrange("(t q) f -> q t f", q=128)
    y_t = y_dram.rearrange("(t q) f -> q t f", q=128)
    xn_t = xnD.rearrange("(t q) f -> q t f", q=128)
    kn_t = knD.rearrange("(t q) f -> q t f", q=128)
    q_t = qD.rearrange("(t q) f -> q t f", q=128)

    BARS = c.sbuf(p + "bars", [1, 8], F32)
    bar = p + "bar"
    c.sem(bar)

    def phase_barrier(sem_list):
        for s_ in sem_list:
            if s_ in c.cnt:
                G.wait_ge(c.sem(s_), c.cnt[s_])
        inst = G.memset(BARS[0:1, 0:1], 0.0)
        return c.inc(bar, inst)

    def engines_join(bv_):
        for e_ in (T, V, S, Y):
            e_.wait_ge(c.sem(bar), bv_)

    stA = ExitStack()
    def asb(n_, sh, dt_):
        return stA.enter_context(c.nc.sbuf_tensor(n_, sh, dt_))
    RWB = asb(p + "rwb", [128, 1024], F32)
    RWR = asb(p + "rwr", [1, 1024], F32)
    SQD = asb(p + "sqd", [128, 1024], F32)
    WST = asb(p + "wst", [128, 16 * 1024], BF16)
    XCT2 = asb(p + "xct2", [128, 8 * 2048], BF16)
    XGF = asb(p + "xgf", [128, 2048], F32)
    XCF = asb(p + "xcf", [128, 2048], F32)
    CW = asb(p + "cw", [128, 64], F32)
    CB = asb(p + "cb", [128, 16], F32)
    GT = asb(p + "gt", [128, 112], F32)
    GB = asb(p + "gb", [128, 41 * 16], F32)
    IPB = asb(p + "ipb", [1, 6176], BF16)
    GBR = asb(p + "gbr", [1, 112], BF16)
    GSTG = asb(p + "gstg", [16, 512], F32)
    VTB = asb(p + "vtb", [128, 32], F32)
    NS = asb(p + "ns", [128, 64], F32)
    ELR = asb(p + "elr", [1, 32], F32)

    wl = p + "wl"
    c.acquire(Y, "XTR")
    c.acquire(Y, "OUT")
    c.acquire(S, "OUT")
    c.acquire(V, "ST")
    c.acquire(S, "XNT")
    c.acquire(T, "PT0"), c.acquire(T, "PT1")
    for i in range(6):
        c.acquire(T, "PS%d" % i)

    load_f32(c, RWR[0:1, :], w["rms_w"].rearrange("(o a) -> o a", o=1), sem=wl)
    load_bf16(c, IPB[0:1, :], w["in_proj_b"].rearrange("(o a) -> o a", o=1), sem=wl)
    load_f32(c, CW[:, :].rearrange("q (g j) -> q g j", g=16),
             w["conv_w"].rearrange("(g q) j -> q g j", q=128), sem=wl)
    load_f32(c, CB[:, :], w["conv_b"].rearrange("(g q) -> q g", q=128),
             slow=True, sem=wl)
    c.wait(T, wl, c.cnt[wl])
    bm0, bv0 = p + "bm0", p + "bv0"
    c.sem(bm0), c.sem(bv0)
    for half in range(2):
        if half > 0:
            c.wait(T, bv0, c.cnt[bv0])
        inst = T.matmul(PS[0][:, 0:512], ONESF[0:1, 0:128],
                        RWR[0:1, 512 * half:512 * (half + 1)], start=True, stop=True)
        c.inc(bm0, inst)
        c.wait(V, bm0)
        inst = V.tensor_copy(RWB[:, 512 * half:512 * (half + 1)], PS[0][:, 0:512])
        c.inc(bv0, inst)

    sx, sv, ss, sm, se = (p + "1" + k for k in "xvsme")
    zy = [p + "zy0", p + "zy1"]
    for k in [sx, sv, ss, sm, se] + zy:
        c.sem(k)
    X32 = XTR[:, 0:1024]
    XR32 = XTR[:, 1024:2048]
    prev_e7 = prev_ss1 = None
    c.wait(V, bv0, c.cnt[bv0])
    for t in range(NT):
        XO32 = OUT[:, 1024 * (t % 2):1024 * (t % 2) + 1024]
        if t >= 1:
            Y.wait_ge(c.sem("Ve"), prev_e7)
            Y.wait_ge(c.sem(ss), prev_ss1)
        c.inc(sx, Y.dma_start(out=X32, in_=x_t[:, t, :]), 16)
        c.wait(V, sx)
        c.wait(S, sx)
        e1 = c.ve(V.reduce_sum(st[:, 0:1], X32, axis=AX.X))
        c.vw(V, e1)
        e_mu = c.ve(V.tensor_scalar_mul(st[:, 1:2], st[:, 0:1], 1.0 / D))
        inst = S.activation(SQD[:, :], X32, AF.Square, accum_out=st[:, 2:3])
        prev_ss1 = c.inc(ss, inst)
        c.wait(V, ss)
        e2 = c.ve(V.tensor_scalar_mul(st[:, 3:4], st[:, 2:3], 1.0 / D))
        c.vw(V, e_mu)
        e3 = c.ve(V.tensor_mul(st[:, 4:5], st[:, 1:2], st[:, 1:2]))
        c.vw(V, e3)
        e4 = c.ve(V.tensor_sub(st[:, 5:6], st[:, 3:4], st[:, 4:5]))
        c.vw(V, e4)
        inst = V.tensor_scalar_add(st[:, 5:6], st[:, 5:6], 1e-5)
        c.inc(sv, inst)
        c.wait(S, sv)
        inst = S.activation(st[:, 6:7], st[:, 5:6], AF.Sqrt)
        c.inc(ss, inst)
        c.wait(V, ss)
        e5 = c.ve(V.reciprocal(st[:, 7:8], st[:, 6:7]))
        c.vw(V, e5)
        e6 = c.ve(V.scalar_tensor_tensor(st[:, 8:9], st[:, 1:2], -1.0, st[:, 7:8],
                                         ALU.mult, ALU.mult))
        c.vw(V, e6)
        if t >= 2:
            c.wait(V, zy[t % 2], 16 * (t // 2))
        e7 = c.ve(V.tensor_scalar(XO32, X32, st[:, 7:8], st[:, 8:9],
                                  ALU.mult, ALU.add))
        prev_e7 = e7
        c.vw(Y, e7)
        c.inc(zy[t % 2], Y.dma_start(out=xn_t[:, t, :], in_=XO32), 16)
        c.vw(S, e7)
        inst = S.activation(SQD[:, :], XO32, AF.Square, accum_out=st[:, 9:10])
        c.inc(ss, inst)
        c.wait(V, ss)
        e8 = c.ve(V.tensor_scalar(st[:, 10:11], st[:, 9:10], 1.0 / D, 1e-6,
                                  ALU.mult, ALU.add))
        c.vw(V, e8)
        inst = V.tensor_copy(st[:, 11:12], st[:, 10:11])
        c.inc(sv, inst)
        c.wait(S, sv)
        inst = S.activation(st[:, 12:13], st[:, 11:12], AF.Sqrt)
        c.inc(ss, inst)
        c.wait(V, ss)
        e9 = c.ve(V.reciprocal(st[:, 13:14], st[:, 12:13]))
        c.vw(V, e9)
        ea = c.ve(V.tensor_scalar_mul(XR32, XO32, st[:, 13:14]))
        if t >= 1:
            c.wait(V, sm, t)
        c.vw(V, ea)
        inst = V.tensor_mul(XNB[:, 0:1024], XR32, RWB[:, :])
        c.inc(sv, inst)
        c.wait(T, sv)
        pt = PT[t % 2]
        if t >= 2:
            c.wait(T, se, t - 1)
        for f in range(KD):
            inst = T.transpose(pt[:, f * 128:(f + 1) * 128],
                               XNB[:, f * 128:(f + 1) * 128], IDN[:, :])
        c.inc(sm, inst)
        c.wait(S, sm)
        xnt_dst = XNT[:, :].rearrange("q (f tt) -> q f tt", f=KD)[:, :, t * 128:(t + 1) * 128]
        inst = S.activation(xnt_dst, pt[:, :].rearrange("q (f tt) -> q f tt", f=KD),
                            AF.Copy)
        c.inc(se, inst)
    xnt_done = c.cnt[se]

    pm_, pe_, pv_, ps_, tm2 = (p + "2" + k for k in "mevst")
    py = [p + "2y0", p + "2y1"]
    for k in [pm_, pe_, pv_, ps_, tm2] + py:
        c.sem(k)
    c.wait(T, se, xnt_done)
    for e_ in (S, T, V, Y):
        c.wait(e_, zy[0])
        c.wait(e_, zy[1])
        c.wait(e_, sv)
        c.wait(e_, ss)
        e_.wait_ge(c.sem("Ve"), c.cnt["Ve"])
    ZSTG = OUT

    def tspill(dstD, kc, t):
        for zi in range(2):
            c.wait(T, pe_, c.cnt[pe_])
            c.wait(T, ps_, c.cnt[ps_])
            for i in range(4):
                inst = T.transpose(PS[3][:, 128 * i:128 * (i + 1)],
                                   ZSTG[:, 512 * zi + 128 * i:512 * zi + 128 * (i + 1)],
                                   IDNF[:, :])
            c.inc(tm2, inst)
            c.wait(S, tm2)
            c.wait(S, py[0], c.cnt[py[0]])
            c.wait(S, py[1], c.cnt[py[1]])
            inst = S.activation(ZSTG[:, 1024:1536], PS[3][:, 0:512], AF.Copy)
            c.inc(ps_, inst)
            c.wait(Y, ps_)
            dd = dstD.rearrange("(kc zi i q) tt -> q kc zi i tt", q=128, i=4, zi=2)[
                :, kc, zi, :, 128 * t:128 * (t + 1)]
            c.inc(py[t % 2], Y.dma_start(
                out=dd, in_=ZSTG[:, 1024:1536].rearrange("q (i a) -> q i a", i=4)), 16)

    for kch in range(7):
        wcols = 1024 if kch < 6 else 32
        G.wait_ge(c.sem(pm_), c.cnt[pm_])
        load_bf16(c, WST[:, 0:8 * 1024].rearrange("q (k i) -> q k i", k=KD)[:, :, 0:wcols],
                  w["in_proj_w"].rearrange("(k q) i -> q k i", q=128)[
                      :, :, 1024 * kch:1024 * kch + wcols],
                  sem=p + "wi%d" % kch)
        c.wait(T, p + "wi%d" % kch, c.cnt[p + "wi%d" % kch])
        for t in range(NT):
            no2 = 2 if kch < 6 else 1
            c.wait(T, pe_, c.cnt[pe_])
            for o2 in range(no2):
                ncol = min(512, wcols - 512 * o2)
                bank = PS[1 + o2]
                inst = T.matmul(bank[:, 0:ncol], ONES[0:1, 0:128],
                                IPB[0:1, 1024 * kch + 512 * o2:1024 * kch + 512 * o2 + ncol],
                                start=True, stop=False)
                for k in range(KD):
                    inst = T.matmul(bank[:, 0:ncol],
                                    XNT[:, 2048 * k + 128 * t: 2048 * k + 128 * (t + 1)],
                                    WST[:, 1024 * k + 512 * o2: 1024 * k + 512 * o2 + ncol],
                                    start=False, stop=(k == KD - 1))
            c.inc(pm_, inst)
            c.wait(S, pm_)
            c.wait(S, tm2, c.cnt[tm2])
            c.wait(S, py[0], c.cnt[py[0]])
            c.wait(S, py[1], c.cnt[py[1]])
            if kch < 6:
                inst = S.activation(ZSTG[:, 0:512], PS[1][:, 0:512], AF.Copy)
                inst = S.activation(ZSTG[:, 512:1024], PS[2][:, 0:512], AF.Copy)
                c.inc(pe_, inst, 2)
            if kch < 2:
                c.wait(T, pe_, c.cnt[pe_])
                tspill(zTD, kch, t)
            elif kch < 4:
                c.wait(T, pe_, c.cnt[pe_])
                tspill(xgTD, kch - 2, t)
            elif kch < 6:
                c.wait(S, pe_, c.cnt[pe_])
                inst = S.activation(SQD[:, :], ZSTG[:, 0:1024], AF.Square)
                c.inc(ps_, inst)
                c.wait(V, ps_)
                e1 = c.ve(V.tensor_reduce(NS[:, 0:8], SQD[:, :].rearrange(
                    "q (h d) -> q h d", h=8), axis=AX.X, op=ALU.add))
                c.vw(V, e1)
                e2 = c.ve(V.tensor_copy(NS[:, 8:16], NS[:, 0:8]))
                c.vw(S, e2)
                inst = S.activation(NS[:, 8:16], NS[:, 8:16], AF.Sqrt)
                c.inc(ps_, inst)
                c.wait(V, ps_)
                e3 = c.ve(V.tensor_scalar_add(NS[:, 8:16], NS[:, 8:16], 1e-6))
                c.vw(V, e3)
                e4 = c.ve(V.reciprocal(NS[:, 16:24], NS[:, 8:16]))
                c.vw(V, e4)
                iv = NS[:, 16:24]
                ib = bass.AP(iv.tensor, iv.offset, [iv.ap[0], [1, 8], [0, 128]])
                e5 = c.ve(V.tensor_tensor(
                    ZSTG[:, 0:1024].rearrange("q (h d) -> q h d", h=8),
                    ZSTG[:, 0:1024].rearrange("q (h d) -> q h d", h=8),
                    ib, ALU.mult))
                c.vw(Y, e5)
                c.inc(py[t % 2], Y.dma_start(
                    out=kn_t[:, t, 1024 * (kch - 4):1024 * (kch - 3)],
                    in_=ZSTG[:, 0:1024]), 16)
            else:
                inst = S.activation(ZSTG[:, 0:32], PS[1][:, 0:32], AF.Copy)
                c.inc(pe_, inst)
                c.wait(Y, pe_)
                c.inc(py[t % 2], Y.dma_start(out=vD[128 * t:128 * (t + 1), :],
                                             in_=ZSTG[:, 0:32]), 16)

    cv, cl2 = p + "cv", p + "cl2"
    cy = [p + "cy0", p + "cy1"]
    for k in [cv, cl2] + cy:
        c.sem(k)
    G.wait_ge(c.sem(py[0]), c.cnt[py[0]])
    G.wait_ge(c.sem(py[1]), c.cnt[py[1]])
    for g in range(16):
        c.wait(G, cv, c.cnt[cv])
        c.wait(G, cy[0], c.cnt[cy[0]])
        c.wait(G, cy[1], c.cnt[cy[1]])
        load_f32(c, XGF[:, :], xgTD[128 * g:128 * (g + 1), :], sem=cl2)
        c.wait(V, cl2, c.cnt[cl2])
        c.wait(V, cy[g % 2], c.cnt[cy[g % 2]])
        inst = V.tensor_scalar(XCF[:, :], XGF[:, :], CW[:, 4 * g + 3:4 * g + 4],
                               CB[:, g:g + 1], ALU.mult, ALU.add)
        c.inc(cv, inst)
        for j in (2, 1, 0):
            sh = 3 - j
            c.wait(V, cv, c.cnt[cv])
            inst = V.scalar_tensor_tensor(XCF[:, sh:2048], XGF[:, 0:2048 - sh],
                                          CW[:, 4 * g + j:4 * g + j + 1],
                                          XCF[:, sh:2048], ALU.mult, ALU.add)
            c.inc(cv, inst)
        c.wait(S, cv, c.cnt[cv])
        inst = S.activation(XGF[:, :], XCF[:, :], AF.Silu)
        c.inc(cv, inst)
        c.wait(Y, cv, c.cnt[cv])
        c.inc(cy[g % 2], Y.dma_start(out=xcD[128 * g:128 * (g + 1), :],
                                     in_=XGF[:, :]), 16)
        c.wait(S, cv, c.cnt[cv])
        dst = (XNT[:, 2048 * g:2048 * (g + 1)] if g < 8
               else XCT2[:, 2048 * (g - 8):2048 * (g - 7)])
        inst = S.activation(dst, XGF[:, :], AF.Copy)
        c.inc(cv, inst)

    def xct_lhsT(k, t):
        return (XNT[:, 2048 * k + 128 * t:2048 * k + 128 * (t + 1)] if k < 8
                else XCT2[:, 2048 * (k - 8) + 128 * t:2048 * (k - 8) + 128 * (t + 1)])

    qm, qe, qs2, qy0, qy1 = (p + "4" + k for k in "mesab")
    for k in [qm, qe, qs2, qy0, qy1]:
        c.sem(k)
    qy = [qy0, qy1]
    c.wait(T, cv, c.cnt[cv])
    for cb in range(2):
        G.wait_ge(c.sem(qm), c.cnt[qm])
        load_bf16(c, WST[:, :].rearrange("q (k i) -> q k i", k=16),
                  w["Q_w"].rearrange("(k q) i -> q k i", q=128)[
                      :, :, 1024 * cb:1024 * (cb + 1)],
                  sem=p + "wq%d" % cb)
        c.wait(T, p + "wq%d" % cb, c.cnt[p + "wq%d" % cb])
        for t in range(NT):
            c.wait(T, qe, c.cnt[qe])
            for o2 in range(2):
                bank = PS[1 + o2]
                for k in range(16):
                    inst = T.matmul(bank[:, 0:512], xct_lhsT(k, t),
                                    WST[:, 1024 * k + 512 * o2:1024 * k + 512 * o2 + 512],
                                    start=(k == 0), stop=(k == 15))
            c.inc(qm, inst)
            c.wait(S, qm)
            c.wait(S, py[0], c.cnt[py[0]])
            c.wait(S, py[1], c.cnt[py[1]])
            inst = S.activation(ZSTG[:, 0:512], PS[1][:, 0:512], AF.Copy)
            inst = S.activation(ZSTG[:, 512:1024], PS[2][:, 0:512], AF.Copy)
            c.inc(qe, inst, 2)
            c.wait(S, qe, c.cnt[qe])
            inst = S.activation(SQD[:, :], ZSTG[:, 0:1024], AF.Square)
            c.inc(qs2, inst)
            c.wait(V, qs2)
            e1 = c.ve(V.tensor_reduce(NS[:, 0:8], SQD[:, :].rearrange(
                "q (h d) -> q h d", h=8), axis=AX.X, op=ALU.add))
            c.vw(V, e1)
            e2 = c.ve(V.tensor_copy(NS[:, 8:16], NS[:, 0:8]))
            c.vw(S, e2)
            inst = S.activation(NS[:, 8:16], NS[:, 8:16], AF.Sqrt)
            c.inc(qs2, inst)
            c.wait(V, qs2)
            e3 = c.ve(V.tensor_scalar_add(NS[:, 8:16], NS[:, 8:16], 1e-6))
            c.vw(V, e3)
            e4 = c.ve(V.reciprocal(NS[:, 16:24], NS[:, 8:16]))
            c.vw(V, e4)
            iv = NS[:, 16:24]
            ib = bass.AP(iv.tensor, iv.offset, [iv.ap[0], [1, 8], [0, 128]])
            e5 = c.ve(V.tensor_tensor(
                ZSTG[:, 0:1024].rearrange("q (h d) -> q h d", h=8),
                ZSTG[:, 0:1024].rearrange("q (h d) -> q h d", h=8),
                ib, ALU.mult))
            c.vw(Y, e5)
            c.inc(py[t % 2], Y.dma_start(
                out=q_t[:, t, 1024 * cb:1024 * (cb + 1)], in_=ZSTG[:, 0:1024]), 16)

    gc, gm2, gty = p + "gc", p + "gm2", p + "gty"
    gvy = [p + "gv0", p + "gv1"]
    for k in [gc, gm2, gty] + gvy:
        c.sem(k)
    G.wait_ge(c.sem(qm), c.cnt[qm])
    load_bf16(c, WST[:, :].rearrange("q (k i) -> q k i", k=16)[:, :, 0:112],
              w["gate_w"].rearrange("(k q) i -> q k i", q=128), sem=p + "wg")
    load_bf16(c, GBR[0:1, :], w["gate_b"].rearrange("(o a) -> o a", o=1), sem=p + "wg")
    load_f32(c, ELR[0:1, 0:16], w["log_dt_scale"].rearrange("(o a) -> o a", o=1),
             sem=p + "wg")
    load_f32(c, ELR[0:1, 16:32], w["frq"], sem=p + "wg")
    c.wait(S, p + "wg", c.cnt[p + "wg"])
    inst = S.activation(ELR[0:1, 0:16], ELR[0:1, 0:16], AF.Exp)
    c.inc(gc, inst)
    c.wait(T, gc, c.cnt[gc])
    inst = T.matmul(PS[0][:, 0:16], ONESF[0:1, 0:128], ELR[0:1, 0:16],
                    start=True, stop=True)
    inst = T.matmul(PS[0][:, 16:32], ONESF[0:1, 0:128], ELR[0:1, 16:32],
                    start=True, stop=True)
    c.inc(gm2, inst)
    c.wait(V, gm2, c.cnt[gm2])
    ELB = GB[:, 39 * 16:40 * 16]
    FQB = GB[:, 40 * 16:41 * 16]
    inst = V.tensor_copy(GB[:, 39 * 16:41 * 16], PS[0][:, 0:32])
    c.inc(gc, inst)
    c.wait(T, p + "wg", c.cnt[p + "wg"])
    c.wait(T, gc, c.cnt[gc])

    def gs(i):
        return GB[:, 16 * i:16 * (i + 1)]

    for t in range(NT):
        c.wait(T, gc, c.cnt[gc])
        inst = T.matmul(PS[3][:, 0:112], ONES[0:1, 0:128], GBR[0:1, :],
                        start=True, stop=False)
        for k in range(16):
            inst = T.matmul(PS[3][:, 0:112], xct_lhsT(k, t),
                            WST[:, 1024 * k:1024 * k + 112],
                            start=False, stop=(k == 15))
        c.inc(gm2, inst)
        c.wait(S, gm2, c.cnt[gm2])
        inst = S.activation(GT[:, :], PS[3][:, 0:112], AF.Copy)
        c.inc(gc, inst)

        a0, o1, sB, sC, sdt, b5, r6 = (GT[:, 16 * i:16 * (i + 1)] for i in range(7))

        def vop(inst):
            c.inc(gc, inst)

        def wv(e_):
            c.wait(e_, gc, c.cnt[gc])

        def softplus(src, out):
            wv(V); vop(V.tensor_scalar_mul(gs(0), src, -1.0))
            wv(V); vop(V.tensor_tensor(gs(1), src, gs(0), ALU.min))
            wv(S); vop(S.activation(gs(2), gs(1), AF.Exp))
            wv(S); vop(S.activation(gs(3), gs(2), AF.Ln, bias=1.0))
            wv(V); vop(V.tensor_scalar(gs(4), src, 0.0, None, ALU.max))
            wv(V); vop(V.tensor_add(out, gs(4), gs(3)))

        softplus(a0, gs(5))
        softplus(sdt, gs(6))
        wv(V); vop(V.tensor_mul(gs(7), gs(6), ELB))
        wv(V); vop(V.tensor_scalar_add(gs(7), gs(7), 1e-4))
        wv(S); vop(S.activation(gs(8), o1, AF.Tanh))
        wv(V); vop(V.tensor_scalar_add(gs(8), gs(8), 1.0))
        wv(V); vop(V.tensor_mul(gs(9), gs(8), FQB))
        wv(V); vop(V.scalar_tensor_tensor(gs(10), gs(7), 0.5, gs(5),
                                          ALU.mult, ALU.mult))
        wv(V); vop(V.scalar_tensor_tensor(gs(11), gs(7), 0.5, gs(9),
                                          ALU.mult, ALU.mult))
        wv(V); vop(V.tensor_scalar_add(gs(12), gs(10), 1.0))
        wv(V); vop(V.tensor_mul(gs(13), gs(12), gs(12)))
        wv(V); vop(V.tensor_mul(gs(14), gs(11), gs(11)))
        wv(V); vop(V.tensor_add(gs(13), gs(13), gs(14)))
        wv(V); vop(V.tensor_mul(gs(15), gs(10), gs(10)))
        wv(V); vop(V.tensor_add(gs(16), gs(15), gs(14)))
        wv(V); vop(V.tensor_scalar(gs(17), gs(16), -1.0, 1.0,
                                   ALU.mult, ALU.add))
        wv(V); vop(V.reciprocal(gs(18), gs(13)))
        wv(V); vop(V.tensor_mul(gs(19), gs(17), gs(18)))
        wv(V); vop(V.scalar_tensor_tensor(gs(20), gs(11), 2.0, gs(18),
                                          ALU.mult, ALU.mult))
        wv(V); vop(V.tensor_mul(gs(21), gs(19), gs(19)))
        wv(V); vop(V.tensor_mul(gs(22), gs(20), gs(20)))
        wv(V); vop(V.tensor_add(gs(21), gs(21), gs(22)))
        wv(S); vop(S.activation(gs(23), gs(21), AF.Sqrt))
        wv(S); vop(S.activation(gs(24), r6, AF.Sigmoid))
        wv(V); vop(V.tensor_scalar(gs(25), gs(23), 1e-8, 1.0,
                                   ALU.max, ALU.min))
        wv(S); vop(S.activation(gs(26), gs(25), AF.Ln))
        wv(V); vop(V.scalar_tensor_tensor(gs(27), gs(24), 8.0, gs(26),
                                          ALU.mult, ALU.mult))
        wv(S); vop(S.activation(gs(28), gs(27), AF.Exp))
        wv(V); vop(V.reciprocal(gs(29), gs(23)))
        wv(V); vop(V.tensor_mul(gs(30), gs(28), gs(29)))
        wv(V); vop(V.tensor_mul(gs(31), gs(30), gs(19)))
        wv(V); vop(V.tensor_mul(gs(32), gs(30), gs(20)))
        wv(V); vop(V.tensor_mul(gs(33), gs(28), gs(28)))
        wv(V); vop(V.tensor_scalar(gs(34), gs(33), -1.0, 1.0,
                                   ALU.mult, ALU.add))
        wv(V); vop(V.tensor_scalar(gs(34), gs(34), 0.0, 1.0,
                                   ALU.max, ALU.min))
        wv(V); vop(V.tensor_scalar_add(gs(34), gs(34), 1e-6))
        wv(S); vop(S.activation(gs(35), gs(34), AF.Sqrt))
        wv(S); vop(S.activation(gs(36), sB, AF.Sigmoid))
        wv(V); vop(V.tensor_mul(gs(37), gs(36), gs(35)))
        wv(S); vop(S.activation(gs(38), b5, AF.Sigmoid))
        wv(S); vop(S.activation(gs(0), sC, AF.Sigmoid))

        c.wait(G, gvy[0], c.cnt[gvy[0]])
        c.wait(G, gvy[1], c.cnt[gvy[1]])
        load_f32(c, VTB[:, :], vD[128 * t:128 * (t + 1), :], sem=p + "vl")
        c.wait(V, p + "vl", c.cnt[p + "vl"])
        wv(V)
        vp_ = gs(37)
        vpb = bass.AP(vp_.tensor, vp_.offset, [vp_.ap[0], [1, 16], [0, 2]])
        vop(V.tensor_tensor(VTB[:, :].rearrange("q (h s) -> q h s", s=2),
                            VTB[:, :].rearrange("q (h s) -> q h s", s=2),
                            vpb, ALU.mult))
        wv(Y)
        c.inc(gvy[t % 2], Y.dma_start(out=vD[128 * t:128 * (t + 1), :],
                                      in_=VTB[:, :]), 16)
        wv(T)
        c.wait(T, gty, c.cnt[gty])
        for gi, srcv in enumerate([gs(31), gs(32), gs(38), gs(0)]):
            inst = T.transpose(PS[4][0:16, 128 * gi:128 * gi + 128], srcv, IDNF[:, :])
        c.inc(gm2, inst)
        c.wait(S, gm2, c.cnt[gm2])
        c.wait(S, gvy[0], c.cnt[gvy[0]])
        c.wait(S, gvy[1], c.cnt[gvy[1]])
        inst = S.activation(GSTG[:, :], PS[4][0:16, 0:512], AF.Copy)
        c.inc(gty, inst)
        c.wait(Y, gty, c.cnt[gty])
        dd = g4D.rearrange("g h (tb tt) -> h g tb tt", tt=128)[:, :, t, :]
        c.inc(gvy[t % 2], Y.dma_start(
            out=dd, in_=GSTG[:, :].rearrange("h (g tt) -> h g tt", g=4)), 16)

    bv_ = phase_barrier([wl, sx, sv, ss, sm, se, bm0, bv0, pm_, pe_, pv_, ps_,
                         tm2, py[0], py[1], cv, cl2, cy[0], cy[1], qm, qe, qs2,
                         gc, gm2, gty, gvy[0], gvy[1], p + "vl", p + "wg",
                         zy[0], zy[1], "Ve"]
                        + [p + "wi%d" % i for i in range(7)]
                        + [p + "wq%d" % i for i in range(2)])
    stA.close()
    stB = ExitStack()
    def bsb(n_, sh, dt_):
        return stB.enter_context(c.nc.sbuf_tensor(n_, sh, dt_))
    KSC = bsb(p + "ksc", [16, 2 * CH * 128], F32)
    QSC = bsb(p + "qsc", [16, 2 * CH * 128], F32)
    VSC = bsb(p + "vsc", [16, 2 * CH * 2], F32)
    G4C = bsb(p + "g4c", [16, 2 * 4 * CH], F32)
    RETB = bsb(p + "retb", [16, 2 * CH * 2], F32)
    PP = bsb(p + "pp", [16, 256], F32)
    PR = bsb(p + "pr", [16, 256], F32)
    SD = bsb(p + "sd", [16, 256], F32)
    UU = bsb(p + "uu", [16, 256], F32)
    PD = bsb(p + "pd", [16, 8], F32)
    engines_join(bv_)

    scl = [p + "sc0", p + "sc1"]
    ry = [p + "ry0", p + "ry1"]
    for k in scl + ry:
        c.sem(k)
    kn_s = knD.rearrange("(cb t) (h d) -> h cb t d", t=CH, d=128)
    q_s = qD.rearrange("(cb t) (h d) -> h cb t d", t=CH, d=128)
    v_s = vD.rearrange("(cb t) (h s) -> h cb t s", t=CH, s=2)
    g_s = g4D.rearrange("g h (cb t) -> h g cb t", t=CH)
    ret_s = retD.rearrange("(cb t) (h s) -> h cb t s", t=CH, s=2)

    def issue_chunk_load(ch):
        sl = ch % 2
        o = sl * CH
        c.inc(scl[sl], G.dma_start(
            out=KSC[:, :].rearrange("h (t d) -> h t d", d=128)[:, o:o + CH, :],
            in_=kn_s[:, ch, :, :]), 16)
        c.inc(scl[sl], G.dma_start(
            out=QSC[:, :].rearrange("h (t d) -> h t d", d=128)[:, o:o + CH, :],
            in_=q_s[:, ch, :, :]), 16)
        c.inc(scl[sl], G.dma_start(
            out=VSC[:, :].rearrange("h (t s) -> h t s", s=2)[:, o:o + CH, :],
            in_=v_s[:, ch, :, :]), 16)
        c.inc(scl[sl], G.dma_start(
            out=G4C[:, :].rearrange("h (sl g t) -> h sl g t", g=4, t=CH)[:, sl, :, :],
            in_=g_s[:, :, ch, :]), 16)

    ei = c.ve(V.memset(PP[:, :], 0.0))
    ei = c.ve(V.memset(PR[:, :], 0.0))
    issue_chunk_load(0)
    issue_chunk_load(1)
    P3 = PP[:, :].rearrange("h (s d) -> h s d", s=2)
    SD3 = SD[:, :].rearrange("h (s d) -> h s d", s=2)
    U3 = UU[:, :].rearrange("h (s d) -> h s d", s=2)
    for ch in range(NCH):
        sl = ch % 2
        o = sl * CH
        c.wait(V, scl[sl], c.cnt[scl[sl]])
        c.wait(V, ry[sl], c.cnt[ry[sl]])
        c.vw(V, ei)
        for tt in range(CH):
            kcol = KSC[:, (o + tt) * 128:(o + tt) * 128 + 128]
            qcol = QSC[:, (o + tt) * 128:(o + tt) * 128 + 128]
            kb = bass.AP(kcol.tensor, kcol.offset, [kcol.ap[0], [0, 2], [1, 128]])
            qb = bass.AP(qcol.tensor, qcol.offset, [qcol.ap[0], [0, 2], [1, 128]])
            ctc = G4C[:, sl * 4 * CH + 0 * CH + tt:sl * 4 * CH + 0 * CH + tt + 1]
            stc = G4C[:, sl * 4 * CH + 1 * CH + tt:sl * 4 * CH + 1 * CH + tt + 1]
            btc = G4C[:, sl * 4 * CH + 2 * CH + tt:sl * 4 * CH + 2 * CH + tt + 1]
            vcol = VSC[:, (o + tt) * 2:(o + tt) * 2 + 2]
            v1 = c.ve(V.tensor_scalar_mul(SD[:, :], PP[:, :], ctc))
            c.vw(V, v1)
            v2 = c.ve(V.scalar_tensor_tensor(SD[:, :], PR[:, :], stc, SD[:, :],
                                             ALU.mult, ALU.add))
            c.vw(V, v2)
            v3 = c.ve(V.tensor_tensor(U3, SD3, kb, ALU.mult))
            c.vw(V, v3)
            v4 = c.ve(V.tensor_reduce(PD[:, 0:2], U3, axis=AX.X, op=ALU.add))
            c.vw(V, v4)
            v5 = c.ve(V.scalar_tensor_tensor(PD[:, 2:4], PD[:, 0:2], -1.0, vcol,
                                             ALU.mult, ALU.add))
            c.vw(V, v5)
            v6 = c.ve(V.tensor_scalar_mul(PD[:, 4:6], PD[:, 2:4], btc))
            c.vw(V, v6)
            v7 = c.ve(V.scalar_tensor_tensor(PP[:, 0:128], kcol, PD[:, 4:5],
                                             SD[:, 0:128], ALU.mult, ALU.add))
            v8 = c.ve(V.scalar_tensor_tensor(PP[:, 128:256], kcol, PD[:, 5:6],
                                             SD[:, 128:256], ALU.mult, ALU.add))
            c.vw(V, v8)
            v9 = c.ve(V.tensor_copy(PR[:, 0:128], PP[:, 128:256]))
            va = c.ve(V.tensor_scalar_mul(PR[:, 128:256], PP[:, 0:128], -1.0))
            vb = c.ve(V.tensor_tensor(U3, P3, qb, ALU.mult))
            c.vw(V, vb)
            vc = c.ve(V.tensor_reduce(
                RETB[:, (o + tt) * 2:(o + tt) * 2 + 2], U3, axis=AX.X, op=ALU.add))
            c.vw(V, vc)
        selc = G4C[:, sl * 4 * CH + 3 * CH:sl * 4 * CH + 4 * CH]
        scb = bass.AP(selc.tensor, selc.offset, [selc.ap[0], [1, CH], [0, 2]])
        ei = c.ve(V.tensor_tensor(
            RETB[:, o * 2:(o + CH) * 2].rearrange("h (t s) -> h t s", s=2),
            RETB[:, o * 2:(o + CH) * 2].rearrange("h (t s) -> h t s", s=2),
            scb, ALU.mult))
        c.vw(Y, ei)
        c.inc(ry[sl], Y.dma_start(out=ret_s[:, ch, :, :],
                                  in_=RETB[:, o * 2:(o + CH) * 2].rearrange(
                                      "h (t s) -> h t s", s=2)), 16)
        if ch + 2 < NCH:
            G.wait_ge(c.sem("Ve"), ei)
            issue_chunk_load(ch + 2)

    bv_ = phase_barrier([scl[0], scl[1], ry[0], ry[1], "Ve"])
    stB.close()
    stC = ExitStack()
    def csb(n_, sh, dt_):
        return stC.enter_context(c.nc.sbuf_tensor(n_, sh, dt_))
    WSTC = csb(p + "wstc", [128, 16 * 1024], BF16)
    RWS = csb(p + "rws", [32, 2048], BF16)
    RETT = csb(p + "rett", [32, 2048], BF16)
    RTM = csb(p + "rtm", [128, 32], F32)
    RTB = csb(p + "rtb", [128, 32], BF16)
    ZC = csb(p + "zc", [128, 2048], F32)
    XCB = csb(p + "xcb", [128, 2048], F32)
    XCT3 = csb(p + "xct3", [128, 8 * 2048], BF16)
    GNW = csb(p + "gnw", [128, 16], F32)
    GNB = csb(p + "gnb", [128, 16], F32)
    DPA = csb(p + "dpa", [128, 16], F32)
    ONEC = csb(p + "onec", [128, 1], F32)
    MBB = csb(p + "mbb", [128, 4], F32)
    SC2 = csb(p + "sc2", [1, 8], F32)
    engines_join(bv_)

    fc, fm2, fy2 = p + "fc", p + "fm2", p + "fy2"
    fl = [p + "fl0", p + "fl1"]
    for k in [fc, fm2, fy2] + fl:
        c.sem(k)
    wl2 = p + "wl2"
    load_bf16(c, RWS[:, :], w["readout_w"], sem=wl2)
    load_f32(c, GNW[:, :], w["gn_w"].rearrange("(g q) -> q g", q=128),
             slow=True, sem=wl2)
    load_f32(c, GNB[:, :], w["gn_b"].rearrange("(g q) -> q g", q=128),
             slow=True, sem=wl2)
    load_f32(c, DPA[:, :], w["D_param"].rearrange("(g q) -> q g", q=128),
             slow=True, sem=wl2)
    inst = V.memset(ONEC[:, :], 1.0)
    c.inc(fc, inst)

    for t in range(NT):
        c.wait(G, fc, c.cnt[fc])
        load_f32(c, RTM[:, :], retD[128 * t:128 * (t + 1), :], sem=p + "rl")
        c.wait(S, p + "rl", c.cnt[p + "rl"])
        inst = S.activation(RTB[:, :], RTM[:, :], AF.Copy)
        c.inc(fc, inst)
        c.wait(T, fc, c.cnt[fc])
        bank = PT[t % 2]
        if t >= 2:
            c.wait(T, fm2, c.cnt[fm2] - 1)
        inst = T.transpose(bank[0:32, 0:128], RTB[:, :], IDN[:, :])
        c.inc(fm2, inst)
        c.wait(S, fm2, c.cnt[fm2])
        inst = S.activation(RETT[:, 128 * t:128 * (t + 1)], bank[0:32, 0:128],
                            AF.Copy)
        c.inc(fc, inst)
    c.wait(T, wl2, c.cnt[wl2])
    c.wait(V, wl2, c.cnt[wl2])

    NELEM = float(T_TOK * 128)
    for g in range(16):
        c.wait(G, fc, c.cnt[fc])
        load_f32(c, ZC[:, :], zTD[128 * g:128 * (g + 1), :], sem=fl[g % 2])
        load_f32(c, XCB[:, :], xcD[128 * g:128 * (g + 1), :], sem=fl[g % 2])
        c.wait(T, fc, c.cnt[fc])
        for n4 in range(4):
            bank = PS[1 + (n4 % 2)]
            c.wait(T, fy2, c.cnt[fy2])
            inst = T.matmul(bank[:, 0:512], RWS[:, 128 * g:128 * (g + 1)],
                            RETT[:, 512 * n4:512 * (n4 + 1)], start=True, stop=True)
            c.inc(fm2, inst)
            c.wait(S, fm2, c.cnt[fm2])
            inst = S.activation(XTR[:, 512 * n4:512 * (n4 + 1)], bank[:, 0:512],
                                AF.Copy)
            c.inc(fy2, inst)
        YT = XTR[:, 0:2048]
        c.wait(V, fy2, c.cnt[fy2])
        e1 = c.ve(V.reduce_sum(st[:, 0:1], YT, axis=AX.X))
        c.wait(S, fy2, c.cnt[fy2])
        c.vw(S, e1)
        inst = S.activation(OUT[:, 0:2048], YT, AF.Square, accum_out=st[:, 1:2])
        c.inc(fc, inst)
        c.wait(T, fc, c.cnt[fc])
        c.vw(T, e1)
        inst = T.matmul(PS[3][0:1, 0:1], st[:, 0:1], ONEC[:, :], start=True, stop=True)
        inst = T.matmul(PS[3][0:1, 1:2], st[:, 1:2], ONEC[:, :], start=True, stop=True)
        c.inc(fm2, inst)
        c.wait(V, fm2, c.cnt[fm2])
        e2 = c.ve(V.tensor_scalar_mul(SC2[0:1, 0:2], PS[3][0:1, 0:2], 1.0 / NELEM))
        c.vw(V, e2)
        e3 = c.ve(V.tensor_mul(SC2[0:1, 2:3], SC2[0:1, 0:1], SC2[0:1, 0:1]))
        c.vw(V, e3)
        e4 = c.ve(V.tensor_sub(SC2[0:1, 3:4], SC2[0:1, 1:2], SC2[0:1, 2:3]))
        c.vw(V, e4)
        e5 = c.ve(V.tensor_scalar_add(SC2[0:1, 3:4], SC2[0:1, 3:4], 1e-5))
        c.vw(S, e5)
        inst = S.activation(SC2[0:1, 4:5], SC2[0:1, 3:4], AF.Sqrt)
        c.inc(fc, inst)
        c.wait(V, fc, c.cnt[fc])
        e6 = c.ve(V.reciprocal(SC2[0:1, 5:6], SC2[0:1, 4:5]))
        c.vw(T, e6)
        inst = T.matmul(PS[3][:, 4:5], ONESF[0:1, 0:128], SC2[0:1, 0:1],
                        start=True, stop=True)
        inst = T.matmul(PS[3][:, 5:6], ONESF[0:1, 0:128], SC2[0:1, 5:6],
                        start=True, stop=True)
        c.inc(fm2, inst)
        c.wait(V, fm2, c.cnt[fm2])
        e7 = c.ve(V.tensor_copy(MBB[:, 0:2], PS[3][:, 4:6]))
        c.vw(V, e7)
        e8 = c.ve(V.tensor_mul(MBB[:, 2:3], MBB[:, 1:2], GNW[:, g:g + 1]))
        c.vw(V, e8)
        e9 = c.ve(V.tensor_mul(MBB[:, 3:4], MBB[:, 0:1], MBB[:, 2:3]))
        c.vw(V, e9)
        ea = c.ve(V.tensor_sub(MBB[:, 3:4], GNB[:, g:g + 1], MBB[:, 3:4]))
        c.vw(V, ea)
        eb = c.ve(V.tensor_scalar(YT, YT, MBB[:, 2:3], MBB[:, 3:4],
                                  ALU.mult, ALU.add))
        c.wait(S, fl[g % 2], c.cnt[fl[g % 2]])
        c.wait(S, fc, c.cnt[fc])
        inst = S.activation(OUT[:, 0:2048], ZC[:, :], AF.Silu)
        c.inc(fc, inst)
        c.wait(V, fc, c.cnt[fc])
        c.wait(V, fl[g % 2], c.cnt[fl[g % 2]])
        c.vw(V, eb)
        ec = c.ve(V.tensor_mul(YT, YT, OUT[:, 0:2048]))
        c.vw(V, ec)
        ed = c.ve(V.scalar_tensor_tensor(YT, XCB[:, :], DPA[:, g:g + 1], YT,
                                         ALU.mult, ALU.add))
        c.vw(S, ed)
        dst = (XNT[:, 2048 * g:2048 * (g + 1)] if g < 8
               else XCT3[:, 2048 * (g - 8):2048 * (g - 7)])
        inst = S.activation(dst, YT, AF.Copy)
        c.inc(fc, inst)

    def y3_lhsT(k, t):
        return (XNT[:, 2048 * k + 128 * t:2048 * k + 128 * (t + 1)] if k < 8
                else XCT3[:, 2048 * (k - 8) + 128 * t:2048 * (k - 8) + 128 * (t + 1)])

    om_, oe_, ov_ = p + "om", p + "oe", p + "ov"
    oxl = [p + "ox0", p + "ox1"]
    oy = [p + "oy0", p + "oy1"]
    for k in [om_, oe_, ov_] + oxl + oy:
        c.sem(k)
    G.wait_ge(c.sem(fm2), c.cnt[fm2])
    load_bf16(c, WSTC[:, :].rearrange("q (k i) -> q k i", k=16),
              w["out_w"].rearrange("(k q) i -> q k i", q=128), sem=wl2)
    c.wait(T, wl2, c.cnt[wl2])
    c.wait(T, fc, c.cnt[fc])
    c.wait(G, fc, c.cnt[fc])
    for t in range(NT):
        c.wait(G, ov_, c.cnt[ov_])
        load_f32(c, XTR[:, 0:1024], x_t[:, t, :], sem=oxl[t % 2])
        load_f32(c, XTR[:, 1024:2048], xn_t[:, t, :], sem=oxl[t % 2])
        c.wait(T, oe_, c.cnt[oe_])
        for o2 in range(2):
            bank = PS[1 + o2]
            for k in range(16):
                inst = T.matmul(bank[:, 0:512], y3_lhsT(k, t),
                                WSTC[:, 1024 * k + 512 * o2:1024 * k + 512 * o2 + 512],
                                start=(k == 0), stop=(k == 15))
        c.inc(om_, inst)
        c.wait(S, om_, c.cnt[om_])
        c.wait(S, oy[t % 2], c.cnt[oy[t % 2]])
        OSL = OUT[:, 1024 * (t % 2):1024 * (t % 2) + 1024]
        inst = S.activation(OSL[:, 0:512], PS[1][:, 0:512], AF.Copy)
        inst = S.activation(OSL[:, 512:1024], PS[2][:, 0:512], AF.Copy)
        c.inc(oe_, inst, 2)
        c.wait(V, oe_, c.cnt[oe_])
        c.wait(V, oxl[t % 2], c.cnt[oxl[t % 2]])
        e1 = c.ve(V.tensor_add(OSL, OSL, XTR[:, 0:1024]))
        c.vw(V, e1)
        inst = V.tensor_add(OSL, OSL, XTR[:, 1024:2048])
        c.inc(ov_, inst)
        c.wait(Y, ov_, c.cnt[ov_])
        c.inc(oy[t % 2], Y.dma_start(out=y_t[:, t, :], in_=OSL), 16)
    c.wait(Y, oy[0])
    c.wait(Y, oy[1])
    stC.close()



def make_pools(c):
    pools = {}
    pools["XNT"] = c.sbuf("XNT", [128, KD * 2048], BF16)
    pools["XTR"] = c.sbuf("XTR", [128, 2 * 1024], F32)
    pools["OUT"] = c.sbuf("OUTB", [128, 2 * 1024], F32)
    pools["XNB"] = c.sbuf("XNB", [128, 1024], BF16)
    pools["ST"] = c.sbuf("STAT", [128, 56], F32)
    pools["SQ"] = pools["OUT"][:, 0:1024]
    pools["IDN"] = c.sbuf("IDN", [128, 128], BF16)
    pools["IDNF"] = c.sbuf("IDNF", [128, 128], F32)
    pools["ONES"] = c.sbuf("ONES", [1, 128], BF16)
    pools["ONESF"] = c.sbuf("ONESF", [1, 128], F32)
    pools["PT"] = [c.psum("PT0", [128, 1024], BF16),
                   c.psum("PT1", [128, 1024], BF16)]
    pools["PS"] = [c.psum("PS%d" % i, [128, 512], F32) for i in range(6)]
    return pools


def emit_prologue(c, pools, ident_dram):
    c.inc("cl", c.G.dma_start(out=pools["IDN"][:, :], in_=ident_dram), 16)
    c.inc("cl", c.G.dma_start(out=pools["IDNF"][:, :], in_=ident_dram), 16)
    inst = c.V.memset(pools["ONES"][:, :], 1.0)
    c.inc("cst", inst)
    inst = c.V.memset(pools["ONESF"][:, :], 1.0)
    c.inc("cst", inst)
    c.wait(c.T, "cl", 32)
    c.wait(c.T, "cst", 2)


L2_FFN_NAMES = ["router_w", "w1", "b1", "w2", "b2"]


def build_l1_ffn():
    nc = bass.Bass("TRN2", target_bir_lowering=False)
    stack = ExitStack()
    with stack:
        c = Ctx(nc, stack)
        x = nc.dram_tensor("x", [T_TOK, D], F32, kind="ExternalInput").ap()
        y = nc.dram_tensor("y", [T_TOK, D], F32, kind="ExternalOutput").ap()
        ident = nc.dram_tensor("ident", [128, 128], F32, kind="ExternalInput").ap()
        hs = nc.dram_tensor("hscratch", [T_TOK, D], F32).ap()
        w = {
            "attn_wT": nc.dram_tensor("attn_wT", [1024, 1024], F32, kind="ExternalInput").ap(),
            "gate_wT": nc.dram_tensor("gate_wT", [1024, 8], F32, kind="ExternalInput").ap(),
            "ln_rows": nc.dram_tensor("ln_rows", [4, 1024], F32, kind="ExternalInput").ap(),
            "gpT": nc.dram_tensor("gpT", [8, 1024, 4096], F32, kind="ExternalInput").ap(),
            "upT": nc.dram_tensor("upT", [8, 1024, 4096], F32, kind="ExternalInput").ap(),
            "dpT": nc.dram_tensor("dpT", [8, 4096, 1024], F32, kind="ExternalInput").ap(),
        }
        pools = make_pools(c)
        emit_prologue(c, pools, ident)
        emit_l1_ffn(c, x, y, w, pools, "L1f", hs)
        c.finish()
    return nc


def l1_ffn_in_map(inputs, xshard):
    return {"x": xshard, "ident": _IDENT,
            "attn_wT": np.ascontiguousarray(inputs["L1_ffn_attn_w"].T),
            "gate_wT": np.ascontiguousarray(inputs["L1_ffn_gate_w"].T),
            "ln_rows": np.ascontiguousarray(np.stack([
                inputs["L1_ffn_ln1_g"], inputs["L1_ffn_ln1_b"],
                inputs["L1_ffn_ln2_g"], inputs["L1_ffn_ln2_b"]])),
            "gpT": np.ascontiguousarray(inputs["L1_ffn_gate_proj"].transpose(0, 2, 1)),
            "upT": np.ascontiguousarray(inputs["L1_ffn_up_proj"].transpose(0, 2, 1)),
            "dpT": np.ascontiguousarray(inputs["L1_ffn_down_proj"].transpose(0, 2, 1))}


def build_l1_attn():
    nc = bass.Bass("TRN2", target_bir_lowering=False)
    stack = ExitStack()
    with stack:
        c = Ctx(nc, stack)
        x = nc.dram_tensor("x", [T_TOK, D], F32, kind="ExternalInput").ap()
        y = nc.dram_tensor("y", [T_TOK, D], F32, kind="ExternalOutput").ap()
        ident = nc.dram_tensor("ident", [128, 128], F32, kind="ExternalInput").ap()
        w = {}
        for n_ in ("qW", "kW", "vW", "oW"):
            w[n_] = nc.dram_tensor(n_, [4, 256, 256], F32, kind="ExternalInput").ap()
        for n_ in ("qb", "kb", "vb", "ob"):
            w[n_] = nc.dram_tensor(n_, [4, 256], F32, kind="ExternalInput").ap()
        w["J"] = nc.dram_tensor("J", [1024], F32, kind="ExternalInput").ap()
        pools = make_pools(c)
        emit_prologue(c, pools, ident)
        emit_l1_attn(c, x, y, w, pools, "L1a")
        c.finish()
    return nc


def l1_attn_in_map(inputs, xshard):
    m = {"x": xshard, "ident": _IDENT, "J": inputs["L1_attn_J"]}
    for n_ in ("qW", "qb", "kW", "kb", "vW", "vb", "oW", "ob"):
        m[n_] = inputs["L1_attn_" + n_]
    return m


def build_l0_attn():
    nc = bass.Bass("TRN2", target_bir_lowering=False)
    stack = ExitStack()
    with stack:
        c = Ctx(nc, stack)
        x = nc.dram_tensor("x", [T_TOK, D], F32, kind="ExternalInput").ap()
        y = nc.dram_tensor("y", [T_TOK, D], F32, kind="ExternalOutput").ap()
        ident = nc.dram_tensor("ident", [128, 128], F32, kind="ExternalInput").ap()
        w = {}
        for n_ in ("Wq_l", "Wk_l", "Wv_l", "Wo_l", "Wq_g", "Wk_g", "Wv_g", "Wo_g"):
            w[n_] = nc.dram_tensor(n_, [1024, 1024], F32, kind="ExternalInput").ap()
        pools = make_pools(c)
        emit_prologue(c, pools, ident)
        emit_l0_attn(c, x, y, w, pools, "L0a")
        c.finish()
    return nc


def l0_attn_in_map(inputs, xshard):
    m = {"x": xshard, "ident": _IDENT}
    for n_ in ("Wq_l", "Wk_l", "Wv_l", "Wo_l", "Wq_g", "Wk_g", "Wv_g", "Wo_g"):
        m[n_] = inputs["L0_attn_" + n_]
    return m


def build_l3_ffn():
    nc = bass.Bass("TRN2", target_bir_lowering=False)
    stack = ExitStack()
    with stack:
        c = Ctx(nc, stack)
        x = nc.dram_tensor("x", [T_TOK, D], F32, kind="ExternalInput").ap()
        y = nc.dram_tensor("y", [T_TOK, D], F32, kind="ExternalOutput").ap()
        ident = nc.dram_tensor("ident", [128, 128], F32, kind="ExternalInput").ap()
        w = {
            "gate_wT": nc.dram_tensor("gate_wT", [1024, 8], F32, kind="ExternalInput").ap(),
            "down_wT": nc.dram_tensor("down_wT", [8, 1024, 64], F32, kind="ExternalInput").ap(),
            "down_b": nc.dram_tensor("down_b", [8, 64], F32, kind="ExternalInput").ap(),
            "up_wT": nc.dram_tensor("up_wT", [8, 64, 1024], F32, kind="ExternalInput").ap(),
            "up_b": nc.dram_tensor("up_b", [8, 1024], F32, kind="ExternalInput").ap(),
        }
        pools = make_pools(c)
        emit_prologue(c, pools, ident)
        emit_l3_ffn(c, x, y, w, pools, "L3f")
        c.finish()
    return nc


def l3_ffn_in_map(inputs, xshard):
    return {"x": xshard, "ident": _IDENT,
            "gate_wT": np.ascontiguousarray(inputs["L3_ffn_gate_w"].T),
            "down_wT": np.ascontiguousarray(inputs["L3_ffn_down_w"].transpose(0, 2, 1)),
            "down_b": inputs["L3_ffn_down_b"],
            "up_wT": np.ascontiguousarray(inputs["L3_ffn_up_w"].transpose(0, 2, 1)),
            "up_b": inputs["L3_ffn_up_b"]}


def build_l3_attn():
    nc = bass.Bass("TRN2", target_bir_lowering=False)
    stack = ExitStack()
    with stack:
        c = Ctx(nc, stack)
        x = nc.dram_tensor("x", [T_TOK, D], F32, kind="ExternalInput").ap()
        y = nc.dram_tensor("y", [T_TOK, D], F32, kind="ExternalOutput").ap()
        ident = nc.dram_tensor("ident", [128, 128], F32, kind="ExternalInput").ap()
        w = {
            "rms_w": nc.dram_tensor("rms_w", [1024], F32, kind="ExternalInput").ap(),
            "in_proj_w": nc.dram_tensor("in_proj_w", [1024, 6176], F32, kind="ExternalInput").ap(),
            "in_proj_b": nc.dram_tensor("in_proj_b", [6176], F32, kind="ExternalInput").ap(),
            "conv_w": nc.dram_tensor("conv_w", [2048, 4], F32, kind="ExternalInput").ap(),
            "conv_b": nc.dram_tensor("conv_b", [2048], F32, kind="ExternalInput").ap(),
            "gate_w": nc.dram_tensor("gate_w", [2048, 112], F32, kind="ExternalInput").ap(),
            "gate_b": nc.dram_tensor("gate_b", [112], F32, kind="ExternalInput").ap(),
            "log_dt_scale": nc.dram_tensor("log_dt_scale", [16], F32, kind="ExternalInput").ap(),
            "frq": nc.dram_tensor("frq", [1, 16], F32, kind="ExternalInput").ap(),
            "Q_w": nc.dram_tensor("Q_w", [2048, 2048], F32, kind="ExternalInput").ap(),
            "readout_w": nc.dram_tensor("readout_w", [32, 2048], F32, kind="ExternalInput").ap(),
            "out_w": nc.dram_tensor("out_w", [2048, 1024], F32, kind="ExternalInput").ap(),
            "gn_w": nc.dram_tensor("gn_w", [2048], F32, kind="ExternalInput").ap(),
            "gn_b": nc.dram_tensor("gn_b", [2048], F32, kind="ExternalInput").ap(),
            "D_param": nc.dram_tensor("D_param", [2048], F32, kind="ExternalInput").ap(),
        }
        pools = make_pools(c)
        emit_prologue(c, pools, ident)
        emit_l3_attn(c, x, y, w, pools, "L3a")
        c.finish()
    return nc


_L3A_FRQ = (1.0 / (10000.0 ** (np.arange(16, dtype=np.float64) / 16.0))).astype(
    np.float32).reshape(1, 16)


def l3_attn_in_map(inputs, xshard):
    m = {"x": xshard, "ident": _IDENT, "frq": _L3A_FRQ}
    for k in ["rms_w", "in_proj_w", "in_proj_b", "conv_w", "conv_b", "gate_w",
              "gate_b", "log_dt_scale", "Q_w", "readout_w", "out_w", "gn_w",
              "gn_b", "D_param"]:
        m[k] = inputs["L3_attn_" + k]
    return m


def build_l0_ffn():
    nc = bass.Bass("TRN2", target_bir_lowering=False)
    stack = ExitStack()
    with stack:
        c = Ctx(nc, stack)
        x = nc.dram_tensor("x", [T_TOK, D], F32, kind="ExternalInput").ap()
        y = nc.dram_tensor("y", [T_TOK, D], F32, kind="ExternalOutput").ap()
        ident = nc.dram_tensor("ident", [128, 128], F32, kind="ExternalInput").ap()
        w = {
            "Wqkv": nc.dram_tensor("Wqkv", [6, 1024, 3072], F32, kind="ExternalInput").ap(),
            "qkscale": nc.dram_tensor("qkscale", [6, 128], F32, kind="ExternalInput").ap(),
            "Wo": nc.dram_tensor("Wo", [6, 1024, 1024], F32, kind="ExternalInput").ap(),
            "ln_g": nc.dram_tensor("ln_g", [6, 1024], F32, kind="ExternalInput").ap(),
            "ln_b": nc.dram_tensor("ln_b", [6, 1024], F32, kind="ExternalInput").ap(),
            "W1": nc.dram_tensor("W1", [6, 1024, 5460], F32, kind="ExternalInput").ap(),
            "b1": nc.dram_tensor("b1", [6, 5460], F32, kind="ExternalInput").ap(),
            "W2": nc.dram_tensor("W2", [6, 2730, 1024], F32, kind="ExternalInput").ap(),
            "b2": nc.dram_tensor("b2", [6, 1024], F32, kind="ExternalInput").ap(),
            "masks": nc.dram_tensor("masks", [512, 512], F32, kind="ExternalInput").ap(),
            "rope": nc.dram_tensor("rope", [4, 2048, 64], F32, kind="ExternalInput").ap(),
        }
        pools = make_pools(c)
        emit_prologue(c, pools, ident)
        emit_l0_ffn(c, x, y, w, pools, "L0f")
        c.finish()
    return nc


def _l0_rope_masks():
    DIM_HEAD, WINDOW, n = 64, 256, 2048
    half = np.arange(0, DIM_HEAD, 2, dtype=np.float64)
    inv_freq = 1.0 / 10000.0 ** (half / DIM_HEAD)
    t = np.arange(n, dtype=np.float64)
    freqs = t[:, None] * inv_freq[None, :]
    freqs = np.concatenate([freqs, freqs], -1)
    base_scale = (half + 0.4 * DIM_HEAD) / (1.4 * DIM_HEAD)
    power = (t - n // 2) / (WINDOW // 2)
    scale = base_scale[None, :] ** power[:, None]
    scale = np.concatenate([scale, scale], -1)
    C1 = (np.cos(freqs) * scale).astype(np.float32)
    S1 = (np.sin(freqs) * scale).astype(np.float32)
    C2 = (np.cos(freqs) / scale).astype(np.float32)
    S2 = (np.sin(freqs) / scale).astype(np.float32)
    rope = np.stack([C1, S1, C2, S2])
    i = np.arange(256)[:, None]
    j = np.arange(512)[None, :]
    M1 = np.where((j >= i) & (j <= i + 256), 0.0, BIG_NEG).astype(np.float32)
    j0 = np.arange(512)[None, :]
    M0 = np.where((j0 <= i) & (j0 < 256), 0.0, BIG_NEG).astype(np.float32)
    masks = np.concatenate([M1, M0], axis=0)
    return rope, masks


_L0_ROPE, _L0_MASKS = None, None


def l0_ffn_in_map(inputs, xshard):
    global _L0_ROPE, _L0_MASKS
    if _L0_ROPE is None:
        _L0_ROPE, _L0_MASKS = _l0_rope_masks()
    qk = np.concatenate([inputs["L0_ffn_q_scale"], inputs["L0_ffn_k_scale"]],
                        axis=1)
    return {"x": xshard, "ident": _IDENT,
            "Wqkv": inputs["L0_ffn_Wqkv"], "qkscale": np.ascontiguousarray(qk),
            "Wo": inputs["L0_ffn_Wo"], "ln_g": inputs["L0_ffn_ln_g"],
            "ln_b": inputs["L0_ffn_ln_b"], "W1": inputs["L0_ffn_W1"],
            "b1": inputs["L0_ffn_b1"], "W2": inputs["L0_ffn_W2"],
            "b2": inputs["L0_ffn_b2"], "masks": _L0_MASKS, "rope": _L0_ROPE}


def build_l2_attn():
    nc = bass.Bass("TRN2", target_bir_lowering=False)
    stack = ExitStack()
    with stack:
        c = Ctx(nc, stack)
        x = nc.dram_tensor("x", [T_TOK, D], F32, kind="ExternalInput").ap()
        y = nc.dram_tensor("y", [T_TOK, D], F32, kind="ExternalOutput").ap()
        ident = nc.dram_tensor("ident", [128, 128], F32, kind="ExternalInput").ap()
        w = {
            "wqT": nc.dram_tensor("wqT", [1024, 1024], F32, kind="ExternalInput").ap(),
            "bq": nc.dram_tensor("bq", [1024], F32, kind="ExternalInput").ap(),
            "wuT": nc.dram_tensor("wuT", [1024, 1024], F32, kind="ExternalInput").ap(),
            "bu": nc.dram_tensor("bu", [1024], F32, kind="ExternalInput").ap(),
        }
        pools = make_pools(c)
        emit_prologue(c, pools, ident)
        emit_l2_attn(c, x, y, w, pools, "L2a")
        c.finish()
    return nc


def l2_attn_in_map(inputs, xshard):
    return {"x": xshard, "ident": _IDENT,
            "wqT": np.ascontiguousarray(inputs["L2_attn_wq"].T),
            "bq": inputs["L2_attn_bq"],
            "wuT": np.ascontiguousarray(inputs["L2_attn_wu"].T),
            "bu": inputs["L2_attn_bu"]}


def build_l2_ffn():
    nc = bass.Bass("TRN2", target_bir_lowering=False)
    stack = ExitStack()
    with stack:
        c = Ctx(nc, stack)
        x = nc.dram_tensor("x", [T_TOK, D], F32, kind="ExternalInput").ap()
        y = nc.dram_tensor("y", [T_TOK, D], F32, kind="ExternalOutput").ap()
        ident = nc.dram_tensor("ident", [128, 128], F32, kind="ExternalInput").ap()
        w = {
            "router_w": nc.dram_tensor("router_w", [1024, 8], F32, kind="ExternalInput").ap(),
            "w1": nc.dram_tensor("w1", [1024, 4096], F32, kind="ExternalInput").ap(),
            "b1": nc.dram_tensor("b1", [4096], F32, kind="ExternalInput").ap(),
            "w2": nc.dram_tensor("w2", [4096, 1024], F32, kind="ExternalInput").ap(),
            "b2": nc.dram_tensor("b2", [1024], F32, kind="ExternalInput").ap(),
        }
        pools = make_pools(c)
        emit_prologue(c, pools, ident)
        emit_l2_ffn(c, x, y, w, pools, "L2f")
        c.finish()
    return nc


_IDENT = np.eye(128, dtype=np.float32)


def l2_ffn_in_map(inputs, xshard):
    m = {"x": xshard, "ident": _IDENT}
    for n in L2_FFN_NAMES:
        m[n] = inputs["L2_ffn_" + n]
    return m


_CHAIN = [
    (build_l0_ffn, l0_ffn_in_map),
    (build_l0_attn, l0_attn_in_map),
    (build_l0_ffn, l0_ffn_in_map),
    (build_l1_ffn, l1_ffn_in_map),
    (build_l1_attn, l1_attn_in_map),
    (build_l1_ffn, l1_ffn_in_map),
    (build_l2_ffn, l2_ffn_in_map),
    (build_l2_attn, l2_attn_in_map),
    (build_l2_ffn, l2_ffn_in_map),
    (build_l3_ffn, l3_ffn_in_map),
    (build_l3_attn, l3_attn_in_map),
    (build_l3_ffn, l3_ffn_in_map),
]


def kernel(**inputs):
    cur = np.asarray(inputs["L0_attn_x"], np.float32)
    ncs = {}
    for bf, imf in _CHAIN:
        if bf not in ncs:
            ncs[bf] = bf()
        in_maps = [imf(inputs, np.ascontiguousarray(cur[b])) for b in range(NCORES)]
        res = run_bass_kernel_spmd(ncs[bf], in_maps, core_ids=list(range(NCORES)))
        cur = np.stack([res.results[b]["y"] for b in range(NCORES)], axis=0)
    return cur
